# version 01 (final)
# baseline (speedup 1.0000x reference)
import numpy as np
import contextlib
import ml_dtypes
import concourse.bass as bass
import concourse.mybir as mybir
from concourse.bass import IndirectOffsetOnAxis
from concourse.bass_utils import run_bass_kernel_spmd

F32 = mybir.dt.float32
BF16 = mybir.dt.float16
U32 = mybir.dt.uint32
AF = mybir.ActivationFunctionType
ALU = mybir.AluOpType
AX = mybir.AxisListType

B, SEQ, D = 4, 4096, 1024
NTOK, NT = 2048, 16
NH, DKH = 8, 128
E_EXP, F_FF = 8, 4096
CAP_G, CAP_L = 2560, 384
NSLOT = E_EXP * 2 * CAP_L
DI, NST, DC = 2048, 16, 4
M_FFN, INNER = 64, 16
KB_ = 1024

KBCNT = [32, 31, 30, 29, 28, 27, 26, 25, 16, 15, 14, 13, 12, 11, 10, 9]

def q_tiles_for(half):
    if half == 0:
        return [31, 30, 29, 28, 27, 26, 25, 24, 7, 6, 5, 4, 3, 2, 1, 0]
    return [23, 22, 21, 20, 19, 18, 17, 16, 15, 14, 13, 12, 11, 10, 9, 8]

_STAGE = "full"


class Sem:
    __slots__ = ("name", "count", "h")
    def __init__(self, name):
        self.name, self.count, self.h = name, 0, None


class Prog:
    ENGINES = ("sync", "gpsimd", "tensor", "vector", "scalar")

    def __init__(self, nc):
        self.nc = nc
        self.items = []
        self.sems = []

    def sem(self, name):
        s = Sem(name)
        self.sems.append(s)
        return s

    def on(self, eng, fn, waits=(), incs=()):
        assert eng in self.ENGINES, eng
        for s, a in incs:
            s.count += a
        self.items.append((eng, tuple(waits), fn, tuple(incs)))

    def now(self, sem):
        return (sem, sem.count)

    def emit(self, ctx):
        for s in self.sems:
            s.h = ctx.enter_context(self.nc.semaphore(s.name))
        block = ctx.enter_context(self.nc.Block())
        items = self.items

        def make(name):
            def f(eng):
                for (e, waits, fn, incs) in items:
                    if e != name:
                        continue
                    for (s, v) in waits:
                        eng.wait_ge(s.h, v)
                    inst = fn(eng) if fn is not None else None
                    if incs:
                        if inst is None:
                            inst = eng.nop()
                        for (s, a) in incs:
                            inst = inst.then_inc(s.h, a)
            return f
        for name in self.ENGINES:
            getattr(block, name)(make(name))


class SbufMap:
    BASE = 32736
    LIMIT = 229344 - 32736

    def __init__(self, nc):
        self.nc = nc
        self.regions = []

    def at(self, name, shape, dtype, off, pstart, pend):
        sz = int(np.prod(shape[1:])) * mybir.dt.size(dtype)
        assert off + sz <= self.LIMIT, (name, off, sz, off + sz)
        for (n2, o2, s2, a2, b2) in self.regions:
            if not (pend <= a2 or b2 <= pstart):
                assert off + sz <= o2 or o2 + s2 <= off, \
                    f"overlap {name}[{off},{off + sz}) vs {n2}[{o2},{o2 + s2})"
        self.regions.append((name, off, sz, pstart, pend))
        return self.nc.alloc_sbuf_tensor_at(name, list(shape), dtype, offset=self.BASE + off)


def build_nc():
    nc = bass.Bass("TRN2", target_bir_lowering=False, detect_race_conditions=False)
    dri = lambda n, s, d=F32: nc.dram_tensor(n, s, d, kind="ExternalInput").ap()

    x0T_b = dri("x0T_b", [D, SEQ])
    xqT_in = dri("xqT", [D, NTOK])
    x0_own = dri("x0_own", [NTOK, D])
    wqT = dri("wqT", [D, NH * DKH])
    wkT = dri("wkT", [D, NH * DKH])
    wvT = dri("wvT", [NH, D, D])
    qmask_in = dri("qmask", [NT, 32, 128, 128], BF16)
    hoz_idx_in = dri("hoz_idx", [128, NT], U32)
    ident_in = dri("ident", [128, 128])
    gate_wT = dri("gate_wT", [D, E_EXP])
    wgT_in = dri("wgT", [E_EXP, D, F_FF])
    wuT_in = dri("wuT", [E_EXP, D, F_FF])
    woT_in = dri("woT", [E_EXP, F_FF, D])
    prefix_sel_in = dri("prefix_sel", [8, 1])
    iota8_in = dri("iota8", [8, 1])
    xe_idx_in = dri("xe_idx", [128, 48], U32)
    iota_row8_in = dri("iota_row8", [128, 8])
    inprojT_in = dri("inprojT", [D, 2 * DI])
    convw_in = dri("convw", [DI, DC])
    convb_in = dri("convb", [DI, 1])
    xprojT_in = dri("xprojT", [DI, 33])
    dtw_in = dri("dtw", [DI, 1])
    dtb_in = dri("dtb", [DI, 1])
    Alog_in = dri("Alog", [DI, NST])
    Dp_in = dri("Dp", [DI, 1])
    outprojT_in = dri("outprojT", [DI, D])
    Win_in = dri("Win", [D, M_FFN])
    binc_in = dri("binc", [M_FFN, 1])
    W1_in = dri("W1", [M_FFN, INNER])
    b1_in = dri("b1", [M_FFN, INNER])
    W2_in = dri("W2", [M_FFN, INNER])
    b2_in = dri("b2", [M_FFN, 1])
    Wout_in = dri("Wout", [M_FFN, D])
    bout_in = dri("bout", [1, D])
    edge_sel_in = dri("edge_sel", [24, 4])
    ag1u_idx_in = dri("ag1u_idx", [128, 8, 2], U32)
    bc_idx_in = dri("bc_idx", [128, 2, 33], U32)
    ag2_idx_in = dri("ag2_idx", [128, 16, 4], U32)
    ksel_in = dri("ksel", [16, 2])
    iota16_in = dri("iota16", [16, 1])
    kofs_in = dri("kofs", [2, 1])

    out_final = nc.dram_tensor("out", [NTOK, D], F32, kind="ExternalOutput").ap()
    dbg = {}
    if _STAGE == "attn":
        dbg["dbg_x1"] = nc.dram_tensor("dbg_x1", [NTOK, D], F32, kind="ExternalOutput").ap()
        dbg["dbg_acc"] = nc.dram_tensor("dbg_acc", [NTOK, D], F32, kind="ExternalOutput").ap()
        dbg["dbg_hoz"] = nc.dram_tensor("dbg_hoz", [NTOK, D], F32, kind="ExternalOutput").ap()
        dbg["dbg_xres"] = nc.dram_tensor("dbg_xres", [NTOK, D], F32, kind="ExternalOutput").ap()
        dbg["dbg_st"] = nc.dram_tensor("dbg_st", [128, 8, NT], F32, kind="ExternalOutput").ap()
    if _STAGE == "fulldbg":
        dbg["dbg_u"] = nc.dram_tensor("dbg_u", [128, 8 * 4096], BF16, kind="ExternalOutput").ap()
        dbg["dbg_ssm"] = nc.dram_tensor("dbg_ssm", [33, NTOK], BF16, kind="ExternalOutput").ap()
        dbg["dbg_yg"] = nc.dram_tensor("dbg_yg", [128, 16 * 2048], BF16, kind="ExternalOutput").ap()
        dbg["dbg_x3"] = nc.dram_tensor("dbg_x3", [NTOK, D], F32, kind="ExternalOutput").ap()
        dbg["dbg_x3p"] = nc.dram_tensor("dbg_x3p", [NTOK, D], F32, kind="ExternalOutput").ap()
        dbg["dbg_y"] = nc.dram_tensor("dbg_y", [8192, 512], BF16, kind="ExternalOutput").ap()
        dbg["dbg_z"] = nc.dram_tensor("dbg_z", [DI, NTOK], BF16, kind="ExternalOutput").ap()
        dbg["dbg_dt"] = nc.dram_tensor("dbg_dt", [128, 512], BF16, kind="ExternalOutput").ap()
        dbg["dbg_dl"] = nc.dram_tensor("dbg_dl", [128, 8 * 512], BF16, kind="ExternalOutput").ap()
        dbg["dbg_du"] = nc.dram_tensor("dbg_du", [128, 8 * 512], BF16, kind="ExternalOutput").ap()
        dbg["dbg_bc0"] = nc.dram_tensor("dbg_bc0", [128, 512], BF16, kind="ExternalOutput").ap()
        dbg["dbg_bc1"] = nc.dram_tensor("dbg_bc1", [128, 512], BF16, kind="ExternalOutput").ap()
        dbg["dbg_bc"] = dbg["dbg_bc0"]
        dbg["dbg_da"] = nc.dram_tensor("dbg_da", [128, 512], F32, kind="ExternalOutput").ap()
        dbg["dbg_hb"] = nc.dram_tensor("dbg_hb", [128, 512], BF16, kind="ExternalOutput").ap()
        dbg["dbg_hf"] = nc.dram_tensor("dbg_hf", [128, 8 * 512], BF16, kind="ExternalOutput").ap()
        dbg["dbg_yt"] = nc.dram_tensor("dbg_yt", [128, 512], BF16, kind="ExternalOutput").ap()
    if _STAGE == "moe":
        dbg["dbg_x1"] = nc.dram_tensor("dbg_x1", [NTOK, D], F32, kind="ExternalOutput").ap()
        dbg["dbg_rc"] = nc.dram_tensor("dbg_rc", [128, 12, NT], F32, kind="ExternalOutput").ap()
        dbg["dbg_slot"] = nc.dram_tensor("dbg_slot", [128, 4, NT], F32, kind="ExternalOutput").ap()
        dbg["dbg_eo"] = nc.dram_tensor("dbg_eo", [NSLOT, D], BF16, kind="ExternalOutput").ap()
        dbg["dbg_xp"] = nc.dram_tensor("dbg_xp", [NTOK, D], F32, kind="ExternalOutput").ap()
        dbg["dbg_xd"] = nc.dram_tensor("dbg_xd", [NSLOT, D], BF16, kind="ExternalOutput").ap()
        dbg["dbg_acc"] = nc.dram_tensor("dbg_acc", [NTOK, D], F32, kind="ExternalOutput").ap()
        dbg["dbg_hoz"] = nc.dram_tensor("dbg_hoz", [NTOK, D], F32, kind="ExternalOutput").ap()
        dbg["dbg_xres"] = nc.dram_tensor("dbg_xres", [NTOK, D], F32, kind="ExternalOutput").ap()
        dbg["dbg_st"] = nc.dram_tensor("dbg_st", [128, 8, NT], F32, kind="ExternalOutput").ap()
        

    ag0_in = nc.dram_tensor("ag0_in", [NTOK, D], BF16).ap()
    ag0_out = nc.dram_tensor("ag0_out", [8 * NTOK, D], BF16).ap()
    xd = nc.dram_tensor("xd_disp", [NSLOT, D], BF16).ap()
    eo = nc.dram_tensor("eo_disp", [NSLOT, D], BF16).ap()
    cnt_in = nc.dram_tensor("cnt_in", [1, 16], F32).ap()
    cnt_out = nc.dram_tensor("cnt_out", [8, 16], F32).ap()
    edg_in = nc.dram_tensor("edg_in", [3, D], BF16).ap()
    edg_out = nc.dram_tensor("edg_out", [24, D], BF16).ap()
    zT_d = nc.dram_tensor("zT_d", [DI, NTOK], BF16).ap()
    ag1u_in = nc.dram_tensor("ag1u_in", [DI, NTOK], BF16).ap()
    ag1u_out = nc.dram_tensor("ag1u_out", [8 * DI, NTOK], BF16).ap()
    ag1s_in = nc.dram_tensor("ag1s_in", [33, NTOK], BF16).ap()
    ag1s_out = nc.dram_tensor("ag1s_out", [8 * 33, NTOK], BF16).ap()
    ag2_in = nc.dram_tensor("ag2_in", [8192, 512], BF16).ap()
    ag2_out = nc.dram_tensor("ag2_out", [8 * 8192, 512], BF16).ap()

    ctx = contextlib.ExitStack()
    sbt = lambda n, s, d: ctx.enter_context(nc.sbuf_tensor(n, s, d))
    ident = sbt("identb", [128, 128], BF16)
    identf = sbt("identf", [128, 128], F32)
    stats16 = sbt("stats16", [128, 8, NT], F32)
    sq_scr = sbt("sq_scr", [128, 1024], F32)
    hoz_idx = sbt("hoz_idx_s", [128, NT], U32)
    zeros2k = sbt("zeros2k", [128, 1024], BF16)
    ones16 = sbt("ones16", [128, 16], F32)
    gwT_s = sbt("gwT_s", [128, 8, E_EXP], F32)
    psel_s = sbt("psel_s", [8, 1], F32)
    iota8_s = sbt("iota8_s", [8, 1], F32)
    xe_idx = sbt("xe_idx_s", [128, 48], U32)
    iota_row8 = sbt("iota_row8_s", [128, 8], F32)
    ksel_s = sbt("ksel_s", [16, 2], F32)
    iota16_s = sbt("iota16_s", [16, 1], F32)
    kofs_s = sbt("kofs_s", [2, 1], F32)
    cnt_sb = sbt("cnt_sb", [8, 16], F32)
    offs_c = sbt("offs_c", [8, 2], F32)
    routc = sbt("routc", [128, 12, NT], F32)
    slotcv = sbt("slotcv", [128, 4, NT], U32)
    ones_row = sbt("ones_row", [1, 128], BF16)
    xpj_s = sbt("xpj_s", [128, 16, 33], BF16)
    edge_sel = sbt("edge_sel_s", [24, 4], BF16)
    convw_s = sbt("convw_s", [128, 16, DC], F32)
    convb_s = sbt("convb_s", [128, 16], F32)
    dtw_s = sbt("dtw_s", [128, 8], F32)
    dtb_s = sbt("dtb_s", [128, 8], F32)
    Acol_s = sbt("Acol_s", [128, 8, NST], F32)
    Dp_s = sbt("Dp_s", [128, 8], F32)
    hlast = sbt("hlast", [128, 128], F32)
    ag1u_idx = sbt("ag1u_idx_s", [128, 8, 2], U32)
    bc_idx = sbt("bc_idx_s", [128, 2, 33], U32)
    ag2_idx = sbt("ag2_idx_s", [128, 16, 4], U32)
    binc_s = sbt("binc_s", [M_FFN, 1], F32)
    w1_s = sbt("w1_s", [M_FFN, INNER], F32)
    b1_s = sbt("b1_s", [M_FFN, INNER], F32)
    w2_s = sbt("w2_s", [M_FFN, INNER], F32)
    b2_s = sbt("b2_s", [M_FFN, 1], F32)
    boutT_s = sbt("boutT_s", [128, 8], F32)
    win_s = sbt("win_s", [128, 8, M_FFN], BF16)

    edgeT = sbt("edgeT", [128, 8, 4], BF16)
    e3 = sbt("e3", [128, 16, 4], F32)

    lnstdbg = sbt("lnstdbg", [128, 8, NT], F32) if _STAGE in ("attn", "moe") else None
    psum = ctx.enter_context(nc.psum_tensor("ps", [128, 8, 512], F32))

    smap = SbufMap(nc)
    vh = smap.at("vh", [128, 32, 1024], BF16, 0, 1, 2)
    acc = smap.at("acc", [128, NT, 1024], BF16, 64 * KB_, 1, 2)
    khT = smap.at("khT", [128, 32, 128], BF16, 96 * KB_, 1, 2)
    qhT = smap.at("qhT", [128, NT, 128], BF16, 104 * KB_, 1, 2)
    sstr = smap.at("sstr", [128, 32, 128], BF16, 108 * KB_, 1, 2)
    xtc = smap.at("xtc", [128, 2, 8, 512], BF16, 116 * KB_, 1, 2)
    wvb = smap.at("wvb", [128, 2, 8, 1024], BF16, 132 * KB_, 1, 2)
    wqkb = smap.at("wqkb", [128, 2, 2, 8, 128], BF16, 164 * KB_, 1, 2)
    qmb = smap.at("qmb", [128, 2, 32, 128], BF16, 172 * KB_, 1, 2)
    x1f = smap.at("x1f", [128, NT, 1024], F32, 0, 2, 3.9)
    x1b = smap.at("x1b", [128, NT, 1024], BF16, 128 * KB_, 2, 3.5)
    hoz = smap.at("hoz", [128, NT, 1024], BF16, 160 * KB_, 2, 3.0)
    xres = smap.at("xres", [128, NT, 1024], F32, 64 * KB_, 2, 3.0)
    routr = smap.at("routr", [128, 5, NTOK], F32, 64 * KB_, 3.0, 3.5)
    x1Th = smap.at("x1Th", [128, 8, 512], F32, 104 * KB_, 3.0, 3.15)
    routp = smap.at("routp", [128, 3, NTOK], F32, 104 * KB_, 3.16, 3.5)
    routq = smap.at("routq", [128, 4, NTOK], F32, 160 * KB_, 3.0, 3.5)
    gu = smap.at("gu", [128, 32, 768], BF16, 64 * KB_, 3.5, 3.8)
    wgu = smap.at("wgu", [128, 2, 2, 8, 512], BF16, 112 * KB_, 3.5, 3.8)
    xe = smap.at("xe", [128, 6, 1024], BF16, 160 * KB_, 3.5, 3.8)
    xeT = smap.at("xeT", [128, 8, 768], BF16, 172 * KB_, 3.5, 3.8)
    wob = smap.at("wob", [128, 2, 1024], BF16, 184 * KB_, 3.5, 3.8)
    gsil = smap.at("gsil", [128, 2, 384], BF16, 188 * KB_, 3.5, 3.8)
    eos = smap.at("eos", [128, 2, 1024], BF16, 144 * KB_, 3.5, 3.8)
    cmb = smap.at("cmb", [128, 2, 2, 1024], BF16, 112 * KB_, 3.8, 3.95)
    x2preA = smap.at("x2preA", [128, 8, 1024], F32, 64 * KB_, 3.8, 3.96)
    x2preB = smap.at("x2preB", [128, 8, 1024], F32, 128 * KB_, 3.8, 3.96)
    x2f = smap.at("x2f", [128, NT, 1024], F32, 0, 3.97, 5.6)
    x2b = smap.at("x2b", [128, NT, 1024], BF16, 64 * KB_, 3.97, 4.2)
    x2T = smap.at("x2T", [128, 8, 1024], BF16, 96 * KB_, 4.0, 4.18)
    zst = smap.at("zst", [128, 2048], BF16, 112 * KB_, 4.0, 4.28)
    xinT = smap.at("xinT", [128, 16, 2048], BF16, 128 * KB_, 4.0, 4.45)
    ctmp = smap.at("ctmp", [128, 2048], F32, 116 * KB_, 4.185, 4.285)
    bcb = smap.at("bcb", [128, 2, 512], BF16, 64 * KB_, 4.3, 4.85)
    ccb = smap.at("ccb", [128, 2, 512], BF16, 66 * KB_, 4.3, 4.85)
    dub = smap.at("dub", [128, 2, 512], BF16, 72 * KB_, 4.3, 4.85)
    hball = smap.at("hball", [128, 8, 512], BF16, 74 * KB_, 4.3, 4.85)
    hcb = smap.at("hcb", [128, 2, 512], BF16, 82 * KB_, 4.3, 4.85)
    ytst = smap.at("ytst", [128, 2, 512], BF16, 84 * KB_, 4.3, 4.85)
    dtbc = smap.at("dtbc", [128, 512], BF16, 88 * KB_, 4.3, 4.85)
    dlt8 = smap.at("dlt8", [128, 8, 512], BF16, 90 * KB_, 4.3, 4.85)
    dab = smap.at("dab", [128, 2, 512], F32, 98 * KB_, 4.3, 4.85)
    du8 = smap.at("du8", [128, 8, 512], BF16, 102 * KB_, 4.3, 4.85)
    uT_sc = smap.at("uT_sc", [128, 8, 4096], BF16, 128 * KB_, 4.5, 4.85)
    ygT = smap.at("ygT", [128, 16, 2048], BF16, 128 * KB_, 4.86, 4.94)
    zrl = smap.at("zrl", [128, 2, 2048], BF16, 100 * KB_, 4.86, 4.94)
    aoT = smap.at("aoT", [128, 8, 2048], BF16, 64 * KB_, 4.86, 4.96)
    x3f = smap.at("x3f", [128, NT, 1024], F32, 96 * KB_, 4.975, 5.9)
    x3b = smap.at("x3b", [128, NT, 1024], BF16, 160 * KB_, 4.95, 5.9)
    x3T = smap.at("x3T", [128, 8, NTOK], BF16, 0, 5.62, 5.8)
    stg = smap.at("stg", [128, 1024], F32, 64 * KB_, 5.8, 6.0)
    wout_s = smap.at("wout_s", [M_FFN, D], BF16, 68 * KB_, 5.3, 6.0)
    sfn = smap.at("sfn", [64, 4, NTOK], F32, 32 * KB_, 5.65, 5.85)
    foT = smap.at("foT", [128, 8, NTOK], BF16, 0, 5.81, 5.95)

    p = Prog(nc)
    LD = p.sem("ld")
    XT = p.sem("xts")
    WL = p.sem("wl")
    QM = p.sem("qm")
    MM = p.sem("mm")
    EV = p.sem("ev")
    ACC = p.sem("accs")
    AGS = p.sem("ags")
    STO = p.sem("sto")
    VS = p.sem("vs")
    GP = p.sem("gp")

    p.on("gpsimd", lambda e: e.dma_start(out=ident[:], in_=ident_in), incs=((LD, 16),))
    p.on("sync", lambda e: e.dma_start(out=identf[:], in_=ident_in), incs=((LD, 16),))
    p.on("sync", lambda e: e.dma_start(out=hoz_idx[:], in_=hoz_idx_in), incs=((LD, 16),))
    p.on("sync", lambda e: e.dma_start(out=iota_row8[:], in_=iota_row8_in), incs=((LD, 16),))
    p.on("sync", lambda e: e.dma_start(out=ksel_s[:], in_=ksel_in), incs=((LD, 16),))
    p.on("sync", lambda e: e.dma_start(out=iota16_s[:], in_=iota16_in), incs=((LD, 16),))
    p.on("sync", lambda e: e.dma_start(out=kofs_s[:], in_=kofs_in), incs=((LD, 16),))
    p.on("sync", lambda e: e.dma_start(out=psel_s[:], in_=prefix_sel_in), incs=((LD, 16),))
    p.on("sync", lambda e: e.dma_start(out=iota8_s[:], in_=iota8_in), incs=((LD, 16),))
    p.on("sync", lambda e: e.dma_start(out=xe_idx[:], in_=xe_idx_in), incs=((LD, 16),))
    p.on("sync", lambda e: e.dma_start(out=gwT_s[:], in_=gate_wT.rearrange("(c p) m -> p c m", p=128)), incs=((LD, 16),))
    p.on("gpsimd", lambda e: e.memset(zeros2k[:], 0.0), incs=((LD, 1),))
    p.on("gpsimd", lambda e: e.memset(ones16[:], 1.0), incs=((LD, 1),))
    CONST = p.now(LD)

    bufdep = {}
    def waitdep(key):
        return [bufdep[key]] if key in bufdep else []
    def setdep(key, sem):
        bufdep[key] = (sem, sem.count)

    for h in range(NH):
        par = h & 1
        def ld_wq(e, h=h, par=par):
            return e.dma_start(out=wqkb[:, par, 0],
                               in_=wqT.rearrange("(c p) m -> p c m", p=128)[:, :, h * DKH:(h + 1) * DKH])
        p.on("gpsimd", ld_wq, waits=waitdep(("wqk", par)), incs=((WL, 16),))
        def ld_wk(e, h=h, par=par):
            return e.dma_start(out=wqkb[:, par, 1],
                               in_=wkT.rearrange("(c p) m -> p c m", p=128)[:, :, h * DKH:(h + 1) * DKH])
        p.on("gpsimd", ld_wk, incs=((WL, 16),))
        def ld_wv(e, h=h, par=par):
            return e.dma_start(out=wvb[:, par],
                               in_=wvT[h].rearrange("(c p) v -> p c v", p=128))
        p.on("gpsimd", ld_wv, waits=waitdep(("wv", par)), incs=((WL, 16),))
        wl_ready = p.now(WL)

        for ci in range(8):
            st0 = ci * 512
            cpar = ci & 1
            def ld_xt(e, st0=st0, cpar=cpar):
                return e.dma_start(out=xtc[:, cpar],
                                   in_=x0T_b[:, st0:st0 + 512].rearrange("(c p) k -> p c k", p=128))
            p.on("gpsimd", ld_xt, waits=waitdep(("xt", cpar)) + [CONST], incs=((XT, 16),))
            xt_ready = p.now(XT)

            def mm_k(e, par=par, cpar=cpar):
                for c in range(8):
                    inst = e.matmul(psum[:, 0, :], wqkb[:, par, 1, c], xtc[:, cpar, c],
                                    start=(c == 0), stop=(c == 7))
                return inst
            p.on("tensor", mm_k, waits=[xt_ready, wl_ready] + waitdep(("bank", 0)), incs=((MM, 1),))
            def ev_k(e, ci=ci):
                return e.copy(out=khT.ap().rearrange("p a b -> p (a b)")[:, ci * 512:(ci + 1) * 512],
                              in_=psum[:, 0, :])
            p.on("scalar", ev_k, waits=[p.now(MM)], incs=((EV, 1),))
            setdep(("bank", 0), EV)

            for tt in range(4):
                kb = ci * 4 + tt
                for v2 in range(2):
                    bank = 2 + ((kb * 2 + v2) & 1)
                    def mm_v(e, par=par, cpar=cpar, tt=tt, v2=v2, bank=bank):
                        for c in range(8):
                            inst = e.matmul(psum[:, bank, :],
                                            xtc[:, cpar, c, tt * 128:(tt + 1) * 128],
                                            wvb[:, par, c, v2 * 512:(v2 + 1) * 512],
                                            start=(c == 0), stop=(c == 7))
                        return inst
                    p.on("tensor", mm_v, waits=waitdep(("bank", bank)), incs=((MM, 1),))
                    def ev_v(e, kb=kb, v2=v2, bank=bank):
                        return e.copy(out=vh[:, kb, v2 * 512:(v2 + 1) * 512],
                                      in_=psum[:, bank, :])
                    p.on("scalar", ev_v, waits=[p.now(MM)], incs=((EV, 1),))
                    setdep(("bank", bank), EV)
            setdep(("xt", cpar), MM)

        for qb in range(4):
            cpar = qb & 1
            def ld_xq(e, qb=qb, cpar=cpar):
                return e.dma_start(out=xtc[:, cpar],
                                   in_=xqT_in[:, qb * 512:(qb + 1) * 512].rearrange("(c p) k -> p c k", p=128))
            p.on("gpsimd", ld_xq, waits=waitdep(("xt", cpar)), incs=((XT, 16),))
            xq_ready = p.now(XT)
            for j in range(4):
                li = qb * 4 + j
                def mm_q(e, par=par, cpar=cpar, j=j):
                    for c in range(8):
                        inst = e.matmul(psum[:, 1, 0:128], wqkb[:, par, 0, c],
                                        xtc[:, cpar, c, j * 128:(j + 1) * 128],
                                        start=(c == 0), stop=(c == 7))
                    return inst
                p.on("tensor", mm_q, waits=[xq_ready] + waitdep(("bank", 1)), incs=((MM, 1),))
                def ev_q(e, li=li):
                    return e.copy(out=qhT[:, li], in_=psum[:, 1, 0:128])
                p.on("scalar", ev_q, waits=[p.now(MM)], incs=((EV, 1),))
                setdep(("bank", 1), EV)
            setdep(("xt", cpar), MM)
        setdep(("wqk", par), MM)
        setdep(("wv", par), MM)
        kv_built = p.now(EV)

        for li in range(NT):
            nkb = KBCNT[li]
            mpar = li & 1
            def ld_qm(e, li=li, mpar=mpar, nkb=nkb):
                return e.dma_start(out=qmb[:, mpar, 0:nkb],
                                   in_=qmask_in[li, 0:nkb].rearrange("a p b -> p a b"))
            p.on("sync", ld_qm, waits=waitdep(("qm", mpar)), incs=((QM, 16),))
            qm_ready = p.now(QM)

            for kb in range(nkb):
                bank = 4 + (kb & 1)
                def mm_s(e, kb=kb, li=li, bank=bank):
                    return e.matmul(psum[:, bank, 0:128], khT[:, kb], qhT[:, li],
                                    start=True, stop=True)
                p.on("tensor", mm_s, waits=[kv_built, qm_ready] + waitdep(("bank", bank)),
                     incs=((MM, 1),))
                def ev_s(e, kb=kb, bank=bank, mpar=mpar):
                    return e.scalar_tensor_tensor(out=sstr[:, kb], in0=psum[:, bank, 0:128],
                                                  scalar=0.0, in1=qmb[:, mpar, kb],
                                                  op0=ALU.max, op1=ALU.mult)
                p.on("vector", ev_s, waits=[p.now(MM)], incs=((EV, 1),))
                setdep(("bank", bank), EV)
            s_done = p.now(EV)
            setdep(("qm", mpar), EV)

            for v2 in range(2):
                bank = 6 + v2
                def mm_av(e, li=li, nkb=nkb, v2=v2, bank=bank):
                    for kb in range(nkb):
                        inst = e.matmul(psum[:, bank, :], sstr[:, kb],
                                        vh[:, kb, v2 * 512:(v2 + 1) * 512],
                                        start=(kb == 0), stop=(kb == nkb - 1))
                    return inst
                p.on("tensor", mm_av, waits=[s_done] + waitdep(("bank", bank)),
                     incs=((MM, 1),))
                if h == 0:
                    def ev_av(e, li=li, v2=v2, bank=bank):
                        return e.tensor_copy(acc[:, li, v2 * 512:(v2 + 1) * 512],
                                             psum[:, bank, :])
                else:
                    def ev_av(e, li=li, v2=v2, bank=bank):
                        sl = acc[:, li, v2 * 512:(v2 + 1) * 512]
                        return e.tensor_tensor(sl, sl, psum[:, bank, :], ALU.add)
                p.on("vector", ev_av, waits=[p.now(MM)], incs=((ACC, 1),))
                setdep(("bank", bank), ACC)

    attn_done = p.now(ACC)

    for li in range(NT):
        def ag0_store(e, li=li):
            return e.dma_start(out=ag0_in[li * 128:(li + 1) * 128, :], in_=acc[:, li])
        p.on("gpsimd", ag0_store, waits=([attn_done] if li == 0 else []), incs=((STO, 16),))

    def ag0_run(e):
        return e.collective_compute("AllGather", ALU.bypass, ins=[ag0_in.opt()],
                                    outs=[ag0_out.opt()], replica_groups=[list(range(8))])
    p.on("gpsimd", ag0_run, waits=[p.now(STO)], incs=((AGS, 1),))

    for li in range(NT):
        def ag0_load(e, li=li):
            return e.indirect_dma_start(
                out=hoz[:, li], out_offset=None, in_=ag0_out,
                in_offset=IndirectOffsetOnAxis(ap=hoz_idx[:, li:li + 1], axis=0))
        p.on("gpsimd", ag0_load, waits=([p.now(AGS), CONST] if li == 0 else []),
             incs=((GP, 16),))
    hoz_ld = p.now(GP)

    def ld_x0(e):
        return e.dma_start(out=x1f[:], in_=x0_own.rearrange("(a p) b -> p a b", p=128))
    p.on("sync", ld_x0, waits=[attn_done], incs=((XT, 16),))
    x0_ld = p.now(XT)

    for li in range(NT):
        def res(e, li=li):
            return e.scalar_tensor_tensor(out=xres[:, li], in0=x1f[:, li], scalar=2.0,
                                          in1=hoz[:, li], op0=ALU.mult, op1=ALU.add)
        p.on("vector", res, waits=([hoz_ld, x0_ld] if li == 0 else []), incs=((VS, 1),))
    _emit_ln_block(p, NT, xres, x1f, x1b, stats16, sq_scr, VS,
                   lndbg=(lnstdbg if "dbg_st" in dbg else None))
    x1_done = p.now(VS)


    STOP = {"attn": 1, "moe": 2, "l1": 9, "full": 9, "fulldbg": 9}[_STAGE]
    if STOP >= 2:
        MO = p.sem("mo")
        WM = p.sem("wm")
        SC = p.sem("scat")
        psum_bf = psum.ap().bitcast(BF16)

        for qp in range(4):
            for j in range(4):
                li = qp * 4 + j
                for c in range(8):
                    bank = 6 + (c & 1)
                    def tr_x1(e, li=li, c=c, bank=bank):
                        return e.transpose(psum[:, bank, 0:128], x1f[:, li, c * 128:(c + 1) * 128], identf[:])
                    p.on("tensor", tr_x1, waits=[x1_done] + waitdep(("bank", bank)), incs=((MM, 1),))
                    def ev_tx(e, j=j, c=c, bank=bank):
                        return e.tensor_copy(x1Th[:, c, j * 128:(j + 1) * 128], psum[:, bank, 0:128])
                    p.on("vector", ev_tx, waits=[p.now(MM)], incs=((EV, 1),))
                    setdep(("bank", bank), EV)
            def mm_lg(e, qp=qp):
                for c in range(8):
                    inst = e.matmul(psum[0:8, 0, :], gwT_s[:, c], x1Th[:, c, :],
                                    start=(c == 0), stop=(c == 7))
                return inst
            p.on("tensor", mm_lg, waits=[p.now(EV), CONST] + waitdep(("bank", 0)), incs=((MM, 1),))
            def ev_lg(e, qp=qp):
                return e.tensor_copy(routr[0:8, 0, qp * 512:(qp + 1) * 512], psum[0:8, 0, :])
            p.on("vector", ev_lg, waits=[p.now(MM)], incs=((EV, 1),))
            setdep(("bank", 0), EV)
            setdep(("bank", 6), EV)
            setdep(("bank", 7), EV)
        lg_done = p.now(EV)

        ltm = sbt("ltm", [128, 8], F32)
        top8 = sbt("top8", [128, 8], F32)
        tidx = sbt("tidx", [128, 8], U32)
        oh_tm = sbt("oh_tm", [128, 2, NT, 8], F32)
        for li in range(NT):
            def tr_lg(e, li=li):
                return e.transpose(psum[:, 6, 0:8], routr[0:8, 0, li * 128:(li + 1) * 128], identf[0:8, 0:8])
            p.on("tensor", tr_lg, waits=[lg_done] + waitdep("rbank6"), incs=((MM, 1),))
            def ev_tl(e):
                return e.tensor_copy(ltm[:], psum[:, 6, 0:8])
            p.on("vector", ev_tl, waits=[p.now(MM)], incs=((MO, 1),))
            setdep("rbank6", MO)
            def mx8(e):
                return e.max(top8[:], ltm[:])
            p.on("vector", mx8, waits=[p.now(MO)], incs=((MO, 1),))
            def mxi(e):
                return e.max_index(tidx[:], top8[:], ltm[:])
            p.on("vector", mxi, waits=[p.now(MO)], incs=((MO, 1),))
            def cps(e, li=li):
                e.tensor_copy(routc[:, 0, li:li + 1], top8[:, 0:1])
                e.tensor_copy(routc[:, 1, li:li + 1], top8[:, 1:2])
                e.tensor_copy(routc[:, 2, li:li + 1], tidx[:, 0:1])
                return e.tensor_copy(routc[:, 3, li:li + 1], tidx[:, 1:2])
            p.on("vector", cps, waits=[p.now(MO)], incs=((MO, 1),))
            def ohs(e, li=li):
                e.tensor_scalar(out=oh_tm[:, 0, li], in0=iota_row8[:], scalar1=routc[:, 2, li:li + 1],
                                scalar2=None, op0=ALU.is_equal, op1=ALU.bypass)
                return e.tensor_scalar(out=oh_tm[:, 1, li], in0=iota_row8[:], scalar1=routc[:, 3, li:li + 1],
                                       scalar2=None, op0=ALU.is_equal, op1=ALU.bypass)
            p.on("vector", ohs, waits=[p.now(MO)], incs=((MO, 1),))
        def c_ew(e):
            e.tensor_tensor(routc[:, 10, :], routc[:, 0, :], routc[:, 1, :], ALU.subtract)
            return e.nop()
        p.on("vector", c_ew, waits=[p.now(MO)], incs=((MO, 1),))
        def c_sig(e):
            return e.activation(out=routc[:, 4, :], in_=routc[:, 10, :], func=AF.Sigmoid)
        p.on("scalar", c_sig, waits=[p.now(MO)], incs=((MO, 1),))
        def c_ew1(e):
            return e.tensor_scalar(out=routc[:, 5, :], in0=routc[:, 4, :], scalar1=-1.0,
                                   scalar2=1.0, op0=ALU.mult, op1=ALU.add)
        p.on("vector", c_ew1, waits=[p.now(MO)], incs=((MO, 1),))

        for k in range(2):
            for li in range(NT):
                bank = 6 + (li & 1)
                def tr_oh(e, k=k, li=li, bank=bank):
                    return e.transpose(psum[0:8, bank, 0:128], oh_tm[:, k, li], identf[:])
                p.on("tensor", tr_oh, waits=[p.now(MO)] + waitdep(("bank", bank)), incs=((MM, 1),))
                def ev_oh(e, k=k, li=li, bank=bank):
                    return e.tensor_copy(routr[0:8, 1 + k, li * 128:(li + 1) * 128],
                                         psum[0:8, bank, 0:128])
                p.on("vector", ev_oh, waits=[p.now(MM)], incs=((MO, 1),))
                setdep(("bank", bank), MO)
        mask_done = p.now(MO)
        mask_r = lambda k: routr[0:8, 1 + k, :]
        cum_r = lambda k: routr[0:8, 3 + k, :]
        for k in range(2):
            def c_cum(e, k=k):
                return e.tensor_tensor_scan(out=cum_r(k), data0=mask_r(k), data1=mask_r(k),
                                            initial=0.0, op0=ALU.add, op1=ALU.bypass)
            p.on("vector", c_cum, waits=[mask_done], incs=((MO, 1),))
        for k in range(2):
            def tr_cnt(e, k=k):
                return e.transpose(psum[0:1, 6, k * 8:(k + 1) * 8],
                                   routr[0:8, 3 + k, NTOK - 1:NTOK], identf[0:8, 0:8])
            p.on("tensor", tr_cnt, waits=[p.now(MO)] + (waitdep(("bank", 6)) if k == 0 else []),
                 incs=((MM, 1),))
        def ev_cnt(e):
            return e.tensor_copy(cnt_sb[0:1, :], psum[0:1, 6, 0:16])
        p.on("vector", ev_cnt, waits=[p.now(MM)], incs=((MO, 1),))
        setdep(("bank", 6), MO)
        def st_cnt(e):
            return e.dma_start(out=cnt_in, in_=cnt_sb[0:1, :])
        p.on("gpsimd", st_cnt, waits=[p.now(MO)], incs=((STO, 16),))
        def ag_cnt(e):
            return e.collective_compute("AllGather", ALU.bypass, ins=[cnt_in.opt()],
                                        outs=[cnt_out.opt()], replica_groups=[list(range(8))])
        p.on("gpsimd", ag_cnt, waits=[p.now(STO)], incs=((AGS, 1),))
        def ld_cnt(e):
            return e.dma_start(out=cnt_sb[:], in_=cnt_out)
        p.on("gpsimd", ld_cnt, waits=[p.now(AGS)], incs=((GP, 16),))
        def mm_pre(e):
            return e.matmul(psum[0:1, 0, 0:16], psel_s[:], cnt_sb[:], start=True, stop=True)
        p.on("tensor", mm_pre, waits=[p.now(GP)] + waitdep(("bank", 0)), incs=((MM, 1),))
        def ev_pre(e):
            return e.tensor_copy(routq[0:1, 2, 0:16], psum[0:1, 0, 0:16])
        p.on("vector", ev_pre, waits=[p.now(MM)], incs=((MO, 1),))
        for k in range(2):
            def tr_offk(e, k=k):
                return e.transpose(psum[0:8, 6, k * 2:k * 2 + 1],
                                   routq[0:1, 2, k * 8:(k + 1) * 8], identf[0:1, 0:1])
            p.on("tensor", tr_offk, waits=[p.now(MO)] + (waitdep(("bank", 6)) if k == 0 else []),
                 incs=((MM, 1),))
        def ev_offk(e):
            e.tensor_copy(offs_c[0:8, 0:1], psum[0:8, 6, 0:1])
            return e.tensor_copy(offs_c[0:8, 1:2], psum[0:8, 6, 2:3])
        p.on("vector", ev_offk, waits=[p.now(MM)], incs=((MO, 1),))
        setdep(("bank", 6), MO)
        for k in range(2):
            def c_posx(e, k=k):
                return e.tensor_tensor(cum_r(k), cum_r(k), mask_r(k), ALU.subtract)
            p.on("vector", c_posx, waits=[p.now(MO)], incs=((MO, 1),))
        for k in range(2):
            def c_prod(e, k=k):
                return e.tensor_tensor(routq[0:8, k, :], cum_r(k), mask_r(k), ALU.mult)
            p.on("vector", c_prod, waits=[p.now(MO)], incs=((MO, 1),))
        prods_done = p.now(MO)
        for (qi, src_fn, lhs_fn) in [
            (0, lambda k: routq[0:8, k, :], lambda k: ones16[0:8, 0:1]),
            (1, lambda k: mask_r(k), lambda k: offs_c[0:8, k:k + 1]),
            (2, lambda k: mask_r(k), lambda k: iota8_s[:]),
        ]:
            for k in range(2):
                for c4 in range(4):
                    bank = c4 & 1
                    def mm_row(e, qi=qi, k=k, c4=c4, bank=bank, src_fn=src_fn, lhs_fn=lhs_fn):
                        return e.matmul(psum[0:1, bank, :], lhs_fn(k),
                                        src_fn(k)[:, c4 * 512:(c4 + 1) * 512], start=True, stop=True)
                    p.on("tensor", mm_row, waits=[prods_done] + waitdep(("bank", bank)), incs=((MM, 1),))
                    if k == 0:
                        def ev_rw(e, qi=qi, c4=c4, bank=bank):
                            return e.tensor_copy(routp[0:1, qi, c4 * 512:(c4 + 1) * 512],
                                                 psum[0:1, bank, :])
                    else:
                        def ev_rw(e, qi=qi, c4=c4, bank=bank):
                            return e.tensor_copy(routq[0:1, 3, c4 * 512:(c4 + 1) * 512],
                                                 psum[0:1, bank, :])
                    p.on("vector", ev_rw, waits=[p.now(MM)], incs=((MO, 1),))
                    setdep(("bank", bank), MO)
                if k == 1:
                    def mv_row(e, qi=qi):
                        return e.dma_start(out=routp[1:2, qi, :], in_=routq[0:1, 3, :])
                    p.on("gpsimd", mv_row, waits=[p.now(MO)], incs=((GP, 16),))
        rows_done = p.now(GP)
        PT = lambda: routp[0:2, 0, :]
        OT = lambda: routp[0:2, 1, :]
        ET = lambda: routp[0:2, 2, :]
        def q_v1(e):
            e.tensor_tensor(routq[0:2, 0, :], PT(), OT(), ALU.add)
            return e.nop()
        p.on("vector", q_v1, waits=[rows_done], incs=((MO, 1),))
        def q_v2(e):
            e.tensor_scalar(out=routq[0:2, 0, :], in0=routq[0:2, 0, :], scalar1=float(CAP_G),
                            scalar2=None, op0=ALU.is_lt, op1=ALU.bypass)
            return e.nop()
        p.on("vector", q_v2, waits=[p.now(MO)], incs=((MO, 1),))
        def q_v3(e):
            e.tensor_scalar(out=routq[0:2, 1, :], in0=PT(), scalar1=float(CAP_L),
                            scalar2=None, op0=ALU.is_lt, op1=ALU.bypass)
            return e.nop()
        p.on("vector", q_v3, waits=[p.now(MO)], incs=((MO, 1),))
        def q_v4(e):
            return e.tensor_tensor(routq[0:2, 2, :], routq[0:2, 0, :], routq[0:2, 1, :], ALU.mult)
        p.on("vector", q_v4, waits=[p.now(MO)], incs=((MO, 1),))
        VALP = lambda: routq[0:2, 2, :]
        def q_sl(e):
            e.tensor_scalar(out=routq[0:2, 0, :], in0=ET(), scalar1=768.0,
                            scalar2=kofs_s[:], op0=ALU.mult, op1=ALU.add)
            return e.nop()
        p.on("vector", q_sl, waits=[p.now(MO)], incs=((MO, 1),))
        def q_sl2(e):
            return e.tensor_tensor(routq[0:2, 0, :], routq[0:2, 0, :], PT(), ALU.add)
        p.on("vector", q_sl2, waits=[p.now(MO)], incs=((MO, 1),))
        def q_slg(e):
            return e.tensor_tensor(routq[0:2, 1, :], routq[0:2, 0, :], VALP(), ALU.mult)
        p.on("vector", q_slg, waits=[p.now(MO)], incs=((MO, 1),))
        def q_sls(e):
            e.tensor_scalar(out=routq[0:2, 3, :], in0=VALP(), scalar1=-1000000.0,
                            scalar2=1000000.0, op0=ALU.mult, op1=ALU.add)
            return e.nop()
        p.on("vector", q_sls, waits=[p.now(MO)], incs=((MO, 1),))
        def q_sls2(e):
            return e.tensor_tensor(routq[0:2, 3, :], routq[0:2, 0, :], routq[0:2, 3, :], ALU.add)
        p.on("vector", q_sls2, waits=[p.now(MO)], incs=((MO, 1),))
        sb_done = p.now(MO)
        for (src, dst) in [(3, 0), (1, 2), (2, 4)]:
            def mv_blk(e, src=src, dst=dst):
                return e.dma_start(out=routr[dst:dst + 2, 0, :], in_=routq[0:2, src, :])
            p.on("gpsimd", mv_blk, waits=[sb_done], incs=((GP, 16),))
        srow_done = p.now(GP)
        for li in range(NT):
            bank = 6 + (li & 1)
            def tr_sv(e, li=li, bank=bank):
                return e.transpose(psum[:, bank, 0:6], routr[0:6, 0, li * 128:(li + 1) * 128],
                                   identf[0:6, 0:6])
            p.on("tensor", tr_sv, waits=[srow_done] + waitdep(("bank", bank)), incs=((MM, 1),))
            def ev_sv(e, li=li, bank=bank):
                e.tensor_copy(slotcv[:, 0:2, li], psum[:, bank, 0:2])
                e.tensor_copy(slotcv[:, 2:4, li], psum[:, bank, 2:4])
                return e.tensor_copy(routc[:, 8:10, li], psum[:, bank, 4:6])
            p.on("vector", ev_sv, waits=[p.now(MM)], incs=((MO, 1),))
            setdep(("bank", bank), MO)
        def c_w(e):
            e.tensor_tensor(routc[:, 6, :], routc[:, 4, :], routc[:, 8, :], ALU.mult)
            return e.tensor_tensor(routc[:, 7, :], routc[:, 5, :], routc[:, 9, :], ALU.mult)
        p.on("vector", c_w, waits=[p.now(MO)], incs=((MO, 1),))
        rout_done = p.now(MO)
        if "dbg_rc" in dbg:
            def mark(e):
                return e.memset(routc[:, 11, :], 7.0)
            p.on("vector", mark, waits=[rout_done], incs=((MO, 1),))
            def st_rc(e):
                return e.dma_start(out=dbg["dbg_rc"], in_=routc[:])
            p.on("sync", st_rc, waits=[p.now(MO)], incs=((STO, 16),))
            slotf = sbt("slotf_dbg", [128, 4, NT], F32)
            def cvt_slot(e):
                return e.tensor_copy(slotf[:], slotcv[:])
            p.on("vector", cvt_slot, waits=[rout_done], incs=((MO, 1),))
            def st_slot(e):
                return e.dma_start(out=dbg["dbg_slot"], in_=slotf[:])
            p.on("sync", st_slot, waits=[p.now(MO)], incs=((STO, 16),))
            def st_x1dbg(e):
                return e.dma_start(out=dbg["dbg_x1"].rearrange("(a p) b -> p a b", p=128), in_=x1f[:])
            p.on("gpsimd", st_x1dbg, waits=[x1_done], incs=((STO, 16),))

        for r in range(NSLOT // 128):
            def z_xd(e, r=r):
                return e.dma_start(out=xd[r * 128:(r + 1) * 128, :], in_=zeros2k[:])
            p.on("sync", z_xd, waits=([CONST, x1_done] if r == 0 else []), incs=((SC, 16),))
        zdone = p.now(SC)
        for rep in range(1):
            for k in range(2):
                for li in range(NT):
                    def scat(e, k=k, li=li):
                        return e.indirect_dma_start(
                            out=xd, out_offset=IndirectOffsetOnAxis(ap=slotcv[:, k, li:li + 1], axis=0),
                            in_=x1b[:, li], in_offset=None,
                            bounds_check=NSLOT - 1, oob_is_err=False)
                    p.on("gpsimd", scat, waits=([rout_done, zdone] if (rep == 0 and k == 0 and li == 0) else []),
                         incs=((SC, 16),))
        scat_done = p.now(SC)
        dly = sq_scr
        def d0(e):
            return e.memset(dly[:, 0:512], 1.0)
        p.on("gpsimd", d0, waits=[scat_done], incs=((MO, 1),))
        for _ in range(300):
            def dst(e):
                return e.tensor_scalar(out=dly[:, 0:512], in0=dly[:, 0:512], scalar1=1.0000001,
                                       scalar2=None, op0=ALU.mult, op1=ALU.bypass)
            p.on("vector", dst, waits=[p.now(MO)], incs=((MO, 1),))
        scat_done = p.now(MO)

        for ee in range(E_EXP):
            wpar = ee & 1
            for j in range(6):
                def g_xe(e, ee=ee, j=j):
                    return e.indirect_dma_start(
                        out=xe[:, j], out_offset=None, in_=xd,
                        in_offset=IndirectOffsetOnAxis(ap=xe_idx[:, ee * 6 + j:ee * 6 + j + 1], axis=0))
                p.on("gpsimd", g_xe, waits=([scat_done] if (ee == 0 and j == 0) else []) + waitdep(("xe", j)),
                     incs=((GP, 16),))
            xe_ld = p.now(GP)
            for j in range(6):
                for c in range(8):
                    bank = 6 + ((j * 8 + c) & 1)
                    def tr_xe(e, j=j, c=c, bank=bank):
                        return e.transpose(psum_bf[:, bank, 0:128], xe[:, j, c * 128:(c + 1) * 128], ident[:])
                    p.on("tensor", tr_xe, waits=[xe_ld] + waitdep(("bank", bank)), incs=((MM, 1),))
                    def ev_xe(e, j=j, c=c, bank=bank):
                        return e.tensor_copy(xeT[:, c, j * 128:(j + 1) * 128], psum_bf[:, bank, 0:128])
                    p.on("vector", ev_xe, waits=[p.now(MM)], incs=((EV, 1),))
                    setdep(("bank", bank), EV)
                setdep(("xe", j), MM)
            xeT_done = p.now(EV)

            for fc in range(8):
                wp = fc & 1
                def ld_wg(e, ee=ee, fc=fc, wp=wp):
                    return e.dma_start(out=wgu[:, wp, 0],
                                       in_=wgT_in[ee].rearrange("(c p) f -> p c f", p=128)[:, :, fc * 512:(fc + 1) * 512])
                p.on("gpsimd", ld_wg, waits=waitdep(("wgu", wp)), incs=((WM, 16),))
                def ld_wu(e, ee=ee, fc=fc, wp=wp):
                    return e.dma_start(out=wgu[:, wp, 1],
                                       in_=wuT_in[ee].rearrange("(c p) f -> p c f", p=128)[:, :, fc * 512:(fc + 1) * 512])
                p.on("gpsimd", ld_wu, incs=((WM, 16),))
                w_rdy = p.now(WM)
                for ft in range(4):
                    for sh in range(2):
                        gb = sh
                        def mm_g(e, wp=wp, ft=ft, sh=sh, gb=gb):
                            for c in range(8):
                                inst = e.matmul(psum[:, gb, 0:384], wgu[:, wp, 0, c, ft * 128:(ft + 1) * 128],
                                                xeT[:, c, sh * 384:(sh + 1) * 384],
                                                start=(c == 0), stop=(c == 7))
                            return inst
                        p.on("tensor", mm_g, waits=[w_rdy, xeT_done] + waitdep(("bank", gb)), incs=((MM, 1),))
                        def mm_u(e, wp=wp, ft=ft, sh=sh, gb=gb):
                            for c in range(8):
                                inst = e.matmul(psum[:, 2 + gb, 0:384], wgu[:, wp, 1, c, ft * 128:(ft + 1) * 128],
                                                xeT[:, c, sh * 384:(sh + 1) * 384],
                                                start=(c == 0), stop=(c == 7))
                            return inst
                        p.on("tensor", mm_u, waits=waitdep(("bank", 2 + gb)), incs=((MM, 1),))
                        def ev_sil(e, sh=sh, gb=gb):
                            return e.activation(out=gsil[:, sh, :], in_=psum[:, gb, 0:384], func=AF.Silu)
                        p.on("scalar", ev_sil, waits=[p.now(MM)] + waitdep(("gsil", sh)), incs=((EV, 1),))
                        setdep(("bank", gb), EV)
                        def ev_gu(e, fc=fc, ft=ft, sh=sh, gb=gb):
                            return e.tensor_tensor(gu[:, fc * 4 + ft, sh * 384:(sh + 1) * 384],
                                                   gsil[:, sh, :], psum[:, 2 + gb, 0:384], ALU.mult)
                        p.on("vector", ev_gu, waits=[p.now(EV)] + waitdep("gu_free"), incs=((EV, 1),))
                        setdep(("gsil", sh), EV)
                        setdep(("bank", 2 + gb), EV)
                setdep(("wgu", wp), MM)
            gu_done = p.now(EV)

            for sh in range(2):
                for c in range(32):
                    wp = c & 1
                    def ld_wo(e, ee=ee, c=c, wp=wp):
                        return e.dma_start(out=wob[:, wp], in_=woT_in[ee, c * 128:(c + 1) * 128, :])
                    p.on("gpsimd", ld_wo, waits=waitdep(("wob", wp)), incs=((WM, 16),))
                    wo_rdy = p.now(WM)
                    for st3 in range(3):
                        for dc in range(2):
                            bank = 2 + st3 * 2 + dc
                            def mm_eo(e, sh=sh, c=c, wp=wp, st3=st3, dc=dc, bank=bank):
                                return e.matmul(psum[:, bank, :],
                                                gu[:, c, sh * 384 + st3 * 128: sh * 384 + (st3 + 1) * 128],
                                                wob[:, wp, dc * 512:(dc + 1) * 512],
                                                start=(c == 0), stop=(c == 31))
                            p.on("tensor", mm_eo, waits=[wo_rdy, gu_done] +
                                 (waitdep(("bank", bank)) if c == 0 else []), incs=((MM, 1),))
                    setdep(("wob", wp), MM)
                for st3 in range(3):
                    def ev_eo(e, st3=st3):
                        e.tensor_copy(eos[:, 0, 0:512], psum[:, 2 + st3 * 2, :])
                        return e.tensor_copy(eos[:, 0, 512:1024], psum[:, 2 + st3 * 2 + 1, :])
                    p.on("vector", ev_eo, waits=[p.now(MM)] + waitdep("eos"), incs=((EV, 1),))
                    setdep(("bank", 2 + st3 * 2), EV)
                    setdep(("bank", 2 + st3 * 2 + 1), EV)
                    def st_eo(e, ee=ee, sh=sh, st3=st3):
                        base = ee * 768 + sh * 384 + st3 * 128
                        return e.dma_start(out=eo[base:base + 128, :], in_=eos[:, 0])
                    p.on("sync", st_eo, waits=[p.now(EV)], incs=((SC, 16),))
                    if "dbg_eo" in dbg:
                        def st_eod(e, ee=ee, sh=sh, st3=st3):
                            base = ee * 768 + sh * 384 + st3 * 128
                            return e.dma_start(out=dbg["dbg_eo"][base:base + 128, :], in_=eos[:, 0])
                        p.on("sync", st_eod, incs=((SC, 16),))
                    setdep("eos", SC)
            setdep("gu_free", MM)
        eo_done = p.now(SC)
        if "dbg_xd" in dbg:
            xdb = gu.ap().rearrange("p a b -> p (a b)")
            for r in range(NSLOT // 128):
                def ld_xdb(e, r=r):
                    return e.dma_start(out=xdb[:, 0:1024], in_=xd[r * 128:(r + 1) * 128, :])
                p.on("sync", ld_xdb, waits=[eo_done] + ([p.now(SC)] if r else []), incs=((GP, 16),))
                def st_xdb(e, r=r):
                    return e.dma_start(out=dbg["dbg_xd"][r * 128:(r + 1) * 128, :], in_=xdb[:, 0:1024])
                p.on("sync", st_xdb, waits=[p.now(GP)], incs=((SC, 16),))
            eo_done = p.now(SC)

        for li in range(NT):
            cpar = li & 1
            for k in range(2):
                def g_eo(e, li=li, k=k, cpar=cpar):
                    return e.indirect_dma_start(
                        out=cmb[:, cpar, k], out_offset=None, in_=eo,
                        in_offset=IndirectOffsetOnAxis(ap=slotcv[:, 2 + k, li:li + 1], axis=0))
                p.on("gpsimd", g_eo, waits=[eo_done] + (waitdep(("cmb", cpar)) if k == 0 else []),
                     incs=((GP, 16),))
            cmb_ld = p.now(GP)
            xp = x2preA if li < 8 else x2preB
            lj = li & 7
            def c_c1(e, li=li, cpar=cpar, xp=xp, lj=lj):
                return e.scalar_tensor_tensor(out=xp[:, lj], in0=cmb[:, cpar, 0],
                                              scalar=routc[:, 6, li:li + 1], in1=x1f[:, li],
                                              op0=ALU.mult, op1=ALU.add)
            p.on("vector", c_c1, waits=[cmb_ld], incs=((VS, 1),))
            def c_c2(e, li=li, cpar=cpar, xp=xp, lj=lj):
                return e.scalar_tensor_tensor(out=xp[:, lj], in0=cmb[:, cpar, 1],
                                              scalar=routc[:, 7, li:li + 1], in1=xp[:, lj],
                                              op0=ALU.mult, op1=ALU.add)
            p.on("vector", c_c2, waits=[p.now(VS)], incs=((VS, 1),))
            setdep(("cmb", cpar), VS)
        if "dbg_xp" in dbg:
            def st_xpa(e):
                return e.dma_start(out=dbg["dbg_xp"][0:1024, :].rearrange("(a p) b -> p a b", p=128), in_=x2preA[:])
            p.on("gpsimd", st_xpa, waits=[p.now(VS)], incs=((GP, 16),))
            def st_xpb(e):
                return e.dma_start(out=dbg["dbg_xp"][1024:2048, :].rearrange("(a p) b -> p a b", p=128), in_=x2preB[:])
            p.on("gpsimd", st_xpb, incs=((GP, 16),))
            p.on("vector", None, waits=[p.now(GP)])
        xin_fn = lambda li: (x2preA if li < 8 else x2preB)[:, li & 7]
        _emit_ln_block(p, NT, xin_fn, x2f, x2b, stats16, sq_scr, VS)
        x2_done = p.now(VS)


    if STOP >= 3:
        wipj = smap.at("wipj", [128, 2, 8, 128], BF16, 116 * KB_, 4.0, 4.183)
        x4f = smap.at("x4f", [128, NT, 1024], F32, 96 * KB_, 5.91, 6.0)
        wopj = smap.at("wopj", [128, 2, 16, 128], BF16, 108 * KB_, 4.86, 4.97)
        psum_bf = psum.ap().bitcast(BF16)

        p.on("sync", lambda e: e.dma_start(allow_slow_non_contiguous=True, out=convb_s[:], in_=convb_in.rearrange("(g p) o -> p (g o)", p=128)), incs=((LD, 16),))
        p.on("sync", lambda e: e.dma_start(allow_slow_non_contiguous=True, out=dtw_s[:], in_=dtw_in[0:1024].rearrange("(g p) o -> p (g o)", p=128)), incs=((LD, 16),))
        p.on("sync", lambda e: e.dma_start(allow_slow_non_contiguous=True, out=dtb_s[:], in_=dtb_in[0:1024].rearrange("(g p) o -> p (g o)", p=128)), incs=((LD, 16),))
        p.on("sync", lambda e: e.dma_start(out=convw_s[:], in_=convw_in.rearrange("(g p) c -> p g c", p=128)), incs=((LD, 16),))
        p.on("sync", lambda e: e.dma_start(out=Acol_s[:], in_=Alog_in[0:1024].rearrange("(g p) n -> p g n", p=128)), incs=((LD, 16),))
        p.on("sync", lambda e: e.dma_start(allow_slow_non_contiguous=True, out=Dp_s[:], in_=Dp_in[0:1024].rearrange("(g p) o -> p (g o)", p=128)), incs=((LD, 16),))
        p.on("gpsimd", lambda e: e.dma_start(out=edge_sel[:], in_=edge_sel_in), incs=((LD, 16),))
        p.on("sync", lambda e: e.dma_start(out=ag1u_idx[:], in_=ag1u_idx_in), incs=((LD, 16),))
        p.on("sync", lambda e: e.dma_start(out=bc_idx[:], in_=bc_idx_in), incs=((LD, 16),))
        p.on("sync", lambda e: e.dma_start(out=ag2_idx[:], in_=ag2_idx_in), incs=((LD, 16),))
        p.on("gpsimd", lambda e: e.dma_start(out=xpj_s[:], in_=xprojT_in.rearrange("(g p) n -> p g n", p=128)), incs=((LD, 16),))
        p.on("gpsimd", lambda e: e.memset(ones_row[:], 1.0), incs=((LD, 1),))
        lc = p.now(LD)
        def mkA(e):
            return e.activation(out=Acol_s.ap().rearrange("p g n -> p (g n)"),
                                in_=Acol_s.ap().rearrange("p g n -> p (g n)"), func=AF.Exp)
        p.on("scalar", mkA, waits=[lc], incs=((VS, 1),))
        def mkA2(e):
            return e.tensor_scalar(out=Acol_s.ap().rearrange("p g n -> p (g n)"),
                                   in0=Acol_s.ap().rearrange("p g n -> p (g n)"),
                                   scalar1=-1.0, scalar2=None, op0=ALU.mult, op1=ALU.bypass)
        p.on("vector", mkA2, waits=[p.now(VS)], incs=((VS, 1),))
        p.on("vector", lambda e: e.memset(hlast[:], 0.0), incs=((VS, 1),))
        A_ready = p.now(VS)

        def ed_st(e):
            return e.dma_start(out=edg_in, in_=x2b[125:128, 15, :])
        p.on("gpsimd", ed_st, waits=[x2_done], incs=((STO, 16),))
        def ed_ag(e):
            return e.collective_compute("AllGather", ALU.bypass, ins=[edg_in.opt()],
                                        outs=[edg_out.opt()], replica_groups=[list(range(8))])
        p.on("gpsimd", ed_ag, waits=[p.now(STO)], incs=((AGS, 1),))
        def ed_ld(e):
            return e.dma_start(out=x2T[0:24, 0, 0:1024], in_=edg_out)
        p.on("sync", ed_ld, waits=[p.now(AGS)], incs=((VS, 16),))
        def ed_mm(e):
            e.matmul(psum[0:4, 4, :], edge_sel[:], x2T[0:24, 0, 0:512], start=True, stop=True)
            return e.matmul(psum[0:4, 5, :], edge_sel[:], x2T[0:24, 0, 512:1024], start=True, stop=True)
        p.on("tensor", ed_mm, waits=[p.now(VS)] + waitdep(("bank", 4)) + waitdep(("bank", 5)), incs=((MM, 1),))
        def ed_ev(e):
            e.tensor_copy(sq_scr[0:4, 0:512], psum[0:4, 4, :])
            return e.tensor_copy(sq_scr[0:4, 512:1024], psum[0:4, 5, :])
        p.on("vector", ed_ev, waits=[p.now(MM)], incs=((VS, 1),))
        setdep(("bank", 4), VS)
        setdep(("bank", 5), VS)
        for c in range(8):
            bank = 6 + (c & 1)
            def ed_tr(e, c=c, bank=bank):
                return e.transpose(psum[0:128, bank, 0:4], sq_scr[0:4, c * 128:(c + 1) * 128], identf[0:4, 0:4])
            p.on("tensor", ed_tr, waits=[p.now(VS)] + waitdep(("bank", bank)), incs=((MM, 1),))
            def ed_te(e, c=c, bank=bank):
                return e.tensor_copy(edgeT[:, c, :], psum[:, bank, 0:4])
            p.on("vector", ed_te, waits=[p.now(MM)], incs=((VS, 1),))
            setdep(("bank", bank), VS)
        edge_ok = p.now(VS)

        for hp in range(2):
            for j in range(8):
                li = hp * 8 + j
                for c in range(8):
                    bank = 6 + (c & 1)
                    def tr_x2(e, li=li, c=c, bank=bank):
                        return e.transpose(psum_bf[:, bank, 0:128], x2b[:, li, c * 128:(c + 1) * 128], ident[:])
                    p.on("tensor", tr_x2, waits=[x2_done, edge_ok] + waitdep(("bank", bank)), incs=((MM, 1),))
                    def ev_tx2(e, j=j, c=c, bank=bank):
                        return e.tensor_copy(x2T[:, c, j * 128:(j + 1) * 128], psum_bf[:, bank, 0:128])
                    p.on("vector", ev_tx2, waits=[p.now(MM)], incs=((EV, 1),))
                    setdep(("bank", bank), EV)
            x2T_ok = p.now(EV)
            for m in range(32):
                wp = m & 1
                def ld_wip(e, m=m, wp=wp):
                    return e.dma_start(out=wipj[:, wp],
                                       in_=inprojT_in.rearrange("(c p) f -> p c f", p=128)[:, :, m * 128:(m + 1) * 128])
                p.on("gpsimd", ld_wip, waits=waitdep(("wipj", wp)), incs=((WM, 16),))
                w_ok = p.now(WM)
                for h2 in range(2):
                    bank = (m & 1) * 2 + h2
                    def mm_ip(e, m=m, wp=wp, h2=h2, bank=bank):
                        for c in range(8):
                            inst = e.matmul(psum[:, bank, :], wipj[:, wp, c],
                                            x2T[:, c, h2 * 512:(h2 + 1) * 512],
                                            start=(c == 0), stop=(c == 7))
                        return inst
                    p.on("tensor", mm_ip, waits=[w_ok, x2T_ok] + waitdep(("bank", bank)), incs=((MM, 1),))
                if m < 16:
                    def ev_xin(e, m=m, hp=hp):
                        e.tensor_copy(xinT[:, m, hp * 1024:hp * 1024 + 512], psum[:, (m & 1) * 2, :])
                        return e.tensor_copy(xinT[:, m, hp * 1024 + 512:hp * 1024 + 1024], psum[:, (m & 1) * 2 + 1, :])
                    p.on("vector", ev_xin, waits=[p.now(MM)], incs=((EV, 1),))
                else:
                    zp = (m & 1) * 1024
                    def ev_z(e, m=m, zp=zp):
                        e.tensor_copy(zst[:, zp:zp + 512], psum[:, (m & 1) * 2, :])
                        return e.tensor_copy(zst[:, zp + 512:zp + 1024], psum[:, (m & 1) * 2 + 1, :])
                    p.on("vector", ev_z, waits=[p.now(MM)] + waitdep(("zst", m & 1)), incs=((EV, 1),))
                    def st_z(e, m=m, hp=hp, zp=zp):
                        return e.dma_start(out=zT_d[(m - 16) * 128:(m - 15) * 128, hp * 1024:(hp + 1) * 1024],
                                           in_=zst[:, zp:zp + 1024])
                    p.on("sync", st_z, waits=[p.now(EV)], incs=((SC, 16),))
                    setdep(("zst", m & 1), SC)
                setdep(("bank", (m & 1) * 2), EV)
                setdep(("bank", (m & 1) * 2 + 1), EV)
                if hp == 0 and m < 16:
                    def mm_ed(e, m=m, wp=wp):
                        for c in range(8):
                            inst = e.matmul(psum[:, 4, 0:4], wipj[:, wp, c], edgeT[:, c, :],
                                            start=(c == 0), stop=(c == 7))
                        return inst
                    p.on("tensor", mm_ed, waits=[edge_ok] + waitdep(("bank", 4)), incs=((MM, 1),))
                    def ev_ed(e, m=m):
                        return e.tensor_copy(e3[:, m, :], psum[:, 4, 0:4])
                    p.on("vector", ev_ed, waits=[p.now(MM)], incs=((EV, 1),))
                    setdep(("bank", 4), EV)
                setdep(("wipj", wp), MM)
        inproj_ok = p.now(EV)

        for g in range(16):
            def cv0(e, g=g):
                return e.tensor_scalar(out=ctmp[:], in0=xinT[:, g, :], scalar1=convw_s[:, g, 3:4],
                                       scalar2=None, op0=ALU.mult, op1=ALU.bypass)
            p.on("vector", cv0, waits=[inproj_ok] if g == 0 else [p.now(VS)], incs=((VS, 1),))
            for j in range(3):
                sft = 3 - j
                def cvj(e, g=g, j=j, sft=sft):
                    return e.scalar_tensor_tensor(out=ctmp[:, sft:2048], in0=xinT[:, g, 0:2048 - sft],
                                                  scalar=convw_s[:, g, j:j + 1], in1=ctmp[:, sft:2048],
                                                  op0=ALU.mult, op1=ALU.add)
                p.on("vector", cvj, waits=[p.now(VS)], incs=((VS, 1),))
                def cvje(e, g=g, j=j, sft=sft):
                    return e.scalar_tensor_tensor(out=ctmp[:, 0:sft], in0=e3[:, g, j:3],
                                                  scalar=convw_s[:, g, j:j + 1], in1=ctmp[:, 0:sft],
                                                  op0=ALU.mult, op1=ALU.add)
                p.on("vector", cvje, waits=[p.now(VS)], incs=((VS, 1),))
            def cvs(e, g=g):
                return e.activation(out=xinT[:, g, :], in_=ctmp[:], func=AF.Silu,
                                    bias=convb_s[:, g:g + 1], scale=1.0)
            p.on("scalar", cvs, waits=[p.now(VS)], incs=((VS, 1),))
        conv_ok = p.now(VS)

        XPS = p.sem("xps")
        for q4 in range(4):
            bank = q4 & 1
            col = (q4 & 1) * 512
            def mm_xp(e, q4=q4, bank=bank):
                for g in range(16):
                    inst = e.matmul(psum[0:33, bank, :], xpj_s[:, g, :],
                                    xinT[:, g, q4 * 512:(q4 + 1) * 512],
                                    start=(g == 0), stop=(g == 15))
                return inst
            p.on("tensor", mm_xp, waits=[conv_ok, lc] + waitdep(("bank", bank)), incs=((MM, 1),))
            def ev_xp(e, q4=q4, bank=bank, col=col):
                return e.tensor_copy(sq_scr[0:33, col:col + 512], psum[0:33, bank, :])
            p.on("vector", ev_xp, waits=[p.now(MM)] + waitdep(("sqs", col)), incs=((EV, 1),))
            setdep(("bank", bank), EV)
            def st_xp(e, q4=q4, col=col):
                return e.dma_start(out=ag1s_in[:, q4 * 512:(q4 + 1) * 512], in_=sq_scr[0:33, col:col + 512])
            p.on("gpsimd", st_xp, waits=[p.now(EV)], incs=((XPS, 16),))
            setdep(("sqs", col), XPS)
        for g in range(16):
            def st_u(e, g=g):
                return e.dma_start(out=ag1u_in[g * 128:(g + 1) * 128, :], in_=xinT[:, g, :])
            p.on("sync", st_u, waits=[conv_ok] if g == 0 else [], incs=((SC, 16),))
        ag1_ready = p.now(SC)
        def ag1u(e):
            return e.collective_compute("AllGather", ALU.bypass, ins=[ag1u_in.opt()],
                                        outs=[ag1u_out.opt()], replica_groups=[list(range(8))])
        p.on("gpsimd", ag1u, waits=[ag1_ready, p.now(XPS)], incs=((AGS, 1),))
        def ag1s(e):
            return e.collective_compute("AllGather", ALU.bypass, ins=[ag1s_in.opt()],
                                        outs=[ag1s_out.opt()], replica_groups=[list(range(8))])
        p.on("gpsimd", ag1s, waits=[p.now(AGS)], incs=((AGS, 1),))
        ag1_done = p.now(AGS)
        for g in range(8):
            for th in range(2):
                def g_u(e, g=g, th=th):
                    return e.indirect_dma_start(
                        out=uT_sc[:, g, th * 2048:(th + 1) * 2048], out_offset=None,
                        in_=ag1u_out,
                        in_offset=IndirectOffsetOnAxis(ap=ag1u_idx[:, g, th:th + 1], axis=0))
                p.on("gpsimd", g_u, waits=[ag1_done, lc] if (g == 0 and th == 0) else [], incs=((GP, 16),))
        scan_in_ok = p.now(GP)
        if "dbg_u" in dbg:
            def d_u(e):
                return e.dma_start(out=dbg["dbg_u"], in_=uT_sc.ap().rearrange("p a b -> p (a b)"))
            p.on("sync", d_u, waits=[scan_in_ok], incs=((STO, 16),))
            def d_ssm(e):
                return e.dma_start(out=dbg["dbg_ssm"], in_=ag1s_in)
            p.on("sync", d_ssm, incs=((STO, 16),))

        HCS = p.sem("hcs")
        def _dump(key, src_ap):
            def dfn(e, src_ap=src_ap, key=key):
                return e.dma_start(out=dbg[key], in_=src_ap)
            p.on("sync", dfn, waits=[], incs=((STO, 16),))
            tgt = p.now(STO)
            p.on("vector", None, waits=[tgt])
            p.on("scalar", None, waits=[tgt])
            p.on("gpsimd", None, waits=[tgt])
        for c8 in range(8):
            eoff = c8 * 512
            def g_dt(e, c8=c8, eoff=eoff):
                return e.indirect_dma_start(
                    out=dtbc[:, :], out_offset=None, in_=ag1s_out,
                    in_offset=IndirectOffsetOnAxis(ap=bc_idx[:, c8 // 4, 0:1], axis=0),
                    element_offset=(c8 % 4) * 512)
            p.on("gpsimd", g_dt, waits=[scan_in_ok, A_ready] + waitdep("dtbc"), incs=((GP, 16),))
            dt_ok = p.now(GP)
            if c8 == 0 and "dbg_dt" in dbg:
                p.on("sync", None, waits=[dt_ok])
                _dump("dbg_dt", dtbc[:, :])
            for g in range(8):
                def dl1(e, g=g):
                    return e.tensor_scalar(out=dab[:, 0, :], in0=dtbc[:, :], scalar1=dtw_s[:, g:g + 1],
                                           scalar2=dtb_s[:, g:g + 1], op0=ALU.mult, op1=ALU.add)
                p.on("vector", dl1, waits=[dt_ok] + waitdep(("dab", 0)), incs=((VS, 1),))
                def dl2(e):
                    return e.tensor_scalar(out=dab[:, 1, :], in0=dab[:, 0, :], scalar1=-1.0,
                                           scalar2=None, op0=ALU.mult, op1=ALU.bypass)
                p.on("vector", dl2, waits=[p.now(VS)] + waitdep(("dab", 1)), incs=((VS, 1),))
                def dl3(e):
                    return e.tensor_tensor(dab[:, 1, :], dab[:, 1, :], dab[:, 0, :], ALU.min)
                p.on("vector", dl3, waits=[p.now(VS)], incs=((VS, 1),))
                def dl4(e):
                    return e.activation(out=dab[:, 1, :], in_=dab[:, 1, :], func=AF.Exp)
                p.on("scalar", dl4, waits=[p.now(VS)], incs=((VS, 1),))
                def dl5(e):
                    return e.activation(out=dab[:, 1, :], in_=dab[:, 1, :], func=AF.Ln, bias=1.0)
                p.on("scalar", dl5, waits=[p.now(VS)], incs=((VS, 1),))
                def dl6(e):
                    return e.tensor_scalar(out=dab[:, 0, :], in0=dab[:, 0, :], scalar1=0.0,
                                           scalar2=None, op0=ALU.max, op1=ALU.bypass)
                p.on("vector", dl6, waits=[p.now(VS)], incs=((VS, 1),))
                def dl7(e, g=g):
                    return e.tensor_tensor(dlt8[:, g, :], dab[:, 0, :], dab[:, 1, :], ALU.add)
                p.on("vector", dl7, waits=[p.now(VS)], incs=((VS, 1),))
                setdep(("dab", 0), VS)
                setdep(("dab", 1), VS)
                def du1(e, g=g, eoff=eoff):
                    return e.tensor_tensor(du8[:, g, :], dlt8[:, g, :], uT_sc[:, g, eoff:eoff + 512], ALU.mult)
                p.on("vector", du1, waits=[p.now(VS)], incs=((VS, 1),))
            dl_ok = p.now(VS)
            if c8 == 0 and "dbg_dl" in dbg:
                p.on("sync", None, waits=[dl_ok])
                _dump("dbg_dl", dlt8.ap().rearrange("p a b -> p (a b)"))
                _dump("dbg_du", du8.ap().rearrange("p a b -> p (a b)"))
            for n in range(NST):
                np_ = n & 1
                def g_b(e, c8=c8, n=n, np_=np_):
                    return e.indirect_dma_start(
                        out=bcb[:, np_, :], out_offset=None, in_=ag1s_out,
                        in_offset=IndirectOffsetOnAxis(ap=bc_idx[:, c8 // 4, 1 + n:2 + n], axis=0),
                        element_offset=(c8 % 4) * 512)
                p.on("gpsimd", g_b, waits=[dl_ok] + waitdep(("bcbn", np_)), incs=((GP, 16),))
                def g_c(e, c8=c8, n=n, np_=np_):
                    return e.indirect_dma_start(
                        out=ccb[:, np_, :], out_offset=None, in_=ag1s_out,
                        in_offset=IndirectOffsetOnAxis(ap=bc_idx[:, c8 // 4, 17 + n:18 + n], axis=0),
                        element_offset=(c8 % 4) * 512)
                p.on("gpsimd", g_c, waits=waitdep(("ccbn", np_)), incs=((GP, 16),))
                bc_ok = p.now(GP)
                if c8 == 0 and n == 0 and "dbg_bc" in dbg:
                    p.on("sync", None, waits=[bc_ok])
                    _dump("dbg_bc0", bcb[:, 0, :])
                    _dump("dbg_bc1", ccb[:, 0, :])
                for g in range(8):
                    gn = g * 16 + n
                    def mk_da(e, g=g, n=n):
                        return e.activation(out=dab[:, 0, :], in_=dlt8[:, g, :], func=AF.Exp,
                                            scale=Acol_s[:, g, n:n + 1])
                    p.on("scalar", mk_da, waits=[dl_ok] + waitdep(("dab", 0)), incs=((VS, 1),))
                    def mk_dub(e, g=g, np_=np_):
                        return e.tensor_tensor(dub[:, 0, 0:512], du8[:, g, :], bcb[:, np_, :], ALU.mult)
                    p.on("gpsimd", mk_dub, waits=[bc_ok] + waitdep(("dub", 0)), incs=((GP, 1),))
                    def mk_scan(e, g=g, gn=gn):
                        return e.tensor_tensor_scan(out=hball[:, g, :], data0=dab[:, 0, :],
                                                    data1=dub[:, 0, :], initial=hlast[:, gn:gn + 1],
                                                    op0=ALU.mult, op1=ALU.add)
                    if c8 == 0 and n == 0 and g == 0 and "dbg_da" in dbg:
                        p.on("sync", None, waits=[p.now(VS)])
                        _dump("dbg_da", dab[:, 0, :])
                    p.on("vector", mk_scan, waits=[p.now(VS), p.now(GP)] + waitdep(("hb", g)), incs=((VS, 1),))
                    setdep(("dab", 0), VS)
                    setdep(("dub", 0), VS)
                    if c8 == 0 and n == 0 and g == 0 and "dbg_hb" in dbg:
                        p.on("sync", None, waits=[p.now(VS)])
                        _dump("dbg_hb", hball[:, 0, :])
                    def up_hl(e, g=g, gn=gn):
                        return e.tensor_copy(hlast[:, gn:gn + 1], hball[:, g, 511:512])
                    p.on("vector", up_hl, waits=[p.now(VS)], incs=((VS, 1),))
                    def mk_hc(e, g=g, np_=np_):
                        return e.tensor_tensor(hcb[:, 0, 0:512], hball[:, g, :], ccb[:, np_, :], ALU.mult)
                    p.on("vector", mk_hc, waits=[p.now(VS)] + waitdep("hcf"), incs=((HCS, 1),))
                    setdep(("hb", g), HCS)
                    hc_ok = p.now(HCS)
                    def mm_y(e, g=g, n=n):
                        return e.matmul(psum[:, g, :], ident[:], hcb[:, 0, :],
                                        start=(n == 0), stop=(n == NST - 1))
                    p.on("tensor", mm_y, waits=[hc_ok] + (waitdep(("bank", g)) if n == 0 else []),
                         incs=((MM, 1),))
                    setdep("hcf", MM)
                setdep(("bcbn", np_), MM)
                setdep(("ccbn", np_), MM)
            y_mm_ok = p.now(MM)
            if c8 == 0 and "dbg_hf" in dbg:
                p.on("sync", None, waits=[p.now(VS)])
                _dump("dbg_hf", hball.ap().rearrange("p a b -> p (a b)"))
            for g in range(8):
                def ev_y(e, g=g, eoff=eoff):
                    return e.scalar_tensor_tensor(out=ytst[:, 0, :], in0=uT_sc[:, g, eoff:eoff + 512],
                                                  scalar=Dp_s[:, g:g + 1], in1=psum[:, g, :],
                                                  op0=ALU.mult, op1=ALU.add)
                p.on("vector", ev_y, waits=[y_mm_ok] + waitdep(("yt", 0)), incs=((VS, 1),))
                setdep(("bank", g), VS)
                if c8 == 0 and g == 0 and "dbg_yt" in dbg:
                    p.on("sync", None, waits=[p.now(VS)])
                    _dump("dbg_yt", ytst[:, 0, :])
                def st_y(e, g=g, c8=c8):
                    return e.dma_start(out=ag2_in[(g * 8 + c8) * 128:(g * 8 + c8 + 1) * 128, :],
                                       in_=ytst[:, 0, :])
                p.on("sync", st_y, waits=[p.now(VS)], incs=((SC, 16),))
                setdep(("yt", 0), SC)
        scan_done = p.now(SC)

        if "dbg_y" in dbg:
            def d_y(e):
                return e.dma_start(out=dbg["dbg_y"], in_=ag2_in)
            p.on("sync", d_y, waits=[scan_done], incs=((STO, 16),))
            def d_z(e):
                return e.dma_start(out=dbg["dbg_z"], in_=zT_d)
            p.on("sync", d_z, incs=((STO, 16),))
        def ag2(e):
            return e.collective_compute("AllGather", ALU.bypass, ins=[ag2_in.opt()],
                                        outs=[ag2_out.opt()], replica_groups=[list(range(8))])
        p.on("gpsimd", ag2, waits=[scan_done], incs=((AGS, 1),))
        ag2_done = p.now(AGS)
        for gg in range(16):
            for oc in range(4):
                def g_y(e, gg=gg, oc=oc):
                    return e.indirect_dma_start(
                        out=ygT[:, gg, oc * 512:(oc + 1) * 512], out_offset=None,
                        in_=ag2_out,
                        in_offset=IndirectOffsetOnAxis(ap=ag2_idx[:, gg, oc:oc + 1], axis=0))
                p.on("gpsimd", g_y, waits=[ag2_done] if (gg == 0 and oc == 0) else [], incs=((GP, 16),))
        yg_ok = p.now(GP)
        for gg in range(16):
            zp2 = gg & 1
            def ld_z(e, gg=gg, zp2=zp2):
                return e.dma_start(out=zrl[:, zp2, :], in_=zT_d[gg * 128:(gg + 1) * 128, :])
            p.on("sync", ld_z, waits=([yg_ok] if gg == 0 else []) + waitdep(("zrl", zp2)), incs=((SC, 16),))
            def sil_z(e, gg=gg, zp2=zp2):
                return e.activation(out=zrl[:, zp2, :], in_=zrl[:, zp2, :], func=AF.Silu)
            p.on("scalar", sil_z, waits=[p.now(SC)], incs=((VS, 1),))
            def gt_y(e, gg=gg, zp2=zp2):
                return e.tensor_tensor(ygT[:, gg, :], ygT[:, gg, :], zrl[:, zp2, :], ALU.mult)
            p.on("vector", gt_y, waits=[p.now(VS), yg_ok], incs=((VS, 1),))
            setdep(("zrl", zp2), VS)
        yg_gated = p.now(VS)
        if "dbg_yg" in dbg:
            def d_yg(e):
                return e.dma_start(out=dbg["dbg_yg"], in_=ygT.ap().rearrange("p a b -> p (a b)"))
            p.on("sync", d_yg, waits=[yg_gated], incs=((STO, 16),))
            p.on("vector", None, waits=[p.now(STO)])
        for m in range(8):
            wp = m & 1
            def ld_wop(e, m=m, wp=wp):
                return e.dma_start(out=wopj[:, wp],
                                   in_=outprojT_in.rearrange("(c p) f -> p c f", p=128)[:, :, m * 128:(m + 1) * 128])
            p.on("gpsimd", ld_wop, waits=waitdep(("wopj", wp)), incs=((WM, 16),))
            wop_ok = p.now(WM)
            for q4 in range(4):
                bank = q4 & 3
                def mm_op(e, m=m, wp=wp, q4=q4, bank=bank):
                    for gg in range(16):
                        inst = e.matmul(psum[:, bank, :], wopj[:, wp, gg],
                                        ygT[:, gg, q4 * 512:(q4 + 1) * 512],
                                        start=(gg == 0), stop=(gg == 15))
                    return inst
                p.on("tensor", mm_op, waits=[wop_ok, yg_gated] + waitdep(("bank", bank)), incs=((MM, 1),))
                def ev_op(e, m=m, q4=q4, bank=bank):
                    return e.tensor_copy(aoT[:, m, q4 * 512:(q4 + 1) * 512], psum[:, bank, :])
                p.on("vector", ev_op, waits=[p.now(MM)], incs=((EV, 1),))
                setdep(("bank", bank), EV)
            setdep(("wopj", wp), MM)
        op_done = p.now(EV)
        for li in range(NT):
            for c in range(8):
                bank = 6 + (c & 1)
                def tr_ao(e, li=li, c=c, bank=bank):
                    return e.transpose(psum_bf[:, bank, 0:128], aoT[:, c, li * 128:(li + 1) * 128], ident[:])
                p.on("tensor", tr_ao, waits=[op_done] + waitdep(("bank", bank)), incs=((MM, 1),))
                def rs_ao(e, li=li, c=c, bank=bank):
                    return e.tensor_tensor(x2f[:, li, c * 128:(c + 1) * 128],
                                           x2f[:, li, c * 128:(c + 1) * 128],
                                           psum_bf[:, bank, 0:128], ALU.add)
                p.on("vector", rs_ao, waits=[p.now(MM)], incs=((VS, 1),))
                setdep(("bank", bank), VS)
        _emit_ln_block(p, NT, x2f, x3f, x3b, stats16, sq_scr, VS)
        x3_done = p.now(VS)
        if "dbg_x3" in dbg:
            def d_x3(e):
                return e.dma_start(out=dbg["dbg_x3"].rearrange("(a p) b -> p a b", p=128), in_=x3f[:])
            p.on("sync", d_x3, waits=[x3_done], incs=((STO, 16),))
            def d_x3p(e):
                return e.dma_start(out=dbg["dbg_x3p"].rearrange("(a p) b -> p a b", p=128), in_=x2f[:])
            p.on("sync", d_x3p, incs=((STO, 16),))
            p.on("vector", None, waits=[p.now(STO)])

        p.on("gpsimd", lambda e: e.dma_start(out=win_s[:], in_=Win_in.rearrange("(c p) m -> p c m", p=128)), incs=((LD, 16),))
        p.on("gpsimd", lambda e: e.dma_start(out=wout_s[:], in_=Wout_in), waits=[x3_done], incs=((LD, 16),))
        p.on("sync", lambda e: e.dma_start(out=binc_s[:], in_=binc_in), incs=((LD, 16),))
        p.on("sync", lambda e: e.dma_start(out=w1_s[:], in_=W1_in), incs=((LD, 16),))
        p.on("sync", lambda e: e.dma_start(out=b1_s[:], in_=b1_in), incs=((LD, 16),))
        p.on("sync", lambda e: e.dma_start(out=w2_s[:], in_=W2_in), incs=((LD, 16),))
        p.on("sync", lambda e: e.dma_start(out=b2_s[:], in_=b2_in), incs=((LD, 16),))
        p.on("sync", lambda e: e.dma_start(allow_slow_non_contiguous=True, out=boutT_s[:],
                                           in_=bout_in.rearrange("o (g p) -> p (o g)", p=128)), incs=((LD, 16),))
        ffn_c = p.now(LD)
        for li in range(NT):
            hp = li // 8
            j = li % 8
            for c in range(8):
                bank = 6 + (c & 1)
                def tr_x3(e, li=li, c=c, bank=bank):
                    return e.transpose(psum_bf[:, bank, 0:128], x3b[:, li, c * 128:(c + 1) * 128], ident[:])
                p.on("tensor", tr_x3, waits=[x3_done] + waitdep(("bank", bank)), incs=((MM, 1),))
                def ev_tx3(e, li=li, c=c, bank=bank):
                    return e.tensor_copy(x3T[:, c, li * 128:(li + 1) * 128], psum_bf[:, bank, 0:128])
                p.on("vector", ev_tx3, waits=[p.now(MM)], incs=((EV, 1),))
                setdep(("bank", bank), EV)
        x3T_ok = p.now(EV)
        for q4 in range(4):
            bank = q4 & 1
            def mm_s1(e, q4=q4, bank=bank):
                for c in range(8):
                    inst = e.matmul(psum[0:64, bank, :], win_s[:, c], x3T[:, c, q4 * 512:(q4 + 1) * 512],
                                    start=(c == 0), stop=(c == 7))
                return inst
            p.on("tensor", mm_s1, waits=[x3T_ok, ffn_c] + waitdep(("bank", bank)), incs=((MM, 1),))
            def ev_s1(e, q4=q4, bank=bank):
                return e.tensor_scalar(out=sfn[:, 0, q4 * 512:(q4 + 1) * 512], in0=psum[0:64, bank, :],
                                       scalar1=1.0, scalar2=binc_s[:], op0=ALU.mult, op1=ALU.add)
            p.on("vector", ev_s1, waits=[p.now(MM)], incs=((VS, 1),))
            setdep(("bank", bank), VS)
        s_ok = p.now(VS)
        for i in range(INNER):
            def mk_h(e, i=i):
                return e.activation(out=sfn[:, 2, :], in_=sfn[:, 0, :], func=AF.Relu,
                                    scale=w1_s[:, i:i + 1], bias=b1_s[:, i:i + 1])
            p.on("scalar", mk_h, waits=[s_ok if i == 0 else p.now(VS)], incs=((VS, 1),))
            if i == 0:
                def mk_g(e, i=i):
                    return e.tensor_scalar(out=sfn[:, 1, :], in0=sfn[:, 2, :], scalar1=w2_s[:, i:i + 1],
                                           scalar2=b2_s[:], op0=ALU.mult, op1=ALU.add)
            else:
                def mk_g(e, i=i):
                    return e.scalar_tensor_tensor(out=sfn[:, 1, :], in0=sfn[:, 2, :],
                                                  scalar=w2_s[:, i:i + 1], in1=sfn[:, 1, :],
                                                  op0=ALU.mult, op1=ALU.add)
            p.on("vector", mk_g, waits=[p.now(VS)], incs=((VS, 1),))
        g_ok = p.now(VS)
        def g_cast(e):
            return e.tensor_copy(sfn[:, 3, :].bitcast(BF16)[:, 0:2048], sfn[:, 1, :])
        p.on("vector", g_cast, waits=[g_ok], incs=((VS, 1),))
        g2_ok = p.now(VS)
        for m in range(8):
            for q4 in range(4):
                bank = q4 & 1
                def mm_o(e, m=m, q4=q4, bank=bank):
                    return e.matmul(psum[:, bank, :], wout_s[0:64, m * 128:(m + 1) * 128],
                                    sfn[:, 3, :].bitcast(BF16)[0:64, q4 * 512:(q4 + 1) * 512],
                                    start=True, stop=True)
                p.on("tensor", mm_o, waits=[g2_ok, ffn_c] + waitdep(("bank", bank)), incs=((MM, 1),))
                def ev_o(e, m=m, q4=q4, bank=bank):
                    return e.tensor_scalar(out=foT[:, m, q4 * 512:(q4 + 1) * 512], in0=psum[:, bank, :],
                                           scalar1=1.0, scalar2=boutT_s[:, m:m + 1], op0=ALU.mult, op1=ALU.add)
                p.on("vector", ev_o, waits=[p.now(MM)], incs=((VS, 1),))
                setdep(("bank", bank), VS)
        fo_ok = p.now(VS)
        for li in range(NT):
            for c in range(8):
                bank = 6 + (c & 1)
                def tr_fo(e, li=li, c=c, bank=bank):
                    return e.transpose(psum_bf[:, bank, 0:128], foT[:, c, li * 128:(li + 1) * 128], ident[:])
                p.on("tensor", tr_fo, waits=[fo_ok] + waitdep(("bank", bank)), incs=((MM, 1),))
                def rs_fo(e, li=li, c=c, bank=bank):
                    return e.tensor_tensor(x3f[:, li, c * 128:(c + 1) * 128],
                                           x3f[:, li, c * 128:(c + 1) * 128],
                                           psum_bf[:, bank, 0:128], ALU.add)
                p.on("vector", rs_fo, waits=[p.now(MM)], incs=((VS, 1),))
                setdep(("bank", bank), VS)
        _emit_ln_block(p, NT, x3f, x4f, None, stats16, sq_scr, VS)
        x4_done = p.now(VS)


    if STOP == 1:
        out_src, out_done = x1f, x1_done
    elif STOP == 2:
        out_src, out_done = x2f, x2_done
    elif STOP == 3:
        out_src, out_done = x3f, x3_done
    else:
        out_src, out_done = x4f, x4_done
    def st_out(e):
        return e.dma_start(out=out_final.rearrange("(a p) b -> p a b", p=128), in_=out_src[:])
    p.on("sync", st_out, waits=[out_done], incs=((STO, 16),))
    if "dbg_x1" in dbg:
        def st_dbg(e):
            return e.dma_start(out=dbg["dbg_x1"].rearrange("(a p) b -> p a b", p=128), in_=x1f[:])
        p.on("sync", st_dbg, waits=[x1_done], incs=((STO, 16),))
        def st_acc(e):
            return e.dma_start(out=dbg["dbg_acc"].rearrange("(a p) b -> p a b", p=128), in_=acc[:])
        p.on("gpsimd", st_acc, waits=[attn_done], incs=((STO, 16),))
        def st_hoz(e):
            return e.dma_start(out=dbg["dbg_hoz"].rearrange("(a p) b -> p a b", p=128), in_=hoz[:])
        p.on("gpsimd", st_hoz, waits=[hoz_ld], incs=((STO, 16),))
        def st_xres(e):
            return e.dma_start(out=dbg["dbg_xres"].rearrange("(a p) b -> p a b", p=128), in_=xres[:])
        p.on("gpsimd", st_xres, waits=[x1_done], incs=((STO, 16),))
        def st_lnst(e):
            return e.dma_start(out=dbg["dbg_st"], in_=lnstdbg[:])
        p.on("gpsimd", st_lnst, waits=[p.now(VS)], incs=((STO, 16),))
        def st_kh_disabled(e):
            return e.dma_start(out=dbg["dbg_kh"], in_=khT.ap().rearrange("p a b -> p (a b)"))
        def st_qh_disabled(e):
            return e.dma_start(out=dbg["dbg_qh"], in_=qhT.ap().rearrange("p a b -> p (a b)"))
        def st_vh_disabled(e):
            return e.dma_start(out=dbg["dbg_vh"], in_=vh[:])
    p.on("sync", None, waits=[p.now(STO)])

    assert nc.sbuf_base <= SbufMap.BASE, ("bump allocator ran into manual map", nc.sbuf_base)
    p.emit(ctx)
    ctx.close()
    return nc


def _emit_ln_block(p, ntiles, xin, xf, xb, stats16, sq_scr, VS, lndbg=None, post=None):
    if not callable(xin):
        _t = xin
        xin = lambda li: _t[:, li]
    if xf is not None and not callable(xf):
        _f = xf
        xf = lambda li: _f[:, li]
    """LayerNorm ntiles tiles: xin (fp32) -> xf (fp32) + xb (bf16).
    stats16: sbuf (128, 8, ntiles) fp32 scratch (rows: sum, sumsq, s2, diff,
    vareps, rcp, rs, nmrs). Per-tile reduces feed one batched stat chain; every
    chained DVE op is separated by a producer-inc/consumer-wait (tiny-op RAW)."""
    for li in range(ntiles):
        def lnsum(e, li=li):
            return e.reduce_sum(out=stats16[:, 0, li:li + 1], in_=xin(li), axis=AX.X)
        p.on("vector", lnsum, incs=((VS, 1),))
        def lnsq(e, li=li):
            return e.activation(out=sq_scr[:], in_=xin(li), func=AF.Square)
        p.on("scalar", lnsq, waits=[p.now(VS)], incs=((VS, 1),))
        def lnred(e, li=li):
            return e.reduce_sum(out=stats16[:, 1, li:li + 1], in_=sq_scr[:], axis=AX.X)
        p.on("vector", lnred, waits=[p.now(VS)], incs=((VS, 1),))
    n = ntiles
    def c_s2(e):
        return e.tensor_tensor(stats16[:, 2, 0:n], stats16[:, 0, 0:n],
                               stats16[:, 0, 0:n], ALU.mult)
    p.on("vector", c_s2, waits=[p.now(VS)], incs=((VS, 1),))
    def c_diff(e):
        return e.scalar_tensor_tensor(out=stats16[:, 3, 0:n], in0=stats16[:, 2, 0:n],
                                      scalar=-1.0 / D, in1=stats16[:, 1, 0:n],
                                      op0=ALU.mult, op1=ALU.add)
    p.on("vector", c_diff, waits=[p.now(VS)], incs=((VS, 1),))
    def c_veps(e):
        return e.tensor_scalar(out=stats16[:, 4, 0:n], in0=stats16[:, 3, 0:n],
                               scalar1=1.0 / D, scalar2=1e-5, op0=ALU.mult, op1=ALU.add)
    p.on("vector", c_veps, waits=[p.now(VS)], incs=((VS, 1),))
    def c_rcp(e):
        return e.reciprocal(stats16[:, 5, 0:n], stats16[:, 4, 0:n])
    p.on("vector", c_rcp, waits=[p.now(VS)], incs=((VS, 1),))
    def c_rs(e):
        return e.activation(out=stats16[:, 6, 0:n], in_=stats16[:, 5, 0:n], func=AF.Sqrt)
    p.on("scalar", c_rs, waits=[p.now(VS)], incs=((VS, 1),))
    def c_nm(e):
        e.tensor_scalar(out=stats16[:, 2, 0:n], in0=stats16[:, 0, 0:n],
                        scalar1=-1.0 / D, scalar2=None, op0=ALU.mult, op1=ALU.bypass)
        return e.nop()
    p.on("vector", c_nm, waits=[p.now(VS)], incs=((VS, 1),))
    def c_nmrs(e):
        return e.tensor_tensor(stats16[:, 7, 0:n], stats16[:, 2, 0:n],
                               stats16[:, 6, 0:n], ALU.mult)
    p.on("vector", c_nmrs, waits=[p.now(VS)], incs=((VS, 1),))
    rdy = p.now(VS)
    for li in range(ntiles):
        def lnout(e, li=li):
            inst = e.tensor_scalar(out=xf(li), in0=xin(li), scalar1=stats16[:, 6, li:li + 1],
                                   scalar2=stats16[:, 7, li:li + 1], op0=ALU.mult, op1=ALU.add)
            if xb is not None:
                inst = e.tensor_copy(xb[:, li], xf(li))
            return inst
        p.on("vector", lnout, waits=([rdy] if li == 0 else []) + (post(li)[0] if post else []),
             incs=((VS, 1),))
        if post:
            post(li)[1]()
    if lndbg is not None:
        def lndump(e):
            return e.tensor_copy(lndbg[:, 0:8, 0:ntiles], stats16[:, 0:8, 0:ntiles])
        p.on("vector", lndump, incs=((VS, 1),))


def _prep(inputs):
    i = inputs
    o = {}
    o["x0"] = np.ascontiguousarray(i["L0_attn_x"])
    o["x0T"] = np.ascontiguousarray(i["L0_attn_x"].transpose(0, 2, 1))
    o["wqT"] = np.ascontiguousarray(i["L0_attn_Wq"].transpose(2, 0, 1).reshape(D, NH * DKH))
    o["wkT"] = np.ascontiguousarray(i["L0_attn_Wk"].transpose(2, 0, 1).reshape(D, NH * DKH))
    o["wvT"] = np.ascontiguousarray(i["L0_attn_Wv"].transpose(0, 2, 1))
    o["ident"] = np.eye(128, dtype=np.float32)
    o["gate_wT"] = np.ascontiguousarray(i["L0_ffn_gate_w"].T)
    o["wgT"] = np.ascontiguousarray(i["L0_ffn_wi_gate"].transpose(0, 2, 1))
    o["wuT"] = np.ascontiguousarray(i["L0_ffn_wi_up"].transpose(0, 2, 1))
    o["woT"] = np.ascontiguousarray(i["L0_ffn_wo"].transpose(0, 2, 1))
    o["iota8"] = np.arange(8, dtype=np.float32).reshape(8, 1)
    o["iota_row8"] = np.tile(np.arange(8, dtype=np.float32), (128, 1))
    ksel = np.zeros((16, 2), dtype=np.float32)
    ksel[0:8, 0] = 1.0
    ksel[8:16, 1] = 1.0
    o["ksel"] = ksel
    o["iota16"] = np.concatenate([np.arange(8), np.arange(8)]).astype(np.float32).reshape(16, 1)
    o["kofs"] = np.array([[0.0], [float(CAP_L)]], dtype=np.float32)
    xei = np.zeros((128, 48), dtype=np.uint32)
    for col in range(48):
        xei[:, col] = col * 128 + np.arange(128, dtype=np.uint32)
    o["xe_idx"] = xei
    bf = np.float16
    ones = (np.ones((128, 128), dtype=np.float32) / SEQ).astype(bf)
    zer = np.zeros((128, 128), dtype=bf)
    triu = (np.triu(np.ones((128, 128), dtype=np.float32)) / SEQ).astype(bf)
    for half in (0, 1):
        tiles = q_tiles_for(half)
        qm = np.zeros((NT, 32, 128, 128), dtype=bf)
        for li in range(NT):
            t = tiles[li]
            for kb in range(KBCNT[li]):
                if kb < t:
                    qm[li, kb] = ones
                elif kb == t:
                    qm[li, kb] = triu
        o[f"qmask_{half}"] = qm
    o["inprojT"] = np.ascontiguousarray(i["L1_attn_in_proj_w"].T)
    o["convw"] = np.ascontiguousarray(i["L1_attn_conv_w"].reshape(DI, DC))
    o["convb"] = np.ascontiguousarray(i["L1_attn_conv_b"].reshape(DI, 1))
    o["xprojT"] = np.ascontiguousarray(i["L1_attn_x_proj_w"].T)
    o["outprojT"] = np.ascontiguousarray(i["L1_attn_out_proj_w"].T)
    o["dtw_full"] = np.ascontiguousarray(i["L1_attn_dt_w"].reshape(DI, 1))
    o["dtb_full"] = np.ascontiguousarray(i["L1_attn_dt_b"].reshape(DI, 1))
    o["Alog_full"] = np.ascontiguousarray(i["L1_attn_A_log"])
    o["Dp_full"] = np.ascontiguousarray(i["L1_attn_D_param"].reshape(DI, 1))
    o["Win"] = np.ascontiguousarray(i["L1_ffn_Win"])
    o["binc"] = np.ascontiguousarray(i["L1_ffn_bin_"].reshape(M_FFN, 1))
    o["W1"] = np.ascontiguousarray(i["L1_ffn_W1"])
    o["b1"] = np.ascontiguousarray(i["L1_ffn_b1"])
    o["W2"] = np.ascontiguousarray(i["L1_ffn_W2"])
    o["b2"] = np.ascontiguousarray(i["L1_ffn_b2"].reshape(M_FFN, 1))
    o["Wout"] = np.ascontiguousarray(i["L1_ffn_Wout"])
    o["bout"] = np.ascontiguousarray(i["L1_ffn_bout"].reshape(1, D))
    return o


def _core_maps(prep, inputs):
    maps = []
    for cid in range(8):
        b, half = cid >> 1, cid & 1
        tiles = q_tiles_for(half)
        m = {}
        m["x0T_b"] = prep["x0T"][b]
        m["x0_own"] = prep["x0"][b, half * NTOK:(half + 1) * NTOK]
        xq = np.concatenate([prep["x0T"][b][:, t * 128:(t + 1) * 128] for t in tiles], axis=1)
        m["xqT"] = np.ascontiguousarray(xq)
        m["wqT"] = prep["wqT"]; m["wkT"] = prep["wkT"]; m["wvT"] = prep["wvT"]
        m["qmask"] = prep[f"qmask_{half}"]
        for k in ["ident", "gate_wT", "wgT", "wuT", "woT", "iota8", "iota_row8",
                  "ksel", "iota16", "kofs", "xe_idx"]:
            m[k] = prep[k]
        psel = np.zeros((8, 1), dtype=np.float32)
        psel[:cid] = 1.0
        m["prefix_sel"] = psel
        hidx = np.zeros((128, NT), dtype=np.uint32)
        for li_own in range(NT):
            t = half * NT + li_own
            src_core = cid if t in tiles else (cid ^ 1)
            src_tiles = q_tiles_for(src_core & 1)
            li_src = src_tiles.index(t)
            base = src_core * NTOK + li_src * 128
            hidx[:, li_own] = base + np.arange(128, dtype=np.uint32)
        m["hoz_idx"] = hidx
        hperm = np.arange(DI).reshape(2, 1024)[[half, 1 - half]].reshape(DI)
        m["dtw"] = prep["dtw_full"][hperm]
        m["dtb"] = prep["dtb_full"][hperm]
        m["Alog"] = np.ascontiguousarray(prep["Alog_full"][hperm])
        m["Dp"] = prep["Dp_full"][hperm]
        for k in ["inprojT", "convw", "convb", "xprojT", "outprojT", "Win", "binc",
                  "W1", "b1", "W2", "b2", "Wout", "bout"]:
            m[k] = prep[k]
        es = np.zeros((24, 4), dtype=np.float32)
        if half == 1:
            for j in range(3):
                es[(b * 2) * 3 + j, j] = 1.0
        m["edge_sel"] = es
        pp = np.arange(128, dtype=np.uint32)
        agu = np.zeros((128, 8, 2), dtype=np.uint32)
        for g in range(8):
            for th in range(2):
                agu[:, g, th] = (b * 2 + th) * 2048 + half * 1024 + g * 128 + pp
        m["ag1u_idx"] = agu
        bci = np.zeros((128, 2, 33), dtype=np.uint32)
        for th in range(2):
            bci[:, th, :] = (b * 2 + th) * 33 + np.arange(33, dtype=np.uint32)[None, :]
        m["bc_idx"] = bci
        ag2 = np.zeros((128, 16, 4), dtype=np.uint32)
        for gg in range(16):
            for oc in range(4):
                ag2[:, gg, oc] = ((b * 2 + gg // 8) * 8192 +
                                  ((gg % 8) * 8 + half * 4 + oc) * 128 + pp)
        m["ag2_idx"] = ag2
        maps.append(m)
    return maps


def kernel(**inputs):
    prep = _prep(inputs)
    in_maps = _core_maps(prep, inputs)
    nc = build_nc()
    res = run_bass_kernel_spmd(nc, in_maps, core_ids=list(range(8)))
    outs = []
    for cid in range(8):
        outs.append(res.results[cid]["out"])
    full = np.zeros((B, SEQ, D), dtype=np.float32)
    for cid in range(8):
        b, half = cid >> 1, cid & 1
        full[b, half * NTOK:(half + 1) * NTOK] = outs[cid]
    return full


def _debug_run(inputs):
    prep = _prep(inputs)
    in_maps = _core_maps(prep, inputs)
    nc = build_nc()
    return run_bass_kernel_spmd(nc, in_maps, core_ids=list(range(8)))


# version 02
# speedup vs baseline: 1.0058x; 1.0058x over previous
import numpy as np
import contextlib
import ml_dtypes
import concourse.bass as bass
import concourse.mybir as mybir
from concourse.bass import IndirectOffsetOnAxis
from concourse.bass_utils import run_bass_kernel_spmd

F32 = mybir.dt.float32
BF16 = mybir.dt.float16
U32 = mybir.dt.uint32
AF = mybir.ActivationFunctionType
ALU = mybir.AluOpType
AX = mybir.AxisListType

B, SEQ, D = 4, 4096, 1024
NTOK, NT = 2048, 16
NH, DKH = 8, 128
E_EXP, F_FF = 8, 4096
CAP_G, CAP_L = 2560, 384
NSLOT = E_EXP * 2 * CAP_L
DI, NST, DC = 2048, 16, 4
M_FFN, INNER = 64, 16
KB_ = 1024

KBCNT = [32, 31, 30, 29, 28, 27, 26, 25, 16, 15, 14, 13, 12, 11, 10, 9]

def q_tiles_for(half):
    if half == 0:
        return [31, 30, 29, 28, 27, 26, 25, 24, 7, 6, 5, 4, 3, 2, 1, 0]
    return [23, 22, 21, 20, 19, 18, 17, 16, 15, 14, 13, 12, 11, 10, 9, 8]

_STAGE = "full"


class Sem:
    __slots__ = ("name", "count", "h")
    def __init__(self, name):
        self.name, self.count, self.h = name, 0, None


class Prog:
    ENGINES = ("sync", "gpsimd", "tensor", "vector", "scalar")

    def __init__(self, nc):
        self.nc = nc
        self.items = []
        self.sems = []

    def sem(self, name):
        s = Sem(name)
        self.sems.append(s)
        return s

    def on(self, eng, fn, waits=(), incs=()):
        assert eng in self.ENGINES, eng
        for s, a in incs:
            s.count += a
        self.items.append((eng, tuple(waits), fn, tuple(incs)))

    def now(self, sem):
        return (sem, sem.count)

    def emit(self, ctx):
        for s in self.sems:
            s.h = ctx.enter_context(self.nc.semaphore(s.name))
        block = ctx.enter_context(self.nc.Block())
        items = self.items

        def make(name):
            def f(eng):
                seen = {}
                for (e, waits, fn, incs) in items:
                    if e != name:
                        continue
                    for (s, v) in waits:
                        if seen.get(s.name, -1) >= v:
                            continue
                        eng.wait_ge(s.h, v)
                        seen[s.name] = v
                    inst = fn(eng) if fn is not None else None
                    if incs:
                        if inst is None:
                            inst = eng.nop()
                        for (s, a) in incs:
                            inst = inst.then_inc(s.h, a)
            return f
        for name in self.ENGINES:
            getattr(block, name)(make(name))


class SbufMap:
    BASE = 32736
    LIMIT = 229344 - 32736

    def __init__(self, nc):
        self.nc = nc
        self.regions = []

    def at(self, name, shape, dtype, off, pstart, pend):
        sz = int(np.prod(shape[1:])) * mybir.dt.size(dtype)
        assert off + sz <= self.LIMIT, (name, off, sz, off + sz)
        for (n2, o2, s2, a2, b2) in self.regions:
            if not (pend <= a2 or b2 <= pstart):
                assert off + sz <= o2 or o2 + s2 <= off, \
                    f"overlap {name}[{off},{off + sz}) vs {n2}[{o2},{o2 + s2})"
        self.regions.append((name, off, sz, pstart, pend))
        return self.nc.alloc_sbuf_tensor_at(name, list(shape), dtype, offset=self.BASE + off)


def build_nc():
    nc = bass.Bass("TRN2", target_bir_lowering=False, detect_race_conditions=False)
    dri = lambda n, s, d=F32: nc.dram_tensor(n, s, d, kind="ExternalInput").ap()

    x0T_b = dri("x0T_b", [D, SEQ])
    xqT_in = dri("xqT", [D, NTOK])
    x0_own = dri("x0_own", [NTOK, D])
    wqT = dri("wqT", [D, NH * DKH])
    wkT = dri("wkT", [D, NH * DKH])
    wvT = dri("wvT", [NH, D, D])
    qmask_in = dri("qmask", [NT, 32, 128, 128], BF16)
    hoz_idx_in = dri("hoz_idx", [128, NT], U32)
    ident_in = dri("ident", [128, 128])
    gate_wT = dri("gate_wT", [D, E_EXP])
    wgT_in = dri("wgT", [E_EXP, D, F_FF])
    wuT_in = dri("wuT", [E_EXP, D, F_FF])
    woT_in = dri("woT", [E_EXP, F_FF, D])
    prefix_sel_in = dri("prefix_sel", [8, 1])
    iota8_in = dri("iota8", [8, 1])
    xe_idx_in = dri("xe_idx", [128, 48], U32)
    iota_row8_in = dri("iota_row8", [128, 8])
    inprojT_in = dri("inprojT", [D, 2 * DI])
    convw_in = dri("convw", [DI, DC])
    convb_in = dri("convb", [DI, 1])
    xprojT_in = dri("xprojT", [DI, 33])
    dtw_in = dri("dtw", [DI, 1])
    dtb_in = dri("dtb", [DI, 1])
    Alog_in = dri("Alog", [DI, NST])
    Dp_in = dri("Dp", [DI, 1])
    outprojT_in = dri("outprojT", [DI, D])
    Win_in = dri("Win", [D, M_FFN])
    binc_in = dri("binc", [M_FFN, 1])
    W1_in = dri("W1", [M_FFN, INNER])
    b1_in = dri("b1", [M_FFN, INNER])
    W2_in = dri("W2", [M_FFN, INNER])
    b2_in = dri("b2", [M_FFN, 1])
    Wout_in = dri("Wout", [M_FFN, D])
    bout_in = dri("bout", [1, D])
    edge_sel_in = dri("edge_sel", [24, 4])
    ag1u_idx_in = dri("ag1u_idx", [128, 8, 2], U32)
    bc_idx_in = dri("bc_idx", [128, 2, 33], U32)
    ag2_idx_in = dri("ag2_idx", [128, 16, 4], U32)
    ksel_in = dri("ksel", [16, 2])
    iota16_in = dri("iota16", [16, 1])
    kofs_in = dri("kofs", [2, 1])

    out_final = nc.dram_tensor("out", [NTOK, D], F32, kind="ExternalOutput").ap()
    dbg = {}
    if _STAGE == "attn":
        dbg["dbg_x1"] = nc.dram_tensor("dbg_x1", [NTOK, D], F32, kind="ExternalOutput").ap()
        dbg["dbg_acc"] = nc.dram_tensor("dbg_acc", [NTOK, D], F32, kind="ExternalOutput").ap()
        dbg["dbg_hoz"] = nc.dram_tensor("dbg_hoz", [NTOK, D], F32, kind="ExternalOutput").ap()
        dbg["dbg_xres"] = nc.dram_tensor("dbg_xres", [NTOK, D], F32, kind="ExternalOutput").ap()
        dbg["dbg_st"] = nc.dram_tensor("dbg_st", [128, 8, NT], F32, kind="ExternalOutput").ap()
    if _STAGE == "fulldbg":
        dbg["dbg_u"] = nc.dram_tensor("dbg_u", [128, 8 * 4096], BF16, kind="ExternalOutput").ap()
        dbg["dbg_ssm"] = nc.dram_tensor("dbg_ssm", [33, NTOK], BF16, kind="ExternalOutput").ap()
        dbg["dbg_yg"] = nc.dram_tensor("dbg_yg", [128, 16 * 2048], BF16, kind="ExternalOutput").ap()
        dbg["dbg_x3"] = nc.dram_tensor("dbg_x3", [NTOK, D], F32, kind="ExternalOutput").ap()
        dbg["dbg_x3p"] = nc.dram_tensor("dbg_x3p", [NTOK, D], F32, kind="ExternalOutput").ap()
        dbg["dbg_y"] = nc.dram_tensor("dbg_y", [8192, 512], BF16, kind="ExternalOutput").ap()
        dbg["dbg_z"] = nc.dram_tensor("dbg_z", [DI, NTOK], BF16, kind="ExternalOutput").ap()
        dbg["dbg_dt"] = nc.dram_tensor("dbg_dt", [128, 512], BF16, kind="ExternalOutput").ap()
        dbg["dbg_dl"] = nc.dram_tensor("dbg_dl", [128, 8 * 512], BF16, kind="ExternalOutput").ap()
        dbg["dbg_du"] = nc.dram_tensor("dbg_du", [128, 8 * 512], BF16, kind="ExternalOutput").ap()
        dbg["dbg_bc0"] = nc.dram_tensor("dbg_bc0", [128, 512], BF16, kind="ExternalOutput").ap()
        dbg["dbg_bc1"] = nc.dram_tensor("dbg_bc1", [128, 512], BF16, kind="ExternalOutput").ap()
        dbg["dbg_bc"] = dbg["dbg_bc0"]
        dbg["dbg_da"] = nc.dram_tensor("dbg_da", [128, 512], F32, kind="ExternalOutput").ap()
        dbg["dbg_hb"] = nc.dram_tensor("dbg_hb", [128, 512], BF16, kind="ExternalOutput").ap()
        dbg["dbg_hf"] = nc.dram_tensor("dbg_hf", [128, 8 * 512], BF16, kind="ExternalOutput").ap()
        dbg["dbg_yt"] = nc.dram_tensor("dbg_yt", [128, 512], BF16, kind="ExternalOutput").ap()
    if _STAGE == "moe":
        dbg["dbg_x1"] = nc.dram_tensor("dbg_x1", [NTOK, D], F32, kind="ExternalOutput").ap()
        dbg["dbg_rc"] = nc.dram_tensor("dbg_rc", [128, 12, NT], F32, kind="ExternalOutput").ap()
        dbg["dbg_slot"] = nc.dram_tensor("dbg_slot", [128, 4, NT], F32, kind="ExternalOutput").ap()
        dbg["dbg_eo"] = nc.dram_tensor("dbg_eo", [NSLOT, D], BF16, kind="ExternalOutput").ap()
        dbg["dbg_xp"] = nc.dram_tensor("dbg_xp", [NTOK, D], F32, kind="ExternalOutput").ap()
        dbg["dbg_xd"] = nc.dram_tensor("dbg_xd", [NSLOT, D], BF16, kind="ExternalOutput").ap()
        dbg["dbg_acc"] = nc.dram_tensor("dbg_acc", [NTOK, D], F32, kind="ExternalOutput").ap()
        dbg["dbg_hoz"] = nc.dram_tensor("dbg_hoz", [NTOK, D], F32, kind="ExternalOutput").ap()
        dbg["dbg_xres"] = nc.dram_tensor("dbg_xres", [NTOK, D], F32, kind="ExternalOutput").ap()
        dbg["dbg_st"] = nc.dram_tensor("dbg_st", [128, 8, NT], F32, kind="ExternalOutput").ap()
        

    ag0_in = nc.dram_tensor("ag0_in", [NTOK, D], BF16).ap()
    ag0_out = nc.dram_tensor("ag0_out", [8 * NTOK, D], BF16).ap()
    xd = nc.dram_tensor("xd_disp", [NSLOT, D], BF16).ap()
    eo = nc.dram_tensor("eo_disp", [NSLOT, D], BF16).ap()
    cnt_in = nc.dram_tensor("cnt_in", [1, 16], F32).ap()
    cnt_out = nc.dram_tensor("cnt_out", [8, 16], F32).ap()
    edg_in = nc.dram_tensor("edg_in", [3, D], BF16).ap()
    edg_out = nc.dram_tensor("edg_out", [24, D], BF16).ap()
    zT_d = nc.dram_tensor("zT_d", [DI, NTOK], BF16).ap()
    ag1u_in = nc.dram_tensor("ag1u_in", [DI, NTOK], BF16).ap()
    ag1u_out = nc.dram_tensor("ag1u_out", [8 * DI, NTOK], BF16).ap()
    ag1s_in = nc.dram_tensor("ag1s_in", [33, NTOK], BF16).ap()
    ag1s_out = nc.dram_tensor("ag1s_out", [8 * 33, NTOK], BF16).ap()
    ag2_in = nc.dram_tensor("ag2_in", [8192, 512], BF16).ap()
    ag2_out = nc.dram_tensor("ag2_out", [8 * 8192, 512], BF16).ap()

    ctx = contextlib.ExitStack()
    sbt = lambda n, s, d: ctx.enter_context(nc.sbuf_tensor(n, s, d))
    ident = sbt("identb", [128, 128], BF16)
    identf = sbt("identf", [128, 128], F32)
    stats16 = sbt("stats16", [128, 8, NT], F32)
    sq_scr = sbt("sq_scr", [128, 1024], F32)
    hoz_idx = sbt("hoz_idx_s", [128, NT], U32)
    zeros2k = sbt("zeros2k", [128, 1024], BF16)
    ones16 = sbt("ones16", [128, 16], F32)
    gwT_s = sbt("gwT_s", [128, 8, E_EXP], F32)
    psel_s = sbt("psel_s", [8, 1], F32)
    iota8_s = sbt("iota8_s", [8, 1], F32)
    xe_idx = sbt("xe_idx_s", [128, 48], U32)
    iota_row8 = sbt("iota_row8_s", [128, 8], F32)
    ksel_s = sbt("ksel_s", [16, 2], F32)
    iota16_s = sbt("iota16_s", [16, 1], F32)
    kofs_s = sbt("kofs_s", [2, 1], F32)
    cnt_sb = sbt("cnt_sb", [8, 16], F32)
    offs_c = sbt("offs_c", [8, 2], F32)
    routc = sbt("routc", [128, 12, NT], F32)
    slotcv = sbt("slotcv", [128, 4, NT], U32)
    ones_row = sbt("ones_row", [1, 128], BF16)
    xpj_s = sbt("xpj_s", [128, 16, 33], BF16)
    edge_sel = sbt("edge_sel_s", [24, 4], BF16)
    convw_s = sbt("convw_s", [128, 16, DC], F32)
    convb_s = sbt("convb_s", [128, 16], F32)
    dtw_s = sbt("dtw_s", [128, 8], F32)
    dtb_s = sbt("dtb_s", [128, 8], F32)
    Acol_s = sbt("Acol_s", [128, 8, NST], F32)
    Dp_s = sbt("Dp_s", [128, 8], F32)
    hlast = sbt("hlast", [128, 128], F32)
    ag1u_idx = sbt("ag1u_idx_s", [128, 8, 2], U32)
    bc_idx = sbt("bc_idx_s", [128, 2, 33], U32)
    ag2_idx = sbt("ag2_idx_s", [128, 16, 4], U32)
    binc_s = sbt("binc_s", [M_FFN, 1], F32)
    w1_s = sbt("w1_s", [M_FFN, INNER], F32)
    b1_s = sbt("b1_s", [M_FFN, INNER], F32)
    w2_s = sbt("w2_s", [M_FFN, INNER], F32)
    b2_s = sbt("b2_s", [M_FFN, 1], F32)
    boutT_s = sbt("boutT_s", [128, 8], F32)
    win_s = sbt("win_s", [128, 8, M_FFN], BF16)

    edgeT = sbt("edgeT", [128, 8, 4], BF16)
    e3 = sbt("e3", [128, 16, 4], F32)

    lnstdbg = sbt("lnstdbg", [128, 8, NT], F32) if _STAGE in ("attn", "moe") else None
    psum = ctx.enter_context(nc.psum_tensor("ps", [128, 8, 512], F32))

    smap = SbufMap(nc)
    vh = smap.at("vh", [128, 32, 1024], BF16, 0, 1, 2)
    acc = smap.at("acc", [128, NT, 1024], BF16, 64 * KB_, 1, 2)
    khT = smap.at("khT", [128, 32, 128], BF16, 96 * KB_, 1, 2)
    qhT = smap.at("qhT", [128, NT, 128], BF16, 104 * KB_, 1, 2)
    sstr = smap.at("sstr", [128, 32, 128], BF16, 108 * KB_, 1, 2)
    xtc = smap.at("xtc", [128, 2, 8, 512], BF16, 116 * KB_, 1, 2)
    wvb = smap.at("wvb", [128, 2, 8, 1024], BF16, 132 * KB_, 1, 2)
    wqkb = smap.at("wqkb", [128, 2, 2, 8, 128], BF16, 164 * KB_, 1, 2)
    qmb = smap.at("qmb", [128, 2, 32, 128], BF16, 172 * KB_, 1, 2)
    x1f = smap.at("x1f", [128, NT, 1024], F32, 0, 2, 3.9)
    x1b = smap.at("x1b", [128, NT, 1024], BF16, 128 * KB_, 2, 3.5)
    hoz = smap.at("hoz", [128, NT, 1024], BF16, 160 * KB_, 2, 3.0)
    xres = smap.at("xres", [128, NT, 1024], F32, 64 * KB_, 2, 3.0)
    routr = smap.at("routr", [128, 5, NTOK], F32, 64 * KB_, 3.0, 3.5)
    x1Th = smap.at("x1Th", [128, 8, 512], F32, 104 * KB_, 3.0, 3.15)
    routp = smap.at("routp", [128, 3, NTOK], F32, 104 * KB_, 3.16, 3.5)
    routq = smap.at("routq", [128, 4, NTOK], F32, 160 * KB_, 3.0, 3.5)
    gu = smap.at("gu", [128, 32, 768], BF16, 64 * KB_, 3.5, 3.8)
    wgu = smap.at("wgu", [128, 2, 2, 8, 512], BF16, 112 * KB_, 3.5, 3.8)
    xe = smap.at("xe", [128, 6, 1024], BF16, 160 * KB_, 3.5, 3.8)
    xeT = smap.at("xeT", [128, 8, 768], BF16, 172 * KB_, 3.5, 3.8)
    wob = smap.at("wob", [128, 2, 1024], BF16, 184 * KB_, 3.5, 3.8)
    gsil = smap.at("gsil", [128, 2, 384], BF16, 188 * KB_, 3.5, 3.8)
    eos = smap.at("eos", [128, 2, 1024], BF16, 144 * KB_, 3.5, 3.8)
    cmb = smap.at("cmb", [128, 2, 2, 1024], BF16, 112 * KB_, 3.8, 3.95)
    x2preA = smap.at("x2preA", [128, 8, 1024], F32, 64 * KB_, 3.8, 3.96)
    x2preB = smap.at("x2preB", [128, 8, 1024], F32, 128 * KB_, 3.8, 3.96)
    x2f = smap.at("x2f", [128, NT, 1024], F32, 0, 3.97, 5.6)
    x2b = smap.at("x2b", [128, NT, 1024], BF16, 64 * KB_, 3.97, 4.2)
    x2T = smap.at("x2T", [128, 8, 1024], BF16, 96 * KB_, 4.0, 4.18)
    zst = smap.at("zst", [128, 2048], BF16, 112 * KB_, 4.0, 4.28)
    xinT = smap.at("xinT", [128, 16, 2048], BF16, 128 * KB_, 4.0, 4.45)
    ctmp = smap.at("ctmp", [128, 2048], F32, 116 * KB_, 4.185, 4.285)
    bcb = smap.at("bcb", [128, 2, 512], BF16, 64 * KB_, 4.3, 4.85)
    ccb = smap.at("ccb", [128, 2, 512], BF16, 66 * KB_, 4.3, 4.85)
    dub = smap.at("dub", [128, 2, 512], BF16, 72 * KB_, 4.3, 4.85)
    hball = smap.at("hball", [128, 8, 512], BF16, 74 * KB_, 4.3, 4.85)
    hcb = smap.at("hcb", [128, 2, 512], BF16, 82 * KB_, 4.3, 4.85)
    ytst = smap.at("ytst", [128, 2, 512], BF16, 84 * KB_, 4.3, 4.85)
    dtbc = smap.at("dtbc", [128, 512], BF16, 88 * KB_, 4.3, 4.85)
    dlt8 = smap.at("dlt8", [128, 8, 512], BF16, 90 * KB_, 4.3, 4.85)
    dab = smap.at("dab", [128, 2, 512], F32, 98 * KB_, 4.3, 4.85)
    du8 = smap.at("du8", [128, 8, 512], BF16, 102 * KB_, 4.3, 4.85)
    uT_sc = smap.at("uT_sc", [128, 8, 4096], BF16, 128 * KB_, 4.5, 4.85)
    ygT = smap.at("ygT", [128, 16, 2048], BF16, 128 * KB_, 4.86, 4.94)
    zrl = smap.at("zrl", [128, 2, 2048], BF16, 100 * KB_, 4.86, 4.94)
    aoT = smap.at("aoT", [128, 8, 2048], BF16, 64 * KB_, 4.86, 4.96)
    x3f = smap.at("x3f", [128, NT, 1024], F32, 96 * KB_, 4.975, 5.9)
    x3b = smap.at("x3b", [128, NT, 1024], BF16, 160 * KB_, 4.95, 5.9)
    x3T = smap.at("x3T", [128, 8, NTOK], BF16, 0, 5.62, 5.8)
    stg = smap.at("stg", [128, 1024], F32, 64 * KB_, 5.8, 6.0)
    wout_s = smap.at("wout_s", [M_FFN, D], BF16, 68 * KB_, 5.3, 6.0)
    sfn = smap.at("sfn", [64, 4, NTOK], F32, 32 * KB_, 5.65, 5.85)
    foT = smap.at("foT", [128, 8, NTOK], BF16, 0, 5.81, 5.95)

    p = Prog(nc)
    LD = p.sem("ld")
    XT = p.sem("xts")
    WL = p.sem("wl")
    QM = p.sem("qm")
    MM = p.sem("mm")
    EV = p.sem("ev")
    ACC = p.sem("accs")
    AGS = p.sem("ags")
    STO = p.sem("sto")
    VS = p.sem("vs")
    GP = p.sem("gp")

    p.on("gpsimd", lambda e: e.dma_start(out=ident[:], in_=ident_in), incs=((LD, 16),))
    p.on("sync", lambda e: e.dma_start(out=identf[:], in_=ident_in), incs=((LD, 16),))
    p.on("sync", lambda e: e.dma_start(out=hoz_idx[:], in_=hoz_idx_in), incs=((LD, 16),))
    p.on("sync", lambda e: e.dma_start(out=iota_row8[:], in_=iota_row8_in), incs=((LD, 16),))
    p.on("sync", lambda e: e.dma_start(out=ksel_s[:], in_=ksel_in), incs=((LD, 16),))
    p.on("sync", lambda e: e.dma_start(out=iota16_s[:], in_=iota16_in), incs=((LD, 16),))
    p.on("sync", lambda e: e.dma_start(out=kofs_s[:], in_=kofs_in), incs=((LD, 16),))
    p.on("sync", lambda e: e.dma_start(out=psel_s[:], in_=prefix_sel_in), incs=((LD, 16),))
    p.on("sync", lambda e: e.dma_start(out=iota8_s[:], in_=iota8_in), incs=((LD, 16),))
    p.on("sync", lambda e: e.dma_start(out=xe_idx[:], in_=xe_idx_in), incs=((LD, 16),))
    p.on("sync", lambda e: e.dma_start(out=gwT_s[:], in_=gate_wT.rearrange("(c p) m -> p c m", p=128)), incs=((LD, 16),))
    p.on("gpsimd", lambda e: e.memset(zeros2k[:], 0.0), incs=((LD, 1),))
    p.on("gpsimd", lambda e: e.memset(ones16[:], 1.0), incs=((LD, 1),))
    CONST = p.now(LD)

    bufdep = {}
    def waitdep(key):
        return [bufdep[key]] if key in bufdep else []
    def setdep(key, sem):
        bufdep[key] = (sem, sem.count)

    for h in range(NH):
        par = h & 1
        def ld_wq(e, h=h, par=par):
            return e.dma_start(out=wqkb[:, par, 0],
                               in_=wqT.rearrange("(c p) m -> p c m", p=128)[:, :, h * DKH:(h + 1) * DKH])
        p.on("gpsimd", ld_wq, waits=waitdep(("wqk", par)), incs=((WL, 16),))
        def ld_wk(e, h=h, par=par):
            return e.dma_start(out=wqkb[:, par, 1],
                               in_=wkT.rearrange("(c p) m -> p c m", p=128)[:, :, h * DKH:(h + 1) * DKH])
        p.on("gpsimd", ld_wk, incs=((WL, 16),))
        def ld_wv(e, h=h, par=par):
            return e.dma_start(out=wvb[:, par],
                               in_=wvT[h].rearrange("(c p) v -> p c v", p=128))
        p.on("gpsimd", ld_wv, waits=waitdep(("wv", par)), incs=((WL, 16),))
        wl_ready = p.now(WL)

        for ci in range(8):
            st0 = ci * 512
            cpar = ci & 1
            def ld_xt(e, st0=st0, cpar=cpar):
                return e.dma_start(out=xtc[:, cpar],
                                   in_=x0T_b[:, st0:st0 + 512].rearrange("(c p) k -> p c k", p=128))
            p.on("gpsimd", ld_xt, waits=waitdep(("xt", cpar)) + [CONST], incs=((XT, 16),))
            xt_ready = p.now(XT)

            def mm_k(e, par=par, cpar=cpar):
                for c in range(8):
                    inst = e.matmul(psum[:, 0, :], wqkb[:, par, 1, c], xtc[:, cpar, c],
                                    start=(c == 0), stop=(c == 7))
                return inst
            p.on("tensor", mm_k, waits=[xt_ready, wl_ready] + waitdep(("bank", 0)), incs=((MM, 1),))
            def ev_k(e, ci=ci):
                return e.copy(out=khT.ap().rearrange("p a b -> p (a b)")[:, ci * 512:(ci + 1) * 512],
                              in_=psum[:, 0, :])
            p.on("scalar", ev_k, waits=[p.now(MM)], incs=((EV, 1),))
            setdep(("bank", 0), EV)

            for tt in range(4):
                kb = ci * 4 + tt
                for v2 in range(2):
                    bank = 2 + ((kb * 2 + v2) & 1)
                    def mm_v(e, par=par, cpar=cpar, tt=tt, v2=v2, bank=bank):
                        for c in range(8):
                            inst = e.matmul(psum[:, bank, :],
                                            xtc[:, cpar, c, tt * 128:(tt + 1) * 128],
                                            wvb[:, par, c, v2 * 512:(v2 + 1) * 512],
                                            start=(c == 0), stop=(c == 7))
                        return inst
                    p.on("tensor", mm_v, waits=waitdep(("bank", bank)), incs=((MM, 1),))
                    def ev_v(e, kb=kb, v2=v2, bank=bank):
                        return e.copy(out=vh[:, kb, v2 * 512:(v2 + 1) * 512],
                                      in_=psum[:, bank, :])
                    p.on("scalar", ev_v, waits=[p.now(MM)], incs=((EV, 1),))
                    setdep(("bank", bank), EV)
            setdep(("xt", cpar), MM)

        for qb in range(4):
            cpar = qb & 1
            def ld_xq(e, qb=qb, cpar=cpar):
                return e.dma_start(out=xtc[:, cpar],
                                   in_=xqT_in[:, qb * 512:(qb + 1) * 512].rearrange("(c p) k -> p c k", p=128))
            p.on("gpsimd", ld_xq, waits=waitdep(("xt", cpar)), incs=((XT, 16),))
            xq_ready = p.now(XT)
            for j in range(4):
                li = qb * 4 + j
                def mm_q(e, par=par, cpar=cpar, j=j):
                    for c in range(8):
                        inst = e.matmul(psum[:, 1, 0:128], wqkb[:, par, 0, c],
                                        xtc[:, cpar, c, j * 128:(j + 1) * 128],
                                        start=(c == 0), stop=(c == 7))
                    return inst
                p.on("tensor", mm_q, waits=[xq_ready] + waitdep(("bank", 1)), incs=((MM, 1),))
                def ev_q(e, li=li):
                    return e.copy(out=qhT[:, li], in_=psum[:, 1, 0:128])
                p.on("scalar", ev_q, waits=[p.now(MM)], incs=((EV, 1),))
                setdep(("bank", 1), EV)
            setdep(("xt", cpar), MM)
        setdep(("wqk", par), MM)
        setdep(("wv", par), MM)
        kv_built = p.now(EV)

        for li in range(NT):
            nkb = KBCNT[li]
            mpar = li & 1
            def ld_qm(e, li=li, mpar=mpar, nkb=nkb):
                return e.dma_start(out=qmb[:, mpar, 0:nkb],
                                   in_=qmask_in[li, 0:nkb].rearrange("a p b -> p a b"))
            p.on("sync", ld_qm, waits=waitdep(("qm", mpar)), incs=((QM, 16),))
            qm_ready = p.now(QM)

            for kb in range(nkb):
                bank = 4 + (kb & 1)
                def mm_s(e, kb=kb, li=li, bank=bank):
                    return e.matmul(psum[:, bank, 0:128], khT[:, kb], qhT[:, li],
                                    start=True, stop=True)
                p.on("tensor", mm_s, waits=[kv_built, qm_ready] + waitdep(("bank", bank)),
                     incs=((MM, 1),))
                def ev_s(e, kb=kb, bank=bank, mpar=mpar):
                    return e.scalar_tensor_tensor(out=sstr[:, kb], in0=psum[:, bank, 0:128],
                                                  scalar=0.0, in1=qmb[:, mpar, kb],
                                                  op0=ALU.max, op1=ALU.mult)
                p.on("vector", ev_s, waits=[p.now(MM)], incs=((EV, 1),))
                setdep(("bank", bank), EV)
            s_done = p.now(EV)
            setdep(("qm", mpar), EV)

            for v2 in range(2):
                bank = 6 + v2
                def mm_av(e, li=li, nkb=nkb, v2=v2, bank=bank):
                    for kb in range(nkb):
                        inst = e.matmul(psum[:, bank, :], sstr[:, kb],
                                        vh[:, kb, v2 * 512:(v2 + 1) * 512],
                                        start=(kb == 0), stop=(kb == nkb - 1))
                    return inst
                p.on("tensor", mm_av, waits=[s_done] + waitdep(("bank", bank)),
                     incs=((MM, 1),))
                if h == 0:
                    def ev_av(e, li=li, v2=v2, bank=bank):
                        return e.tensor_copy(acc[:, li, v2 * 512:(v2 + 1) * 512],
                                             psum[:, bank, :])
                else:
                    def ev_av(e, li=li, v2=v2, bank=bank):
                        sl = acc[:, li, v2 * 512:(v2 + 1) * 512]
                        return e.tensor_tensor(sl, sl, psum[:, bank, :], ALU.add)
                p.on("vector", ev_av, waits=[p.now(MM)], incs=((ACC, 1),))
                setdep(("bank", bank), ACC)

    attn_done = p.now(ACC)

    for li in range(NT):
        def ag0_store(e, li=li):
            return e.dma_start(out=ag0_in[li * 128:(li + 1) * 128, :], in_=acc[:, li])
        p.on("gpsimd", ag0_store, waits=([attn_done] if li == 0 else []), incs=((STO, 16),))

    def ag0_run(e):
        return e.collective_compute("AllGather", ALU.bypass, ins=[ag0_in.opt()],
                                    outs=[ag0_out.opt()], replica_groups=[list(range(8))])
    p.on("gpsimd", ag0_run, waits=[p.now(STO)], incs=((AGS, 1),))

    for li in range(NT):
        def ag0_load(e, li=li):
            return e.indirect_dma_start(
                out=hoz[:, li], out_offset=None, in_=ag0_out,
                in_offset=IndirectOffsetOnAxis(ap=hoz_idx[:, li:li + 1], axis=0))
        p.on("gpsimd", ag0_load, waits=([p.now(AGS), CONST] if li == 0 else []),
             incs=((GP, 16),))
    hoz_ld = p.now(GP)

    def ld_x0(e):
        return e.dma_start(out=x1f[:], in_=x0_own.rearrange("(a p) b -> p a b", p=128))
    p.on("sync", ld_x0, waits=[attn_done], incs=((XT, 16),))
    x0_ld = p.now(XT)

    for li in range(NT):
        def res(e, li=li):
            return e.scalar_tensor_tensor(out=xres[:, li], in0=x1f[:, li], scalar=2.0,
                                          in1=hoz[:, li], op0=ALU.mult, op1=ALU.add)
        p.on("vector", res, waits=([hoz_ld, x0_ld] if li == 0 else []), incs=((VS, 1),))
    _emit_ln_block(p, NT, xres, x1f, x1b, stats16, sq_scr, VS,
                   lndbg=(lnstdbg if "dbg_st" in dbg else None))
    x1_done = p.now(VS)


    STOP = {"attn": 1, "moe": 2, "l1": 9, "full": 9, "fulldbg": 9}[_STAGE]
    if STOP >= 2:
        MO = p.sem("mo")
        WM = p.sem("wm")
        SC = p.sem("scat")
        psum_bf = psum.ap().bitcast(BF16)

        for qp in range(4):
            for j in range(4):
                li = qp * 4 + j
                for c in range(8):
                    bank = 6 + (c & 1)
                    def tr_x1(e, li=li, c=c, bank=bank):
                        return e.transpose(psum[:, bank, 0:128], x1f[:, li, c * 128:(c + 1) * 128], identf[:])
                    p.on("tensor", tr_x1, waits=[x1_done] + waitdep(("bank", bank)), incs=((MM, 1),))
                    def ev_tx(e, j=j, c=c, bank=bank):
                        return e.tensor_copy(x1Th[:, c, j * 128:(j + 1) * 128], psum[:, bank, 0:128])
                    p.on("vector", ev_tx, waits=[p.now(MM)], incs=((EV, 1),))
                    setdep(("bank", bank), EV)
            def mm_lg(e, qp=qp):
                for c in range(8):
                    inst = e.matmul(psum[0:8, 0, :], gwT_s[:, c], x1Th[:, c, :],
                                    start=(c == 0), stop=(c == 7))
                return inst
            p.on("tensor", mm_lg, waits=[p.now(EV), CONST] + waitdep(("bank", 0)), incs=((MM, 1),))
            def ev_lg(e, qp=qp):
                return e.tensor_copy(routr[0:8, 0, qp * 512:(qp + 1) * 512], psum[0:8, 0, :])
            p.on("vector", ev_lg, waits=[p.now(MM)], incs=((EV, 1),))
            setdep(("bank", 0), EV)
            setdep(("bank", 6), EV)
            setdep(("bank", 7), EV)
        lg_done = p.now(EV)

        ltm = sbt("ltm", [128, 8], F32)
        top8 = sbt("top8", [128, 8], F32)
        tidx = sbt("tidx", [128, 8], U32)
        oh_tm = sbt("oh_tm", [128, 2, NT, 8], F32)
        for li in range(NT):
            def tr_lg(e, li=li):
                return e.transpose(psum[:, 6, 0:8], routr[0:8, 0, li * 128:(li + 1) * 128], identf[0:8, 0:8])
            p.on("tensor", tr_lg, waits=[lg_done] + waitdep("rbank6"), incs=((MM, 1),))
            def ev_tl(e):
                return e.tensor_copy(ltm[:], psum[:, 6, 0:8])
            p.on("vector", ev_tl, waits=[p.now(MM)], incs=((MO, 1),))
            setdep("rbank6", MO)
            def mx8(e):
                return e.max(top8[:], ltm[:])
            p.on("vector", mx8, waits=[p.now(MO)], incs=((MO, 1),))
            def mxi(e):
                return e.max_index(tidx[:], top8[:], ltm[:])
            p.on("vector", mxi, waits=[p.now(MO)], incs=((MO, 1),))
            def cps(e, li=li):
                e.tensor_copy(routc[:, 0, li:li + 1], top8[:, 0:1])
                e.tensor_copy(routc[:, 1, li:li + 1], top8[:, 1:2])
                e.tensor_copy(routc[:, 2, li:li + 1], tidx[:, 0:1])
                return e.tensor_copy(routc[:, 3, li:li + 1], tidx[:, 1:2])
            p.on("vector", cps, waits=[p.now(MO)], incs=((MO, 1),))
            def ohs(e, li=li):
                e.tensor_scalar(out=oh_tm[:, 0, li], in0=iota_row8[:], scalar1=routc[:, 2, li:li + 1],
                                scalar2=None, op0=ALU.is_equal, op1=ALU.bypass)
                return e.tensor_scalar(out=oh_tm[:, 1, li], in0=iota_row8[:], scalar1=routc[:, 3, li:li + 1],
                                       scalar2=None, op0=ALU.is_equal, op1=ALU.bypass)
            p.on("vector", ohs, waits=[p.now(MO)], incs=((MO, 1),))
        def c_ew(e):
            e.tensor_tensor(routc[:, 10, :], routc[:, 0, :], routc[:, 1, :], ALU.subtract)
            return e.nop()
        p.on("vector", c_ew, waits=[p.now(MO)], incs=((MO, 1),))
        def c_sig(e):
            return e.activation(out=routc[:, 4, :], in_=routc[:, 10, :], func=AF.Sigmoid)
        p.on("scalar", c_sig, waits=[p.now(MO)], incs=((MO, 1),))
        def c_ew1(e):
            return e.tensor_scalar(out=routc[:, 5, :], in0=routc[:, 4, :], scalar1=-1.0,
                                   scalar2=1.0, op0=ALU.mult, op1=ALU.add)
        p.on("vector", c_ew1, waits=[p.now(MO)], incs=((MO, 1),))

        for k in range(2):
            for li in range(NT):
                bank = 6 + (li & 1)
                def tr_oh(e, k=k, li=li, bank=bank):
                    return e.transpose(psum[0:8, bank, 0:128], oh_tm[:, k, li], identf[:])
                p.on("tensor", tr_oh, waits=[p.now(MO)] + waitdep(("bank", bank)), incs=((MM, 1),))
                def ev_oh(e, k=k, li=li, bank=bank):
                    return e.tensor_copy(routr[0:8, 1 + k, li * 128:(li + 1) * 128],
                                         psum[0:8, bank, 0:128])
                p.on("vector", ev_oh, waits=[p.now(MM)], incs=((MO, 1),))
                setdep(("bank", bank), MO)
        mask_done = p.now(MO)
        mask_r = lambda k: routr[0:8, 1 + k, :]
        cum_r = lambda k: routr[0:8, 3 + k, :]
        for k in range(2):
            def c_cum(e, k=k):
                return e.tensor_tensor_scan(out=cum_r(k), data0=mask_r(k), data1=mask_r(k),
                                            initial=0.0, op0=ALU.add, op1=ALU.bypass)
            p.on("vector", c_cum, waits=[mask_done], incs=((MO, 1),))
        for k in range(2):
            def tr_cnt(e, k=k):
                return e.transpose(psum[0:1, 6, k * 8:(k + 1) * 8],
                                   routr[0:8, 3 + k, NTOK - 1:NTOK], identf[0:8, 0:8])
            p.on("tensor", tr_cnt, waits=[p.now(MO)] + (waitdep(("bank", 6)) if k == 0 else []),
                 incs=((MM, 1),))
        def ev_cnt(e):
            return e.tensor_copy(cnt_sb[0:1, :], psum[0:1, 6, 0:16])
        p.on("vector", ev_cnt, waits=[p.now(MM)], incs=((MO, 1),))
        setdep(("bank", 6), MO)
        def st_cnt(e):
            return e.dma_start(out=cnt_in, in_=cnt_sb[0:1, :])
        p.on("gpsimd", st_cnt, waits=[p.now(MO)], incs=((STO, 16),))
        def ag_cnt(e):
            return e.collective_compute("AllGather", ALU.bypass, ins=[cnt_in.opt()],
                                        outs=[cnt_out.opt()], replica_groups=[list(range(8))])
        p.on("gpsimd", ag_cnt, waits=[p.now(STO)], incs=((AGS, 1),))
        def ld_cnt(e):
            return e.dma_start(out=cnt_sb[:], in_=cnt_out)
        p.on("gpsimd", ld_cnt, waits=[p.now(AGS)], incs=((GP, 16),))
        def mm_pre(e):
            return e.matmul(psum[0:1, 0, 0:16], psel_s[:], cnt_sb[:], start=True, stop=True)
        p.on("tensor", mm_pre, waits=[p.now(GP)] + waitdep(("bank", 0)), incs=((MM, 1),))
        def ev_pre(e):
            return e.tensor_copy(routq[0:1, 2, 0:16], psum[0:1, 0, 0:16])
        p.on("vector", ev_pre, waits=[p.now(MM)], incs=((MO, 1),))
        for k in range(2):
            def tr_offk(e, k=k):
                return e.transpose(psum[0:8, 6, k * 2:k * 2 + 1],
                                   routq[0:1, 2, k * 8:(k + 1) * 8], identf[0:1, 0:1])
            p.on("tensor", tr_offk, waits=[p.now(MO)] + (waitdep(("bank", 6)) if k == 0 else []),
                 incs=((MM, 1),))
        def ev_offk(e):
            e.tensor_copy(offs_c[0:8, 0:1], psum[0:8, 6, 0:1])
            return e.tensor_copy(offs_c[0:8, 1:2], psum[0:8, 6, 2:3])
        p.on("vector", ev_offk, waits=[p.now(MM)], incs=((MO, 1),))
        setdep(("bank", 6), MO)
        for k in range(2):
            def c_posx(e, k=k):
                return e.tensor_tensor(cum_r(k), cum_r(k), mask_r(k), ALU.subtract)
            p.on("vector", c_posx, waits=[p.now(MO)], incs=((MO, 1),))
        for k in range(2):
            def c_prod(e, k=k):
                return e.tensor_tensor(routq[0:8, k, :], cum_r(k), mask_r(k), ALU.mult)
            p.on("vector", c_prod, waits=[p.now(MO)], incs=((MO, 1),))
        prods_done = p.now(MO)
        for (qi, src_fn, lhs_fn) in [
            (0, lambda k: routq[0:8, k, :], lambda k: ones16[0:8, 0:1]),
            (1, lambda k: mask_r(k), lambda k: offs_c[0:8, k:k + 1]),
            (2, lambda k: mask_r(k), lambda k: iota8_s[:]),
        ]:
            for k in range(2):
                for c4 in range(4):
                    bank = c4 & 1
                    def mm_row(e, qi=qi, k=k, c4=c4, bank=bank, src_fn=src_fn, lhs_fn=lhs_fn):
                        return e.matmul(psum[0:1, bank, :], lhs_fn(k),
                                        src_fn(k)[:, c4 * 512:(c4 + 1) * 512], start=True, stop=True)
                    p.on("tensor", mm_row, waits=[prods_done] + waitdep(("bank", bank)), incs=((MM, 1),))
                    if k == 0:
                        def ev_rw(e, qi=qi, c4=c4, bank=bank):
                            return e.tensor_copy(routp[0:1, qi, c4 * 512:(c4 + 1) * 512],
                                                 psum[0:1, bank, :])
                    else:
                        def ev_rw(e, qi=qi, c4=c4, bank=bank):
                            return e.tensor_copy(routq[0:1, 3, c4 * 512:(c4 + 1) * 512],
                                                 psum[0:1, bank, :])
                    p.on("vector", ev_rw, waits=[p.now(MM)], incs=((MO, 1),))
                    setdep(("bank", bank), MO)
                if k == 1:
                    def mv_row(e, qi=qi):
                        return e.dma_start(out=routp[1:2, qi, :], in_=routq[0:1, 3, :])
                    p.on("gpsimd", mv_row, waits=[p.now(MO)], incs=((GP, 16),))
        rows_done = p.now(GP)
        PT = lambda: routp[0:2, 0, :]
        OT = lambda: routp[0:2, 1, :]
        ET = lambda: routp[0:2, 2, :]
        def q_v1(e):
            e.tensor_tensor(routq[0:2, 0, :], PT(), OT(), ALU.add)
            return e.nop()
        p.on("vector", q_v1, waits=[rows_done], incs=((MO, 1),))
        def q_v2(e):
            e.tensor_scalar(out=routq[0:2, 0, :], in0=routq[0:2, 0, :], scalar1=float(CAP_G),
                            scalar2=None, op0=ALU.is_lt, op1=ALU.bypass)
            return e.nop()
        p.on("vector", q_v2, waits=[p.now(MO)], incs=((MO, 1),))
        def q_v3(e):
            e.tensor_scalar(out=routq[0:2, 1, :], in0=PT(), scalar1=float(CAP_L),
                            scalar2=None, op0=ALU.is_lt, op1=ALU.bypass)
            return e.nop()
        p.on("vector", q_v3, waits=[p.now(MO)], incs=((MO, 1),))
        def q_v4(e):
            return e.tensor_tensor(routq[0:2, 2, :], routq[0:2, 0, :], routq[0:2, 1, :], ALU.mult)
        p.on("vector", q_v4, waits=[p.now(MO)], incs=((MO, 1),))
        VALP = lambda: routq[0:2, 2, :]
        def q_sl(e):
            e.tensor_scalar(out=routq[0:2, 0, :], in0=ET(), scalar1=768.0,
                            scalar2=kofs_s[:], op0=ALU.mult, op1=ALU.add)
            return e.nop()
        p.on("vector", q_sl, waits=[p.now(MO)], incs=((MO, 1),))
        def q_sl2(e):
            return e.tensor_tensor(routq[0:2, 0, :], routq[0:2, 0, :], PT(), ALU.add)
        p.on("vector", q_sl2, waits=[p.now(MO)], incs=((MO, 1),))
        def q_slg(e):
            return e.tensor_tensor(routq[0:2, 1, :], routq[0:2, 0, :], VALP(), ALU.mult)
        p.on("vector", q_slg, waits=[p.now(MO)], incs=((MO, 1),))
        def q_sls(e):
            e.tensor_scalar(out=routq[0:2, 3, :], in0=VALP(), scalar1=-1000000.0,
                            scalar2=1000000.0, op0=ALU.mult, op1=ALU.add)
            return e.nop()
        p.on("vector", q_sls, waits=[p.now(MO)], incs=((MO, 1),))
        def q_sls2(e):
            return e.tensor_tensor(routq[0:2, 3, :], routq[0:2, 0, :], routq[0:2, 3, :], ALU.add)
        p.on("vector", q_sls2, waits=[p.now(MO)], incs=((MO, 1),))
        sb_done = p.now(MO)
        for (src, dst) in [(3, 0), (1, 2), (2, 4)]:
            def mv_blk(e, src=src, dst=dst):
                return e.dma_start(out=routr[dst:dst + 2, 0, :], in_=routq[0:2, src, :])
            p.on("gpsimd", mv_blk, waits=[sb_done], incs=((GP, 16),))
        srow_done = p.now(GP)
        for li in range(NT):
            bank = 6 + (li & 1)
            def tr_sv(e, li=li, bank=bank):
                return e.transpose(psum[:, bank, 0:6], routr[0:6, 0, li * 128:(li + 1) * 128],
                                   identf[0:6, 0:6])
            p.on("tensor", tr_sv, waits=[srow_done] + waitdep(("bank", bank)), incs=((MM, 1),))
            def ev_sv(e, li=li, bank=bank):
                e.tensor_copy(slotcv[:, 0:2, li], psum[:, bank, 0:2])
                e.tensor_copy(slotcv[:, 2:4, li], psum[:, bank, 2:4])
                return e.tensor_copy(routc[:, 8:10, li], psum[:, bank, 4:6])
            p.on("vector", ev_sv, waits=[p.now(MM)], incs=((MO, 1),))
            setdep(("bank", bank), MO)
        def c_w(e):
            e.tensor_tensor(routc[:, 6, :], routc[:, 4, :], routc[:, 8, :], ALU.mult)
            return e.tensor_tensor(routc[:, 7, :], routc[:, 5, :], routc[:, 9, :], ALU.mult)
        p.on("vector", c_w, waits=[p.now(MO)], incs=((MO, 1),))
        rout_done = p.now(MO)
        if "dbg_rc" in dbg:
            def mark(e):
                return e.memset(routc[:, 11, :], 7.0)
            p.on("vector", mark, waits=[rout_done], incs=((MO, 1),))
            def st_rc(e):
                return e.dma_start(out=dbg["dbg_rc"], in_=routc[:])
            p.on("sync", st_rc, waits=[p.now(MO)], incs=((STO, 16),))
            slotf = sbt("slotf_dbg", [128, 4, NT], F32)
            def cvt_slot(e):
                return e.tensor_copy(slotf[:], slotcv[:])
            p.on("vector", cvt_slot, waits=[rout_done], incs=((MO, 1),))
            def st_slot(e):
                return e.dma_start(out=dbg["dbg_slot"], in_=slotf[:])
            p.on("sync", st_slot, waits=[p.now(MO)], incs=((STO, 16),))
            def st_x1dbg(e):
                return e.dma_start(out=dbg["dbg_x1"].rearrange("(a p) b -> p a b", p=128), in_=x1f[:])
            p.on("gpsimd", st_x1dbg, waits=[x1_done], incs=((STO, 16),))

        for r in range(NSLOT // 128):
            def z_xd(e, r=r):
                return e.dma_start(out=xd[r * 128:(r + 1) * 128, :], in_=zeros2k[:])
            p.on("sync", z_xd, waits=([CONST, x1_done] if r == 0 else []), incs=((SC, 16),))
        zdone = p.now(SC)
        for rep in range(1):
            for k in range(2):
                for li in range(NT):
                    def scat(e, k=k, li=li):
                        return e.indirect_dma_start(
                            out=xd, out_offset=IndirectOffsetOnAxis(ap=slotcv[:, k, li:li + 1], axis=0),
                            in_=x1b[:, li], in_offset=None,
                            bounds_check=NSLOT - 1, oob_is_err=False)
                    p.on("gpsimd", scat, waits=([rout_done, zdone] if (rep == 0 and k == 0 and li == 0) else []),
                         incs=((SC, 16),))
        scat_done = p.now(SC)
        dly = sq_scr
        def d0(e):
            return e.memset(dly[:, 0:512], 1.0)
        p.on("gpsimd", d0, waits=[scat_done], incs=((MO, 1),))
        for _ in range(300):
            def dst(e):
                return e.tensor_scalar(out=dly[:, 0:512], in0=dly[:, 0:512], scalar1=1.0000001,
                                       scalar2=None, op0=ALU.mult, op1=ALU.bypass)
            p.on("vector", dst, waits=[p.now(MO)], incs=((MO, 1),))
        scat_done = p.now(MO)

        for ee in range(E_EXP):
            wpar = ee & 1
            for j in range(6):
                def g_xe(e, ee=ee, j=j):
                    return e.indirect_dma_start(
                        out=xe[:, j], out_offset=None, in_=xd,
                        in_offset=IndirectOffsetOnAxis(ap=xe_idx[:, ee * 6 + j:ee * 6 + j + 1], axis=0))
                p.on("gpsimd", g_xe, waits=([scat_done] if (ee == 0 and j == 0) else []) + waitdep(("xe", j)),
                     incs=((GP, 16),))
            xe_ld = p.now(GP)
            for j in range(6):
                for c in range(8):
                    bank = 6 + ((j * 8 + c) & 1)
                    def tr_xe(e, j=j, c=c, bank=bank):
                        return e.transpose(psum_bf[:, bank, 0:128], xe[:, j, c * 128:(c + 1) * 128], ident[:])
                    p.on("tensor", tr_xe, waits=[xe_ld] + waitdep(("bank", bank)), incs=((MM, 1),))
                    def ev_xe(e, j=j, c=c, bank=bank):
                        return e.tensor_copy(xeT[:, c, j * 128:(j + 1) * 128], psum_bf[:, bank, 0:128])
                    p.on("vector", ev_xe, waits=[p.now(MM)], incs=((EV, 1),))
                    setdep(("bank", bank), EV)
                setdep(("xe", j), MM)
            xeT_done = p.now(EV)

            for fc in range(8):
                wp = fc & 1
                def ld_wg(e, ee=ee, fc=fc, wp=wp):
                    return e.dma_start(out=wgu[:, wp, 0],
                                       in_=wgT_in[ee].rearrange("(c p) f -> p c f", p=128)[:, :, fc * 512:(fc + 1) * 512])
                p.on("gpsimd", ld_wg, waits=waitdep(("wgu", wp)), incs=((WM, 16),))
                def ld_wu(e, ee=ee, fc=fc, wp=wp):
                    return e.dma_start(out=wgu[:, wp, 1],
                                       in_=wuT_in[ee].rearrange("(c p) f -> p c f", p=128)[:, :, fc * 512:(fc + 1) * 512])
                p.on("gpsimd", ld_wu, incs=((WM, 16),))
                w_rdy = p.now(WM)
                for ft in range(4):
                    for sh in range(2):
                        gb = sh
                        def mm_g(e, wp=wp, ft=ft, sh=sh, gb=gb):
                            for c in range(8):
                                inst = e.matmul(psum[:, gb, 0:384], wgu[:, wp, 0, c, ft * 128:(ft + 1) * 128],
                                                xeT[:, c, sh * 384:(sh + 1) * 384],
                                                start=(c == 0), stop=(c == 7))
                            return inst
                        p.on("tensor", mm_g, waits=[w_rdy, xeT_done] + waitdep(("bank", gb)), incs=((MM, 1),))
                        def mm_u(e, wp=wp, ft=ft, sh=sh, gb=gb):
                            for c in range(8):
                                inst = e.matmul(psum[:, 2 + gb, 0:384], wgu[:, wp, 1, c, ft * 128:(ft + 1) * 128],
                                                xeT[:, c, sh * 384:(sh + 1) * 384],
                                                start=(c == 0), stop=(c == 7))
                            return inst
                        p.on("tensor", mm_u, waits=waitdep(("bank", 2 + gb)), incs=((MM, 1),))
                        def ev_sil(e, sh=sh, gb=gb):
                            return e.activation(out=gsil[:, sh, :], in_=psum[:, gb, 0:384], func=AF.Silu)
                        p.on("scalar", ev_sil, waits=[p.now(MM)] + waitdep(("gsil", sh)), incs=((EV, 1),))
                        setdep(("bank", gb), EV)
                        def ev_gu(e, fc=fc, ft=ft, sh=sh, gb=gb):
                            return e.tensor_tensor(gu[:, fc * 4 + ft, sh * 384:(sh + 1) * 384],
                                                   gsil[:, sh, :], psum[:, 2 + gb, 0:384], ALU.mult)
                        p.on("vector", ev_gu, waits=[p.now(EV)] + waitdep("gu_free"), incs=((EV, 1),))
                        setdep(("gsil", sh), EV)
                        setdep(("bank", 2 + gb), EV)
                setdep(("wgu", wp), MM)
            gu_done = p.now(EV)

            for sh in range(2):
                for c in range(32):
                    wp = c & 1
                    def ld_wo(e, ee=ee, c=c, wp=wp):
                        return e.dma_start(out=wob[:, wp], in_=woT_in[ee, c * 128:(c + 1) * 128, :])
                    p.on("gpsimd", ld_wo, waits=waitdep(("wob", wp)), incs=((WM, 16),))
                    wo_rdy = p.now(WM)
                    for st3 in range(3):
                        for dc in range(2):
                            bank = 2 + st3 * 2 + dc
                            def mm_eo(e, sh=sh, c=c, wp=wp, st3=st3, dc=dc, bank=bank):
                                return e.matmul(psum[:, bank, :],
                                                gu[:, c, sh * 384 + st3 * 128: sh * 384 + (st3 + 1) * 128],
                                                wob[:, wp, dc * 512:(dc + 1) * 512],
                                                start=(c == 0), stop=(c == 31))
                            p.on("tensor", mm_eo, waits=[wo_rdy, gu_done] +
                                 (waitdep(("bank", bank)) if c == 0 else []), incs=((MM, 1),))
                    setdep(("wob", wp), MM)
                for st3 in range(3):
                    def ev_eo(e, st3=st3):
                        e.tensor_copy(eos[:, 0, 0:512], psum[:, 2 + st3 * 2, :])
                        return e.tensor_copy(eos[:, 0, 512:1024], psum[:, 2 + st3 * 2 + 1, :])
                    p.on("vector", ev_eo, waits=[p.now(MM)] + waitdep("eos"), incs=((EV, 1),))
                    setdep(("bank", 2 + st3 * 2), EV)
                    setdep(("bank", 2 + st3 * 2 + 1), EV)
                    def st_eo(e, ee=ee, sh=sh, st3=st3):
                        base = ee * 768 + sh * 384 + st3 * 128
                        return e.dma_start(out=eo[base:base + 128, :], in_=eos[:, 0])
                    p.on("sync", st_eo, waits=[p.now(EV)], incs=((SC, 16),))
                    if "dbg_eo" in dbg:
                        def st_eod(e, ee=ee, sh=sh, st3=st3):
                            base = ee * 768 + sh * 384 + st3 * 128
                            return e.dma_start(out=dbg["dbg_eo"][base:base + 128, :], in_=eos[:, 0])
                        p.on("sync", st_eod, incs=((SC, 16),))
                    setdep("eos", SC)
            setdep("gu_free", MM)
        eo_done = p.now(SC)
        if "dbg_xd" in dbg:
            xdb = gu.ap().rearrange("p a b -> p (a b)")
            for r in range(NSLOT // 128):
                def ld_xdb(e, r=r):
                    return e.dma_start(out=xdb[:, 0:1024], in_=xd[r * 128:(r + 1) * 128, :])
                p.on("sync", ld_xdb, waits=[eo_done] + ([p.now(SC)] if r else []), incs=((GP, 16),))
                def st_xdb(e, r=r):
                    return e.dma_start(out=dbg["dbg_xd"][r * 128:(r + 1) * 128, :], in_=xdb[:, 0:1024])
                p.on("sync", st_xdb, waits=[p.now(GP)], incs=((SC, 16),))
            eo_done = p.now(SC)

        for li in range(NT):
            cpar = li & 1
            for k in range(2):
                def g_eo(e, li=li, k=k, cpar=cpar):
                    return e.indirect_dma_start(
                        out=cmb[:, cpar, k], out_offset=None, in_=eo,
                        in_offset=IndirectOffsetOnAxis(ap=slotcv[:, 2 + k, li:li + 1], axis=0))
                p.on("gpsimd", g_eo, waits=[eo_done] + (waitdep(("cmb", cpar)) if k == 0 else []),
                     incs=((GP, 16),))
            cmb_ld = p.now(GP)
            xp = x2preA if li < 8 else x2preB
            lj = li & 7
            def c_c1(e, li=li, cpar=cpar, xp=xp, lj=lj):
                return e.scalar_tensor_tensor(out=xp[:, lj], in0=cmb[:, cpar, 0],
                                              scalar=routc[:, 6, li:li + 1], in1=x1f[:, li],
                                              op0=ALU.mult, op1=ALU.add)
            p.on("vector", c_c1, waits=[cmb_ld], incs=((VS, 1),))
            def c_c2(e, li=li, cpar=cpar, xp=xp, lj=lj):
                return e.scalar_tensor_tensor(out=xp[:, lj], in0=cmb[:, cpar, 1],
                                              scalar=routc[:, 7, li:li + 1], in1=xp[:, lj],
                                              op0=ALU.mult, op1=ALU.add)
            p.on("vector", c_c2, waits=[p.now(VS)], incs=((VS, 1),))
            setdep(("cmb", cpar), VS)
        if "dbg_xp" in dbg:
            def st_xpa(e):
                return e.dma_start(out=dbg["dbg_xp"][0:1024, :].rearrange("(a p) b -> p a b", p=128), in_=x2preA[:])
            p.on("gpsimd", st_xpa, waits=[p.now(VS)], incs=((GP, 16),))
            def st_xpb(e):
                return e.dma_start(out=dbg["dbg_xp"][1024:2048, :].rearrange("(a p) b -> p a b", p=128), in_=x2preB[:])
            p.on("gpsimd", st_xpb, incs=((GP, 16),))
            p.on("vector", None, waits=[p.now(GP)])
        xin_fn = lambda li: (x2preA if li < 8 else x2preB)[:, li & 7]
        _emit_ln_block(p, NT, xin_fn, x2f, x2b, stats16, sq_scr, VS)
        x2_done = p.now(VS)


    if STOP >= 3:
        wipj = smap.at("wipj", [128, 2, 8, 128], BF16, 116 * KB_, 4.0, 4.183)
        x4f = smap.at("x4f", [128, NT, 1024], F32, 96 * KB_, 5.91, 6.0)
        wopj = smap.at("wopj", [128, 2, 16, 128], BF16, 108 * KB_, 4.86, 4.97)
        psum_bf = psum.ap().bitcast(BF16)

        p.on("sync", lambda e: e.dma_start(allow_slow_non_contiguous=True, out=convb_s[:], in_=convb_in.rearrange("(g p) o -> p (g o)", p=128)), incs=((LD, 16),))
        p.on("sync", lambda e: e.dma_start(allow_slow_non_contiguous=True, out=dtw_s[:], in_=dtw_in[0:1024].rearrange("(g p) o -> p (g o)", p=128)), incs=((LD, 16),))
        p.on("sync", lambda e: e.dma_start(allow_slow_non_contiguous=True, out=dtb_s[:], in_=dtb_in[0:1024].rearrange("(g p) o -> p (g o)", p=128)), incs=((LD, 16),))
        p.on("sync", lambda e: e.dma_start(out=convw_s[:], in_=convw_in.rearrange("(g p) c -> p g c", p=128)), incs=((LD, 16),))
        p.on("sync", lambda e: e.dma_start(out=Acol_s[:], in_=Alog_in[0:1024].rearrange("(g p) n -> p g n", p=128)), incs=((LD, 16),))
        p.on("sync", lambda e: e.dma_start(allow_slow_non_contiguous=True, out=Dp_s[:], in_=Dp_in[0:1024].rearrange("(g p) o -> p (g o)", p=128)), incs=((LD, 16),))
        p.on("gpsimd", lambda e: e.dma_start(out=edge_sel[:], in_=edge_sel_in), incs=((LD, 16),))
        p.on("sync", lambda e: e.dma_start(out=ag1u_idx[:], in_=ag1u_idx_in), incs=((LD, 16),))
        p.on("sync", lambda e: e.dma_start(out=bc_idx[:], in_=bc_idx_in), incs=((LD, 16),))
        p.on("sync", lambda e: e.dma_start(out=ag2_idx[:], in_=ag2_idx_in), incs=((LD, 16),))
        p.on("gpsimd", lambda e: e.dma_start(out=xpj_s[:], in_=xprojT_in.rearrange("(g p) n -> p g n", p=128)), incs=((LD, 16),))
        p.on("gpsimd", lambda e: e.memset(ones_row[:], 1.0), incs=((LD, 1),))
        lc = p.now(LD)
        def mkA(e):
            return e.activation(out=Acol_s.ap().rearrange("p g n -> p (g n)"),
                                in_=Acol_s.ap().rearrange("p g n -> p (g n)"), func=AF.Exp)
        p.on("scalar", mkA, waits=[lc], incs=((VS, 1),))
        def mkA2(e):
            return e.tensor_scalar(out=Acol_s.ap().rearrange("p g n -> p (g n)"),
                                   in0=Acol_s.ap().rearrange("p g n -> p (g n)"),
                                   scalar1=-1.0, scalar2=None, op0=ALU.mult, op1=ALU.bypass)
        p.on("vector", mkA2, waits=[p.now(VS)], incs=((VS, 1),))
        p.on("vector", lambda e: e.memset(hlast[:], 0.0), incs=((VS, 1),))
        A_ready = p.now(VS)

        def ed_st(e):
            return e.dma_start(out=edg_in, in_=x2b[125:128, 15, :])
        p.on("gpsimd", ed_st, waits=[x2_done], incs=((STO, 16),))
        def ed_ag(e):
            return e.collective_compute("AllGather", ALU.bypass, ins=[edg_in.opt()],
                                        outs=[edg_out.opt()], replica_groups=[list(range(8))])
        p.on("gpsimd", ed_ag, waits=[p.now(STO)], incs=((AGS, 1),))
        def ed_ld(e):
            return e.dma_start(out=x2T[0:24, 0, 0:1024], in_=edg_out)
        p.on("sync", ed_ld, waits=[p.now(AGS)], incs=((VS, 16),))
        def ed_mm(e):
            e.matmul(psum[0:4, 4, :], edge_sel[:], x2T[0:24, 0, 0:512], start=True, stop=True)
            return e.matmul(psum[0:4, 5, :], edge_sel[:], x2T[0:24, 0, 512:1024], start=True, stop=True)
        p.on("tensor", ed_mm, waits=[p.now(VS)] + waitdep(("bank", 4)) + waitdep(("bank", 5)), incs=((MM, 1),))
        def ed_ev(e):
            e.tensor_copy(sq_scr[0:4, 0:512], psum[0:4, 4, :])
            return e.tensor_copy(sq_scr[0:4, 512:1024], psum[0:4, 5, :])
        p.on("vector", ed_ev, waits=[p.now(MM)], incs=((VS, 1),))
        setdep(("bank", 4), VS)
        setdep(("bank", 5), VS)
        for c in range(8):
            bank = 6 + (c & 1)
            def ed_tr(e, c=c, bank=bank):
                return e.transpose(psum[0:128, bank, 0:4], sq_scr[0:4, c * 128:(c + 1) * 128], identf[0:4, 0:4])
            p.on("tensor", ed_tr, waits=[p.now(VS)] + waitdep(("bank", bank)), incs=((MM, 1),))
            def ed_te(e, c=c, bank=bank):
                return e.tensor_copy(edgeT[:, c, :], psum[:, bank, 0:4])
            p.on("vector", ed_te, waits=[p.now(MM)], incs=((VS, 1),))
            setdep(("bank", bank), VS)
        edge_ok = p.now(VS)

        for hp in range(2):
            for j in range(8):
                li = hp * 8 + j
                for c in range(8):
                    bank = 6 + (c & 1)
                    def tr_x2(e, li=li, c=c, bank=bank):
                        return e.transpose(psum_bf[:, bank, 0:128], x2b[:, li, c * 128:(c + 1) * 128], ident[:])
                    p.on("tensor", tr_x2, waits=[x2_done, edge_ok] + waitdep(("bank", bank)), incs=((MM, 1),))
                    def ev_tx2(e, j=j, c=c, bank=bank):
                        return e.tensor_copy(x2T[:, c, j * 128:(j + 1) * 128], psum_bf[:, bank, 0:128])
                    p.on("vector", ev_tx2, waits=[p.now(MM)], incs=((EV, 1),))
                    setdep(("bank", bank), EV)
            x2T_ok = p.now(EV)
            for m in range(32):
                wp = m & 1
                def ld_wip(e, m=m, wp=wp):
                    return e.dma_start(out=wipj[:, wp],
                                       in_=inprojT_in.rearrange("(c p) f -> p c f", p=128)[:, :, m * 128:(m + 1) * 128])
                p.on("gpsimd", ld_wip, waits=waitdep(("wipj", wp)), incs=((WM, 16),))
                w_ok = p.now(WM)
                for h2 in range(2):
                    bank = (m & 1) * 2 + h2
                    def mm_ip(e, m=m, wp=wp, h2=h2, bank=bank):
                        for c in range(8):
                            inst = e.matmul(psum[:, bank, :], wipj[:, wp, c],
                                            x2T[:, c, h2 * 512:(h2 + 1) * 512],
                                            start=(c == 0), stop=(c == 7))
                        return inst
                    p.on("tensor", mm_ip, waits=[w_ok, x2T_ok] + waitdep(("bank", bank)), incs=((MM, 1),))
                if m < 16:
                    def ev_xin(e, m=m, hp=hp):
                        e.tensor_copy(xinT[:, m, hp * 1024:hp * 1024 + 512], psum[:, (m & 1) * 2, :])
                        return e.tensor_copy(xinT[:, m, hp * 1024 + 512:hp * 1024 + 1024], psum[:, (m & 1) * 2 + 1, :])
                    p.on("vector", ev_xin, waits=[p.now(MM)], incs=((EV, 1),))
                else:
                    zp = (m & 1) * 1024
                    def ev_z(e, m=m, zp=zp):
                        e.tensor_copy(zst[:, zp:zp + 512], psum[:, (m & 1) * 2, :])
                        return e.tensor_copy(zst[:, zp + 512:zp + 1024], psum[:, (m & 1) * 2 + 1, :])
                    p.on("vector", ev_z, waits=[p.now(MM)] + waitdep(("zst", m & 1)), incs=((EV, 1),))
                    def st_z(e, m=m, hp=hp, zp=zp):
                        return e.dma_start(out=zT_d[(m - 16) * 128:(m - 15) * 128, hp * 1024:(hp + 1) * 1024],
                                           in_=zst[:, zp:zp + 1024])
                    p.on("sync", st_z, waits=[p.now(EV)], incs=((SC, 16),))
                    setdep(("zst", m & 1), SC)
                setdep(("bank", (m & 1) * 2), EV)
                setdep(("bank", (m & 1) * 2 + 1), EV)
                if hp == 0 and m < 16:
                    def mm_ed(e, m=m, wp=wp):
                        for c in range(8):
                            inst = e.matmul(psum[:, 4, 0:4], wipj[:, wp, c], edgeT[:, c, :],
                                            start=(c == 0), stop=(c == 7))
                        return inst
                    p.on("tensor", mm_ed, waits=[edge_ok] + waitdep(("bank", 4)), incs=((MM, 1),))
                    def ev_ed(e, m=m):
                        return e.tensor_copy(e3[:, m, :], psum[:, 4, 0:4])
                    p.on("vector", ev_ed, waits=[p.now(MM)], incs=((EV, 1),))
                    setdep(("bank", 4), EV)
                setdep(("wipj", wp), MM)
        inproj_ok = p.now(EV)

        for g in range(16):
            def cv0(e, g=g):
                return e.tensor_scalar(out=ctmp[:], in0=xinT[:, g, :], scalar1=convw_s[:, g, 3:4],
                                       scalar2=None, op0=ALU.mult, op1=ALU.bypass)
            p.on("vector", cv0, waits=[inproj_ok] if g == 0 else [p.now(VS)], incs=((VS, 1),))
            for j in range(3):
                sft = 3 - j
                def cvj(e, g=g, j=j, sft=sft):
                    return e.scalar_tensor_tensor(out=ctmp[:, sft:2048], in0=xinT[:, g, 0:2048 - sft],
                                                  scalar=convw_s[:, g, j:j + 1], in1=ctmp[:, sft:2048],
                                                  op0=ALU.mult, op1=ALU.add)
                p.on("vector", cvj, waits=[p.now(VS)], incs=((VS, 1),))
                def cvje(e, g=g, j=j, sft=sft):
                    return e.scalar_tensor_tensor(out=ctmp[:, 0:sft], in0=e3[:, g, j:3],
                                                  scalar=convw_s[:, g, j:j + 1], in1=ctmp[:, 0:sft],
                                                  op0=ALU.mult, op1=ALU.add)
                p.on("vector", cvje, waits=[p.now(VS)], incs=((VS, 1),))
            def cvs(e, g=g):
                return e.activation(out=xinT[:, g, :], in_=ctmp[:], func=AF.Silu,
                                    bias=convb_s[:, g:g + 1], scale=1.0)
            p.on("scalar", cvs, waits=[p.now(VS)], incs=((VS, 1),))
        conv_ok = p.now(VS)

        XPS = p.sem("xps")
        for q4 in range(4):
            bank = q4 & 1
            col = (q4 & 1) * 512
            def mm_xp(e, q4=q4, bank=bank):
                for g in range(16):
                    inst = e.matmul(psum[0:33, bank, :], xpj_s[:, g, :],
                                    xinT[:, g, q4 * 512:(q4 + 1) * 512],
                                    start=(g == 0), stop=(g == 15))
                return inst
            p.on("tensor", mm_xp, waits=[conv_ok, lc] + waitdep(("bank", bank)), incs=((MM, 1),))
            def ev_xp(e, q4=q4, bank=bank, col=col):
                return e.tensor_copy(sq_scr[0:33, col:col + 512], psum[0:33, bank, :])
            p.on("vector", ev_xp, waits=[p.now(MM)] + waitdep(("sqs", col)), incs=((EV, 1),))
            setdep(("bank", bank), EV)
            def st_xp(e, q4=q4, col=col):
                return e.dma_start(out=ag1s_in[:, q4 * 512:(q4 + 1) * 512], in_=sq_scr[0:33, col:col + 512])
            p.on("gpsimd", st_xp, waits=[p.now(EV)], incs=((XPS, 16),))
            setdep(("sqs", col), XPS)
        for g in range(16):
            def st_u(e, g=g):
                return e.dma_start(out=ag1u_in[g * 128:(g + 1) * 128, :], in_=xinT[:, g, :])
            p.on("sync", st_u, waits=[conv_ok] if g == 0 else [], incs=((SC, 16),))
        ag1_ready = p.now(SC)
        def ag1u(e):
            return e.collective_compute("AllGather", ALU.bypass, ins=[ag1u_in.opt()],
                                        outs=[ag1u_out.opt()], replica_groups=[list(range(8))])
        p.on("gpsimd", ag1u, waits=[ag1_ready, p.now(XPS)], incs=((AGS, 1),))
        def ag1s(e):
            return e.collective_compute("AllGather", ALU.bypass, ins=[ag1s_in.opt()],
                                        outs=[ag1s_out.opt()], replica_groups=[list(range(8))])
        p.on("gpsimd", ag1s, waits=[p.now(AGS)], incs=((AGS, 1),))
        ag1_done = p.now(AGS)
        for g in range(8):
            for th in range(2):
                def g_u(e, g=g, th=th):
                    return e.indirect_dma_start(
                        out=uT_sc[:, g, th * 2048:(th + 1) * 2048], out_offset=None,
                        in_=ag1u_out,
                        in_offset=IndirectOffsetOnAxis(ap=ag1u_idx[:, g, th:th + 1], axis=0))
                p.on("gpsimd", g_u, waits=[ag1_done, lc] if (g == 0 and th == 0) else [], incs=((GP, 16),))
        scan_in_ok = p.now(GP)
        if "dbg_u" in dbg:
            def d_u(e):
                return e.dma_start(out=dbg["dbg_u"], in_=uT_sc.ap().rearrange("p a b -> p (a b)"))
            p.on("sync", d_u, waits=[scan_in_ok], incs=((STO, 16),))
            def d_ssm(e):
                return e.dma_start(out=dbg["dbg_ssm"], in_=ag1s_in)
            p.on("sync", d_ssm, incs=((STO, 16),))

        HCS = p.sem("hcs")
        def _dump(key, src_ap):
            def dfn(e, src_ap=src_ap, key=key):
                return e.dma_start(out=dbg[key], in_=src_ap)
            p.on("sync", dfn, waits=[], incs=((STO, 16),))
            tgt = p.now(STO)
            p.on("vector", None, waits=[tgt])
            p.on("scalar", None, waits=[tgt])
            p.on("gpsimd", None, waits=[tgt])
        for c8 in range(8):
            eoff = c8 * 512
            def g_dt(e, c8=c8, eoff=eoff):
                return e.indirect_dma_start(
                    out=dtbc[:, :], out_offset=None, in_=ag1s_out,
                    in_offset=IndirectOffsetOnAxis(ap=bc_idx[:, c8 // 4, 0:1], axis=0),
                    element_offset=(c8 % 4) * 512)
            p.on("gpsimd", g_dt, waits=[scan_in_ok, A_ready] + waitdep("dtbc"), incs=((GP, 16),))
            dt_ok = p.now(GP)
            if c8 == 0 and "dbg_dt" in dbg:
                p.on("sync", None, waits=[dt_ok])
                _dump("dbg_dt", dtbc[:, :])
            for g in range(8):
                def dl1(e, g=g):
                    return e.tensor_scalar(out=dab[:, 0, :], in0=dtbc[:, :], scalar1=dtw_s[:, g:g + 1],
                                           scalar2=dtb_s[:, g:g + 1], op0=ALU.mult, op1=ALU.add)
                p.on("vector", dl1, waits=[dt_ok] + waitdep(("dab", 0)), incs=((VS, 1),))
                def dl2(e):
                    return e.tensor_scalar(out=dab[:, 1, :], in0=dab[:, 0, :], scalar1=-1.0,
                                           scalar2=None, op0=ALU.mult, op1=ALU.bypass)
                p.on("vector", dl2, waits=[p.now(VS)] + waitdep(("dab", 1)), incs=((VS, 1),))
                def dl3(e):
                    return e.tensor_tensor(dab[:, 1, :], dab[:, 1, :], dab[:, 0, :], ALU.min)
                p.on("vector", dl3, waits=[p.now(VS)], incs=((VS, 1),))
                def dl4(e):
                    return e.activation(out=dab[:, 1, :], in_=dab[:, 1, :], func=AF.Exp)
                p.on("scalar", dl4, waits=[p.now(VS)], incs=((VS, 1),))
                def dl5(e):
                    return e.activation(out=dab[:, 1, :], in_=dab[:, 1, :], func=AF.Ln, bias=1.0)
                p.on("scalar", dl5, waits=[p.now(VS)], incs=((VS, 1),))
                def dl6(e):
                    return e.tensor_scalar(out=dab[:, 0, :], in0=dab[:, 0, :], scalar1=0.0,
                                           scalar2=None, op0=ALU.max, op1=ALU.bypass)
                p.on("vector", dl6, waits=[p.now(VS)], incs=((VS, 1),))
                def dl7(e, g=g):
                    return e.tensor_tensor(dlt8[:, g, :], dab[:, 0, :], dab[:, 1, :], ALU.add)
                p.on("vector", dl7, waits=[p.now(VS)], incs=((VS, 1),))
                setdep(("dab", 0), VS)
                setdep(("dab", 1), VS)
                def du1(e, g=g, eoff=eoff):
                    return e.tensor_tensor(du8[:, g, :], dlt8[:, g, :], uT_sc[:, g, eoff:eoff + 512], ALU.mult)
                p.on("vector", du1, waits=[p.now(VS)], incs=((VS, 1),))
            dl_ok = p.now(VS)
            if c8 == 0 and "dbg_dl" in dbg:
                p.on("sync", None, waits=[dl_ok])
                _dump("dbg_dl", dlt8.ap().rearrange("p a b -> p (a b)"))
                _dump("dbg_du", du8.ap().rearrange("p a b -> p (a b)"))
            for n in range(NST):
                np_ = n & 1
                def g_b(e, c8=c8, n=n, np_=np_):
                    return e.indirect_dma_start(
                        out=bcb[:, np_, :], out_offset=None, in_=ag1s_out,
                        in_offset=IndirectOffsetOnAxis(ap=bc_idx[:, c8 // 4, 1 + n:2 + n], axis=0),
                        element_offset=(c8 % 4) * 512)
                p.on("gpsimd", g_b, waits=[dl_ok] + waitdep(("bcbn", np_)), incs=((GP, 16),))
                def g_c(e, c8=c8, n=n, np_=np_):
                    return e.indirect_dma_start(
                        out=ccb[:, np_, :], out_offset=None, in_=ag1s_out,
                        in_offset=IndirectOffsetOnAxis(ap=bc_idx[:, c8 // 4, 17 + n:18 + n], axis=0),
                        element_offset=(c8 % 4) * 512)
                p.on("gpsimd", g_c, waits=waitdep(("ccbn", np_)), incs=((GP, 16),))
                bc_ok = p.now(GP)
                if c8 == 0 and n == 0 and "dbg_bc" in dbg:
                    p.on("sync", None, waits=[bc_ok])
                    _dump("dbg_bc0", bcb[:, 0, :])
                    _dump("dbg_bc1", ccb[:, 0, :])
                for g in range(8):
                    gn = g * 16 + n
                    def mk_da(e, g=g, n=n):
                        return e.activation(out=dab[:, 0, :], in_=dlt8[:, g, :], func=AF.Exp,
                                            scale=Acol_s[:, g, n:n + 1])
                    p.on("scalar", mk_da, waits=[dl_ok] + waitdep(("dab", 0)), incs=((VS, 1),))
                    def mk_dub(e, g=g, np_=np_):
                        return e.tensor_tensor(dub[:, 0, 0:512], du8[:, g, :], bcb[:, np_, :], ALU.mult)
                    p.on("gpsimd", mk_dub, waits=[bc_ok] + waitdep(("dub", 0)), incs=((GP, 1),))
                    def mk_scan(e, g=g, gn=gn):
                        return e.tensor_tensor_scan(out=hball[:, g, :], data0=dab[:, 0, :],
                                                    data1=dub[:, 0, :], initial=hlast[:, gn:gn + 1],
                                                    op0=ALU.mult, op1=ALU.add)
                    if c8 == 0 and n == 0 and g == 0 and "dbg_da" in dbg:
                        p.on("sync", None, waits=[p.now(VS)])
                        _dump("dbg_da", dab[:, 0, :])
                    p.on("vector", mk_scan, waits=[p.now(VS), p.now(GP)] + waitdep(("hb", g)), incs=((VS, 1),))
                    setdep(("dab", 0), VS)
                    setdep(("dub", 0), VS)
                    if c8 == 0 and n == 0 and g == 0 and "dbg_hb" in dbg:
                        p.on("sync", None, waits=[p.now(VS)])
                        _dump("dbg_hb", hball[:, 0, :])
                    def up_hl(e, g=g, gn=gn):
                        return e.tensor_copy(hlast[:, gn:gn + 1], hball[:, g, 511:512])
                    p.on("vector", up_hl, waits=[p.now(VS)], incs=((VS, 1),))
                    def mk_hc(e, g=g, np_=np_):
                        return e.tensor_tensor(hcb[:, 0, 0:512], hball[:, g, :], ccb[:, np_, :], ALU.mult)
                    p.on("vector", mk_hc, waits=[p.now(VS)] + waitdep("hcf"), incs=((HCS, 1),))
                    setdep(("hb", g), HCS)
                    hc_ok = p.now(HCS)
                    def mm_y(e, g=g, n=n):
                        return e.matmul(psum[:, g, :], ident[:], hcb[:, 0, :],
                                        start=(n == 0), stop=(n == NST - 1))
                    p.on("tensor", mm_y, waits=[hc_ok] + (waitdep(("bank", g)) if n == 0 else []),
                         incs=((MM, 1),))
                    setdep("hcf", MM)
                setdep(("bcbn", np_), MM)
                setdep(("ccbn", np_), MM)
            y_mm_ok = p.now(MM)
            if c8 == 0 and "dbg_hf" in dbg:
                p.on("sync", None, waits=[p.now(VS)])
                _dump("dbg_hf", hball.ap().rearrange("p a b -> p (a b)"))
            for g in range(8):
                def ev_y(e, g=g, eoff=eoff):
                    return e.scalar_tensor_tensor(out=ytst[:, 0, :], in0=uT_sc[:, g, eoff:eoff + 512],
                                                  scalar=Dp_s[:, g:g + 1], in1=psum[:, g, :],
                                                  op0=ALU.mult, op1=ALU.add)
                p.on("vector", ev_y, waits=[y_mm_ok] + waitdep(("yt", 0)), incs=((VS, 1),))
                setdep(("bank", g), VS)
                if c8 == 0 and g == 0 and "dbg_yt" in dbg:
                    p.on("sync", None, waits=[p.now(VS)])
                    _dump("dbg_yt", ytst[:, 0, :])
                def st_y(e, g=g, c8=c8):
                    return e.dma_start(out=ag2_in[(g * 8 + c8) * 128:(g * 8 + c8 + 1) * 128, :],
                                       in_=ytst[:, 0, :])
                p.on("sync", st_y, waits=[p.now(VS)], incs=((SC, 16),))
                setdep(("yt", 0), SC)
        scan_done = p.now(SC)

        if "dbg_y" in dbg:
            def d_y(e):
                return e.dma_start(out=dbg["dbg_y"], in_=ag2_in)
            p.on("sync", d_y, waits=[scan_done], incs=((STO, 16),))
            def d_z(e):
                return e.dma_start(out=dbg["dbg_z"], in_=zT_d)
            p.on("sync", d_z, incs=((STO, 16),))
        def ag2(e):
            return e.collective_compute("AllGather", ALU.bypass, ins=[ag2_in.opt()],
                                        outs=[ag2_out.opt()], replica_groups=[list(range(8))])
        p.on("gpsimd", ag2, waits=[scan_done], incs=((AGS, 1),))
        ag2_done = p.now(AGS)
        for gg in range(16):
            for oc in range(4):
                def g_y(e, gg=gg, oc=oc):
                    return e.indirect_dma_start(
                        out=ygT[:, gg, oc * 512:(oc + 1) * 512], out_offset=None,
                        in_=ag2_out,
                        in_offset=IndirectOffsetOnAxis(ap=ag2_idx[:, gg, oc:oc + 1], axis=0))
                p.on("gpsimd", g_y, waits=[ag2_done] if (gg == 0 and oc == 0) else [], incs=((GP, 16),))
        yg_ok = p.now(GP)
        for gg in range(16):
            zp2 = gg & 1
            def ld_z(e, gg=gg, zp2=zp2):
                return e.dma_start(out=zrl[:, zp2, :], in_=zT_d[gg * 128:(gg + 1) * 128, :])
            p.on("sync", ld_z, waits=([yg_ok] if gg == 0 else []) + waitdep(("zrl", zp2)), incs=((SC, 16),))
            def sil_z(e, gg=gg, zp2=zp2):
                return e.activation(out=zrl[:, zp2, :], in_=zrl[:, zp2, :], func=AF.Silu)
            p.on("scalar", sil_z, waits=[p.now(SC)], incs=((VS, 1),))
            def gt_y(e, gg=gg, zp2=zp2):
                return e.tensor_tensor(ygT[:, gg, :], ygT[:, gg, :], zrl[:, zp2, :], ALU.mult)
            p.on("vector", gt_y, waits=[p.now(VS), yg_ok], incs=((VS, 1),))
            setdep(("zrl", zp2), VS)
        yg_gated = p.now(VS)
        if "dbg_yg" in dbg:
            def d_yg(e):
                return e.dma_start(out=dbg["dbg_yg"], in_=ygT.ap().rearrange("p a b -> p (a b)"))
            p.on("sync", d_yg, waits=[yg_gated], incs=((STO, 16),))
            p.on("vector", None, waits=[p.now(STO)])
        for m in range(8):
            wp = m & 1
            def ld_wop(e, m=m, wp=wp):
                return e.dma_start(out=wopj[:, wp],
                                   in_=outprojT_in.rearrange("(c p) f -> p c f", p=128)[:, :, m * 128:(m + 1) * 128])
            p.on("gpsimd", ld_wop, waits=waitdep(("wopj", wp)), incs=((WM, 16),))
            wop_ok = p.now(WM)
            for q4 in range(4):
                bank = q4 & 3
                def mm_op(e, m=m, wp=wp, q4=q4, bank=bank):
                    for gg in range(16):
                        inst = e.matmul(psum[:, bank, :], wopj[:, wp, gg],
                                        ygT[:, gg, q4 * 512:(q4 + 1) * 512],
                                        start=(gg == 0), stop=(gg == 15))
                    return inst
                p.on("tensor", mm_op, waits=[wop_ok, yg_gated] + waitdep(("bank", bank)), incs=((MM, 1),))
                def ev_op(e, m=m, q4=q4, bank=bank):
                    return e.tensor_copy(aoT[:, m, q4 * 512:(q4 + 1) * 512], psum[:, bank, :])
                p.on("vector", ev_op, waits=[p.now(MM)], incs=((EV, 1),))
                setdep(("bank", bank), EV)
            setdep(("wopj", wp), MM)
        op_done = p.now(EV)
        for li in range(NT):
            for c in range(8):
                bank = 6 + (c & 1)
                def tr_ao(e, li=li, c=c, bank=bank):
                    return e.transpose(psum_bf[:, bank, 0:128], aoT[:, c, li * 128:(li + 1) * 128], ident[:])
                p.on("tensor", tr_ao, waits=[op_done] + waitdep(("bank", bank)), incs=((MM, 1),))
                def rs_ao(e, li=li, c=c, bank=bank):
                    return e.tensor_tensor(x2f[:, li, c * 128:(c + 1) * 128],
                                           x2f[:, li, c * 128:(c + 1) * 128],
                                           psum_bf[:, bank, 0:128], ALU.add)
                p.on("vector", rs_ao, waits=[p.now(MM)], incs=((VS, 1),))
                setdep(("bank", bank), VS)
        _emit_ln_block(p, NT, x2f, x3f, x3b, stats16, sq_scr, VS)
        x3_done = p.now(VS)
        if "dbg_x3" in dbg:
            def d_x3(e):
                return e.dma_start(out=dbg["dbg_x3"].rearrange("(a p) b -> p a b", p=128), in_=x3f[:])
            p.on("sync", d_x3, waits=[x3_done], incs=((STO, 16),))
            def d_x3p(e):
                return e.dma_start(out=dbg["dbg_x3p"].rearrange("(a p) b -> p a b", p=128), in_=x2f[:])
            p.on("sync", d_x3p, incs=((STO, 16),))
            p.on("vector", None, waits=[p.now(STO)])

        p.on("gpsimd", lambda e: e.dma_start(out=win_s[:], in_=Win_in.rearrange("(c p) m -> p c m", p=128)), incs=((LD, 16),))
        p.on("gpsimd", lambda e: e.dma_start(out=wout_s[:], in_=Wout_in), waits=[x3_done], incs=((LD, 16),))
        p.on("sync", lambda e: e.dma_start(out=binc_s[:], in_=binc_in), incs=((LD, 16),))
        p.on("sync", lambda e: e.dma_start(out=w1_s[:], in_=W1_in), incs=((LD, 16),))
        p.on("sync", lambda e: e.dma_start(out=b1_s[:], in_=b1_in), incs=((LD, 16),))
        p.on("sync", lambda e: e.dma_start(out=w2_s[:], in_=W2_in), incs=((LD, 16),))
        p.on("sync", lambda e: e.dma_start(out=b2_s[:], in_=b2_in), incs=((LD, 16),))
        p.on("sync", lambda e: e.dma_start(allow_slow_non_contiguous=True, out=boutT_s[:],
                                           in_=bout_in.rearrange("o (g p) -> p (o g)", p=128)), incs=((LD, 16),))
        ffn_c = p.now(LD)
        for li in range(NT):
            hp = li // 8
            j = li % 8
            for c in range(8):
                bank = 6 + (c & 1)
                def tr_x3(e, li=li, c=c, bank=bank):
                    return e.transpose(psum_bf[:, bank, 0:128], x3b[:, li, c * 128:(c + 1) * 128], ident[:])
                p.on("tensor", tr_x3, waits=[x3_done] + waitdep(("bank", bank)), incs=((MM, 1),))
                def ev_tx3(e, li=li, c=c, bank=bank):
                    return e.tensor_copy(x3T[:, c, li * 128:(li + 1) * 128], psum_bf[:, bank, 0:128])
                p.on("vector", ev_tx3, waits=[p.now(MM)], incs=((EV, 1),))
                setdep(("bank", bank), EV)
        x3T_ok = p.now(EV)
        for q4 in range(4):
            bank = q4 & 1
            def mm_s1(e, q4=q4, bank=bank):
                for c in range(8):
                    inst = e.matmul(psum[0:64, bank, :], win_s[:, c], x3T[:, c, q4 * 512:(q4 + 1) * 512],
                                    start=(c == 0), stop=(c == 7))
                return inst
            p.on("tensor", mm_s1, waits=[x3T_ok, ffn_c] + waitdep(("bank", bank)), incs=((MM, 1),))
            def ev_s1(e, q4=q4, bank=bank):
                return e.tensor_scalar(out=sfn[:, 0, q4 * 512:(q4 + 1) * 512], in0=psum[0:64, bank, :],
                                       scalar1=1.0, scalar2=binc_s[:], op0=ALU.mult, op1=ALU.add)
            p.on("vector", ev_s1, waits=[p.now(MM)], incs=((VS, 1),))
            setdep(("bank", bank), VS)
        s_ok = p.now(VS)
        for i in range(INNER):
            def mk_h(e, i=i):
                return e.activation(out=sfn[:, 2, :], in_=sfn[:, 0, :], func=AF.Relu,
                                    scale=w1_s[:, i:i + 1], bias=b1_s[:, i:i + 1])
            p.on("scalar", mk_h, waits=[s_ok if i == 0 else p.now(VS)], incs=((VS, 1),))
            if i == 0:
                def mk_g(e, i=i):
                    return e.tensor_scalar(out=sfn[:, 1, :], in0=sfn[:, 2, :], scalar1=w2_s[:, i:i + 1],
                                           scalar2=b2_s[:], op0=ALU.mult, op1=ALU.add)
            else:
                def mk_g(e, i=i):
                    return e.scalar_tensor_tensor(out=sfn[:, 1, :], in0=sfn[:, 2, :],
                                                  scalar=w2_s[:, i:i + 1], in1=sfn[:, 1, :],
                                                  op0=ALU.mult, op1=ALU.add)
            p.on("vector", mk_g, waits=[p.now(VS)], incs=((VS, 1),))
        g_ok = p.now(VS)
        def g_cast(e):
            return e.tensor_copy(sfn[:, 3, :].bitcast(BF16)[:, 0:2048], sfn[:, 1, :])
        p.on("vector", g_cast, waits=[g_ok], incs=((VS, 1),))
        g2_ok = p.now(VS)
        for m in range(8):
            for q4 in range(4):
                bank = q4 & 1
                def mm_o(e, m=m, q4=q4, bank=bank):
                    return e.matmul(psum[:, bank, :], wout_s[0:64, m * 128:(m + 1) * 128],
                                    sfn[:, 3, :].bitcast(BF16)[0:64, q4 * 512:(q4 + 1) * 512],
                                    start=True, stop=True)
                p.on("tensor", mm_o, waits=[g2_ok, ffn_c] + waitdep(("bank", bank)), incs=((MM, 1),))
                def ev_o(e, m=m, q4=q4, bank=bank):
                    return e.tensor_scalar(out=foT[:, m, q4 * 512:(q4 + 1) * 512], in0=psum[:, bank, :],
                                           scalar1=1.0, scalar2=boutT_s[:, m:m + 1], op0=ALU.mult, op1=ALU.add)
                p.on("vector", ev_o, waits=[p.now(MM)], incs=((VS, 1),))
                setdep(("bank", bank), VS)
        fo_ok = p.now(VS)
        for li in range(NT):
            for c in range(8):
                bank = 6 + (c & 1)
                def tr_fo(e, li=li, c=c, bank=bank):
                    return e.transpose(psum_bf[:, bank, 0:128], foT[:, c, li * 128:(li + 1) * 128], ident[:])
                p.on("tensor", tr_fo, waits=[fo_ok] + waitdep(("bank", bank)), incs=((MM, 1),))
                def rs_fo(e, li=li, c=c, bank=bank):
                    return e.tensor_tensor(x3f[:, li, c * 128:(c + 1) * 128],
                                           x3f[:, li, c * 128:(c + 1) * 128],
                                           psum_bf[:, bank, 0:128], ALU.add)
                p.on("vector", rs_fo, waits=[p.now(MM)], incs=((VS, 1),))
                setdep(("bank", bank), VS)
        _emit_ln_block(p, NT, x3f, x4f, None, stats16, sq_scr, VS)
        x4_done = p.now(VS)


    if STOP == 1:
        out_src, out_done = x1f, x1_done
    elif STOP == 2:
        out_src, out_done = x2f, x2_done
    elif STOP == 3:
        out_src, out_done = x3f, x3_done
    else:
        out_src, out_done = x4f, x4_done
    def st_out(e):
        return e.dma_start(out=out_final.rearrange("(a p) b -> p a b", p=128), in_=out_src[:])
    p.on("sync", st_out, waits=[out_done], incs=((STO, 16),))
    if "dbg_x1" in dbg:
        def st_dbg(e):
            return e.dma_start(out=dbg["dbg_x1"].rearrange("(a p) b -> p a b", p=128), in_=x1f[:])
        p.on("sync", st_dbg, waits=[x1_done], incs=((STO, 16),))
        def st_acc(e):
            return e.dma_start(out=dbg["dbg_acc"].rearrange("(a p) b -> p a b", p=128), in_=acc[:])
        p.on("gpsimd", st_acc, waits=[attn_done], incs=((STO, 16),))
        def st_hoz(e):
            return e.dma_start(out=dbg["dbg_hoz"].rearrange("(a p) b -> p a b", p=128), in_=hoz[:])
        p.on("gpsimd", st_hoz, waits=[hoz_ld], incs=((STO, 16),))
        def st_xres(e):
            return e.dma_start(out=dbg["dbg_xres"].rearrange("(a p) b -> p a b", p=128), in_=xres[:])
        p.on("gpsimd", st_xres, waits=[x1_done], incs=((STO, 16),))
        def st_lnst(e):
            return e.dma_start(out=dbg["dbg_st"], in_=lnstdbg[:])
        p.on("gpsimd", st_lnst, waits=[p.now(VS)], incs=((STO, 16),))
        def st_kh_disabled(e):
            return e.dma_start(out=dbg["dbg_kh"], in_=khT.ap().rearrange("p a b -> p (a b)"))
        def st_qh_disabled(e):
            return e.dma_start(out=dbg["dbg_qh"], in_=qhT.ap().rearrange("p a b -> p (a b)"))
        def st_vh_disabled(e):
            return e.dma_start(out=dbg["dbg_vh"], in_=vh[:])
    p.on("sync", None, waits=[p.now(STO)])

    assert nc.sbuf_base <= SbufMap.BASE, ("bump allocator ran into manual map", nc.sbuf_base)
    p.emit(ctx)
    ctx.close()
    return nc


def _emit_ln_block(p, ntiles, xin, xf, xb, stats16, sq_scr, VS, lndbg=None, post=None):
    if not callable(xin):
        _t = xin
        xin = lambda li: _t[:, li]
    if xf is not None and not callable(xf):
        _f = xf
        xf = lambda li: _f[:, li]
    """LayerNorm ntiles tiles: xin (fp32) -> xf (fp32) + xb (bf16).
    stats16: sbuf (128, 8, ntiles) fp32 scratch (rows: sum, sumsq, s2, diff,
    vareps, rcp, rs, nmrs). Per-tile reduces feed one batched stat chain; every
    chained DVE op is separated by a producer-inc/consumer-wait (tiny-op RAW)."""
    for li in range(ntiles):
        def lnsum(e, li=li):
            return e.reduce_sum(out=stats16[:, 0, li:li + 1], in_=xin(li), axis=AX.X)
        p.on("vector", lnsum, incs=((VS, 1),))
        def lnsq(e, li=li):
            return e.activation(out=sq_scr[:], in_=xin(li), func=AF.Square)
        p.on("scalar", lnsq, waits=[p.now(VS)], incs=((VS, 1),))
        def lnred(e, li=li):
            return e.reduce_sum(out=stats16[:, 1, li:li + 1], in_=sq_scr[:], axis=AX.X)
        p.on("vector", lnred, waits=[p.now(VS)], incs=((VS, 1),))
    n = ntiles
    def c_s2(e):
        return e.tensor_tensor(stats16[:, 2, 0:n], stats16[:, 0, 0:n],
                               stats16[:, 0, 0:n], ALU.mult)
    p.on("vector", c_s2, waits=[p.now(VS)], incs=((VS, 1),))
    def c_diff(e):
        return e.scalar_tensor_tensor(out=stats16[:, 3, 0:n], in0=stats16[:, 2, 0:n],
                                      scalar=-1.0 / D, in1=stats16[:, 1, 0:n],
                                      op0=ALU.mult, op1=ALU.add)
    p.on("vector", c_diff, waits=[p.now(VS)], incs=((VS, 1),))
    def c_veps(e):
        return e.tensor_scalar(out=stats16[:, 4, 0:n], in0=stats16[:, 3, 0:n],
                               scalar1=1.0 / D, scalar2=1e-5, op0=ALU.mult, op1=ALU.add)
    p.on("vector", c_veps, waits=[p.now(VS)], incs=((VS, 1),))
    def c_rcp(e):
        return e.reciprocal(stats16[:, 5, 0:n], stats16[:, 4, 0:n])
    p.on("vector", c_rcp, waits=[p.now(VS)], incs=((VS, 1),))
    def c_rs(e):
        return e.activation(out=stats16[:, 6, 0:n], in_=stats16[:, 5, 0:n], func=AF.Sqrt)
    p.on("scalar", c_rs, waits=[p.now(VS)], incs=((VS, 1),))
    def c_nm(e):
        e.tensor_scalar(out=stats16[:, 2, 0:n], in0=stats16[:, 0, 0:n],
                        scalar1=-1.0 / D, scalar2=None, op0=ALU.mult, op1=ALU.bypass)
        return e.nop()
    p.on("vector", c_nm, waits=[p.now(VS)], incs=((VS, 1),))
    def c_nmrs(e):
        return e.tensor_tensor(stats16[:, 7, 0:n], stats16[:, 2, 0:n],
                               stats16[:, 6, 0:n], ALU.mult)
    p.on("vector", c_nmrs, waits=[p.now(VS)], incs=((VS, 1),))
    rdy = p.now(VS)
    for li in range(ntiles):
        def lnout(e, li=li):
            inst = e.tensor_scalar(out=xf(li), in0=xin(li), scalar1=stats16[:, 6, li:li + 1],
                                   scalar2=stats16[:, 7, li:li + 1], op0=ALU.mult, op1=ALU.add)
            if xb is not None:
                inst = e.tensor_copy(xb[:, li], xf(li))
            return inst
        p.on("vector", lnout, waits=([rdy] if li == 0 else []) + (post(li)[0] if post else []),
             incs=((VS, 1),))
        if post:
            post(li)[1]()
    if lndbg is not None:
        def lndump(e):
            return e.tensor_copy(lndbg[:, 0:8, 0:ntiles], stats16[:, 0:8, 0:ntiles])
        p.on("vector", lndump, incs=((VS, 1),))


def _prep(inputs):
    i = inputs
    o = {}
    o["x0"] = np.ascontiguousarray(i["L0_attn_x"])
    o["x0T"] = np.ascontiguousarray(i["L0_attn_x"].transpose(0, 2, 1))
    o["wqT"] = np.ascontiguousarray(i["L0_attn_Wq"].transpose(2, 0, 1).reshape(D, NH * DKH))
    o["wkT"] = np.ascontiguousarray(i["L0_attn_Wk"].transpose(2, 0, 1).reshape(D, NH * DKH))
    o["wvT"] = np.ascontiguousarray(i["L0_attn_Wv"].transpose(0, 2, 1))
    o["ident"] = np.eye(128, dtype=np.float32)
    o["gate_wT"] = np.ascontiguousarray(i["L0_ffn_gate_w"].T)
    o["wgT"] = np.ascontiguousarray(i["L0_ffn_wi_gate"].transpose(0, 2, 1))
    o["wuT"] = np.ascontiguousarray(i["L0_ffn_wi_up"].transpose(0, 2, 1))
    o["woT"] = np.ascontiguousarray(i["L0_ffn_wo"].transpose(0, 2, 1))
    o["iota8"] = np.arange(8, dtype=np.float32).reshape(8, 1)
    o["iota_row8"] = np.tile(np.arange(8, dtype=np.float32), (128, 1))
    ksel = np.zeros((16, 2), dtype=np.float32)
    ksel[0:8, 0] = 1.0
    ksel[8:16, 1] = 1.0
    o["ksel"] = ksel
    o["iota16"] = np.concatenate([np.arange(8), np.arange(8)]).astype(np.float32).reshape(16, 1)
    o["kofs"] = np.array([[0.0], [float(CAP_L)]], dtype=np.float32)
    xei = np.zeros((128, 48), dtype=np.uint32)
    for col in range(48):
        xei[:, col] = col * 128 + np.arange(128, dtype=np.uint32)
    o["xe_idx"] = xei
    bf = np.float16
    ones = (np.ones((128, 128), dtype=np.float32) / SEQ).astype(bf)
    zer = np.zeros((128, 128), dtype=bf)
    triu = (np.triu(np.ones((128, 128), dtype=np.float32)) / SEQ).astype(bf)
    for half in (0, 1):
        tiles = q_tiles_for(half)
        qm = np.zeros((NT, 32, 128, 128), dtype=bf)
        for li in range(NT):
            t = tiles[li]
            for kb in range(KBCNT[li]):
                if kb < t:
                    qm[li, kb] = ones
                elif kb == t:
                    qm[li, kb] = triu
        o[f"qmask_{half}"] = qm
    o["inprojT"] = np.ascontiguousarray(i["L1_attn_in_proj_w"].T)
    o["convw"] = np.ascontiguousarray(i["L1_attn_conv_w"].reshape(DI, DC))
    o["convb"] = np.ascontiguousarray(i["L1_attn_conv_b"].reshape(DI, 1))
    o["xprojT"] = np.ascontiguousarray(i["L1_attn_x_proj_w"].T)
    o["outprojT"] = np.ascontiguousarray(i["L1_attn_out_proj_w"].T)
    o["dtw_full"] = np.ascontiguousarray(i["L1_attn_dt_w"].reshape(DI, 1))
    o["dtb_full"] = np.ascontiguousarray(i["L1_attn_dt_b"].reshape(DI, 1))
    o["Alog_full"] = np.ascontiguousarray(i["L1_attn_A_log"])
    o["Dp_full"] = np.ascontiguousarray(i["L1_attn_D_param"].reshape(DI, 1))
    o["Win"] = np.ascontiguousarray(i["L1_ffn_Win"])
    o["binc"] = np.ascontiguousarray(i["L1_ffn_bin_"].reshape(M_FFN, 1))
    o["W1"] = np.ascontiguousarray(i["L1_ffn_W1"])
    o["b1"] = np.ascontiguousarray(i["L1_ffn_b1"])
    o["W2"] = np.ascontiguousarray(i["L1_ffn_W2"])
    o["b2"] = np.ascontiguousarray(i["L1_ffn_b2"].reshape(M_FFN, 1))
    o["Wout"] = np.ascontiguousarray(i["L1_ffn_Wout"])
    o["bout"] = np.ascontiguousarray(i["L1_ffn_bout"].reshape(1, D))
    return o


def _core_maps(prep, inputs):
    maps = []
    for cid in range(8):
        b, half = cid >> 1, cid & 1
        tiles = q_tiles_for(half)
        m = {}
        m["x0T_b"] = prep["x0T"][b]
        m["x0_own"] = prep["x0"][b, half * NTOK:(half + 1) * NTOK]
        xq = np.concatenate([prep["x0T"][b][:, t * 128:(t + 1) * 128] for t in tiles], axis=1)
        m["xqT"] = np.ascontiguousarray(xq)
        m["wqT"] = prep["wqT"]; m["wkT"] = prep["wkT"]; m["wvT"] = prep["wvT"]
        m["qmask"] = prep[f"qmask_{half}"]
        for k in ["ident", "gate_wT", "wgT", "wuT", "woT", "iota8", "iota_row8",
                  "ksel", "iota16", "kofs", "xe_idx"]:
            m[k] = prep[k]
        psel = np.zeros((8, 1), dtype=np.float32)
        psel[:cid] = 1.0
        m["prefix_sel"] = psel
        hidx = np.zeros((128, NT), dtype=np.uint32)
        for li_own in range(NT):
            t = half * NT + li_own
            src_core = cid if t in tiles else (cid ^ 1)
            src_tiles = q_tiles_for(src_core & 1)
            li_src = src_tiles.index(t)
            base = src_core * NTOK + li_src * 128
            hidx[:, li_own] = base + np.arange(128, dtype=np.uint32)
        m["hoz_idx"] = hidx
        hperm = np.arange(DI).reshape(2, 1024)[[half, 1 - half]].reshape(DI)
        m["dtw"] = prep["dtw_full"][hperm]
        m["dtb"] = prep["dtb_full"][hperm]
        m["Alog"] = np.ascontiguousarray(prep["Alog_full"][hperm])
        m["Dp"] = prep["Dp_full"][hperm]
        for k in ["inprojT", "convw", "convb", "xprojT", "outprojT", "Win", "binc",
                  "W1", "b1", "W2", "b2", "Wout", "bout"]:
            m[k] = prep[k]
        es = np.zeros((24, 4), dtype=np.float32)
        if half == 1:
            for j in range(3):
                es[(b * 2) * 3 + j, j] = 1.0
        m["edge_sel"] = es
        pp = np.arange(128, dtype=np.uint32)
        agu = np.zeros((128, 8, 2), dtype=np.uint32)
        for g in range(8):
            for th in range(2):
                agu[:, g, th] = (b * 2 + th) * 2048 + half * 1024 + g * 128 + pp
        m["ag1u_idx"] = agu
        bci = np.zeros((128, 2, 33), dtype=np.uint32)
        for th in range(2):
            bci[:, th, :] = (b * 2 + th) * 33 + np.arange(33, dtype=np.uint32)[None, :]
        m["bc_idx"] = bci
        ag2 = np.zeros((128, 16, 4), dtype=np.uint32)
        for gg in range(16):
            for oc in range(4):
                ag2[:, gg, oc] = ((b * 2 + gg // 8) * 8192 +
                                  ((gg % 8) * 8 + half * 4 + oc) * 128 + pp)
        m["ag2_idx"] = ag2
        maps.append(m)
    return maps


def kernel(**inputs):
    prep = _prep(inputs)
    in_maps = _core_maps(prep, inputs)
    nc = build_nc()
    res = run_bass_kernel_spmd(nc, in_maps, core_ids=list(range(8)))
    outs = []
    for cid in range(8):
        outs.append(res.results[cid]["out"])
    full = np.zeros((B, SEQ, D), dtype=np.float32)
    for cid in range(8):
        b, half = cid >> 1, cid & 1
        full[b, half * NTOK:(half + 1) * NTOK] = outs[cid]
    return full


def _debug_run(inputs):
    prep = _prep(inputs)
    in_maps = _core_maps(prep, inputs)
    nc = build_nc()
    return run_bass_kernel_spmd(nc, in_maps, core_ids=list(range(8)))
